# version 05
# baseline (speedup 1.0000x reference)
import numpy as np
import ml_dtypes
from contextlib import ExitStack

import concourse.bass as bass
import concourse.mybir as mybir
from concourse.bass_utils import run_bass_kernel_spmd
from concourse.bass import ds

F32 = mybir.dt.float32
BF16 = mybir.dt.bfloat16
I32 = mybir.dt.int32
AF = mybir.ActivationFunctionType
ALU = mybir.AluOpType
AX = mybir.AxisListType
BF = ml_dtypes.bfloat16

NCORE = 8
DM = 2048
TOK = 16384
TSL = TOK // NCORE
NST = 16
DTR = 64
SEQ = 8192
NH = 16
DH = 128
FF0 = 8192
FI = 5632



class NcProxy:
    def __init__(self, nc, pfx, dma_pool=None, cmp_pool=None,
                 sw_names=None, sw_alloc=None):
        self.nc = nc
        self.pfx = pfx
        self.dma_pool = dma_pool
        self.cmp_pool = cmp_pool
        self.sw_names = sw_names or set()
        self.sw_alloc = sw_alloc
        self.ndma = 0
        self.ncmp = 0

    def sbuf_tensor(self, name, shape, dtype):
        return self.nc.sbuf_tensor(self.pfx + name, shape, dtype)

    def psum_tensor(self, name, shape, dtype):
        return self.nc.psum_tensor(self.pfx + name, shape, dtype)

    def semaphore(self, name):
        if self.dma_pool is None:
            return self.nc.semaphore(self.pfx + name)
        import contextlib

        @contextlib.contextmanager
        def _fake(handle):
            yield handle
        if name in self.sw_names:
            h = self.sw_alloc[0][self.sw_alloc[1]]
            self.sw_alloc[1] += 1
        elif name.startswith("s_"):
            h = self.cmp_pool[self.ncmp]
            self.ncmp += 1
        else:
            h = self.dma_pool[self.ndma]
            self.ndma += 1
        return _fake(h)


def _barrier(nc, ncx, e, k, sb=None, clear_from=None):
    if sb is None:
        sb = e(ncx.semaphore(f"bar{k}"))
    with nc.Block() as blk:
        for engname in ("tensor", "scalar", "vector", "gpsimd"):
            def mk(engname=engname):
                def _(eng):
                    eng.drain().then_inc(sb, 1)
                    eng.wait_ge(sb, 6 if clear_from is not None else 5)
                return _
            getattr(blk, engname)(mk())

        @blk.sync
        def _(sy):
            sy.drain().then_inc(sb, 1)
            sy.wait_ge(sb, 5)
            if clear_from is not None:
                sy.sem_clear(range(clear_from, 256))
                sy.nop().then_inc(sb, 1)
                sy.wait_ge(sb, 6)
    return sb


def build_A():
    nc = bass.Bass("TRN2", target_bir_lowering=False)
    xT = nc.dram_tensor("xT", [DM, TSL], F32, kind="ExternalInput").ap()
    wxT = nc.dram_tensor("wxT", [DM, 96], F32, kind="ExternalInput").ap()
    wdtT = nc.dram_tensor("wdtT", [DTR, DM], F32, kind="ExternalInput").ap()
    bdt = nc.dram_tensor("bdt", [128, 16], F32, kind="ExternalInput").ap()
    dtT = nc.dram_tensor("dtT", [DM, TSL], F32, kind="ExternalOutput").ap()
    bc = nc.dram_tensor("bc", [32, TSL], F32, kind="ExternalOutput").ap()

    NK = DM // 128
    NTC = TSL // 512

    with ExitStack() as ctx:
        e = ctx.enter_context
        xb = e(nc.sbuf_tensor("xb", [128, NK * TSL], BF16))
        wx_sb = e(nc.sbuf_tensor("wx_sb", [128, NK * 96], BF16))
        wdt_sb = e(nc.sbuf_tensor("wdt_sb", [DTR, DM], BF16))
        bdt_sb = e(nc.sbuf_tensor("bdt_sb", [128, 16], F32))
        xp_sb = e(nc.sbuf_tensor("xp_sb", [96, TSL], F32))
        dtx_b = e(nc.sbuf_tensor("dtx_b", [DTR, TSL], BF16))
        t_z = e(nc.sbuf_tensor("t_z", [128, 2 * 512], F32))
        t_a = e(nc.sbuf_tensor("t_a", [128, 512], F32))
        t_e = e(nc.sbuf_tensor("t_e", [128, 512], F32))
        t_l = e(nc.sbuf_tensor("t_l", [128, 2 * 512], F32))
        t_r = e(nc.sbuf_tensor("t_r", [128, 512], F32))
        dt_out = e(nc.sbuf_tensor("dt_out", [128, 2 * 512], F32))
        one_c = e(nc.sbuf_tensor("one_c", [128, 1], F32))
        p_xp = e(nc.psum_tensor("p_xp", [96, 512], F32))
        p_dt = e(nc.psum_tensor("p_dt", [128, 2 * 512], F32))
        ld = e(nc.semaphore("ld"))
        lds = e(nc.semaphore("lds"))
        s_xp = e(nc.semaphore("s_xp"))
        s_dt = e(nc.semaphore("s_dt"))
        s_dtx = e(nc.semaphore("s_dtx"))
        s_spZ = e(nc.semaphore("s_spZ"))
        s_spL = e(nc.semaphore("s_spL"))
        s_vr = e(nc.semaphore("s_vr"))
        s_out = e(nc.semaphore("s_out"))
        stp = [e(nc.semaphore("st0")), e(nc.semaphore("st1"))]
        blk = e(nc.Block())

        NLD = 16 + 16 + 1 + 1 + 1

        @blk.gpsimd
        def _(g):
            for k in range(NK):
                g.dma_start(out=xb[:, k * TSL:(k + 1) * TSL],
                            in_=xT[k * 128:(k + 1) * 128, :]).then_inc(ld, 16)
            for k in range(NK):
                g.dma_start(out=wx_sb[:, k * 96:(k + 1) * 96],
                            in_=wxT[k * 128:(k + 1) * 128, :]).then_inc(ld, 16)
            g.dma_start(out=wdt_sb[:], in_=wdtT[:]).then_inc(ld, 16)

        @blk.sync
        def _(sy):
            sy.dma_start(out=bdt_sb[:], in_=bdt[:]).then_inc(lds, 16)

        LD_ALL = 16 * (NK + NK + 1)

        @blk.tensor
        def _(t):
            t.wait_ge(ld, LD_ALL)
            g = 0
            for tc in range(NTC):
                cs = slice(tc * 512, (tc + 1) * 512)
                if tc > 0:
                    t.wait_ge(s_dtx, tc)
                for k in range(NK):
                    mm = t.matmul(p_xp[:, :], wx_sb[:, k * 96:(k + 1) * 96],
                                  xb[:, k * TSL + tc * 512: k * TSL + tc * 512 + 512],
                                  start=(k == 0), stop=(k == NK - 1))
                mm.then_inc(s_xp, 1)
                t.wait_ge(s_dtx, tc + 1)
                for k in range(NK):
                    if g >= 2:
                        t.wait_ge(s_spZ, g - 1)
                    b = g % 2
                    t.matmul(p_dt[:, b * 512:(b + 1) * 512],
                             wdt_sb[:, k * 128:(k + 1) * 128], dtx_b[:, cs],
                             start=True, stop=True).then_inc(s_dt, 1)
                    g += 1

        @blk.scalar
        def _(a):
            a.wait_ge(lds, 16)
            g = 0
            for tc in range(NTC):
                for k in range(NK):
                    b = g % 2
                    zsl = slice(b * 512, (b + 1) * 512)
                    a.wait_ge(s_dt, g + 1)
                    if g >= 2:
                        a.wait_ge(s_vr, g - 1)
                    a.activation(t_z[:, zsl], p_dt[:, zsl], AF.Identity,
                                 bias=bdt_sb[:, k:k + 1]).then_inc(s_spZ, 1)
                    a.drain()
                    a.activation(t_a[:], t_z[:, zsl], AF.Abs)
                    a.drain()
                    a.activation(t_e[:], t_a[:], AF.Exp, scale=-1.0)
                    a.drain()
                    a.activation(t_l[:, zsl], t_e[:], AF.Ln,
                                 bias=one_c[:]).then_inc(s_spL, 1)
                    g += 1

        @blk.vector
        def _(v):
            v.memset(one_c[:], 1.0)
            g = 0
            for tc in range(NTC):
                cs = slice(tc * 512, (tc + 1) * 512)
                v.wait_ge(s_xp, tc + 1)
                v.tensor_copy(xp_sb[:, cs], p_xp[:, :])
                v.tensor_copy(dtx_b[:, cs], p_xp[0:64, :]).then_inc(s_dtx, 1)
                for k in range(NK):
                    b = g % 2
                    zsl = slice(b * 512, (b + 1) * 512)
                    v.wait_ge(s_spZ, g + 1)
                    v.tensor_scalar(t_r[:], t_z[:, zsl], 0.0, None,
                                    ALU.max).then_inc(s_vr, 1)
                    v.wait_ge(s_spL, g + 1)
                    if g >= 2:
                        v.wait_ge(stp[g % 2], 16 * (g // 2))
                    v.drain()
                    v.tensor_add(dt_out[:, zsl], t_r[:], t_l[:, zsl]).then_inc(s_out, 1)
                    g += 1

        @blk.sync
        def _(sy):
            g = 0
            for tc in range(NTC):
                for k in range(NK):
                    b = g % 2
                    sy.wait_ge(s_out, g + 1)
                    sy.dma_start(out=dtT[k * 128:(k + 1) * 128,
                                         tc * 512:(tc + 1) * 512],
                                 in_=dt_out[:, b * 512:(b + 1) * 512]).then_inc(stp[g % 2], 16)
                    g += 1
            sy.wait_ge(s_dtx, NTC)
            sy.dma_start(out=bc[:], in_=xp_sb[64:96, :]).then_inc(stp[0], 16)
            sy.wait_ge(stp[0], 16 * (NTC * NK // 2 + 1))
            sy.wait_ge(stp[1], 16 * (NTC * NK // 2))
    return nc


def build_E():
    nc = bass.Bass("TRN2", target_bir_lowering=False)
    F16 = mybir.dt.float16
    io = dict(
        w2T=nc.dram_tensor("w2T", [FF0, 256], F32, kind="ExternalInput").ap(),
        h128=nc.dram_tensor("h128", [128, 128], BF16, kind="ExternalInput").ap(),
        idt=nc.dram_tensor("idt", [128, 128], BF16, kind="ExternalInput").ap(),
        gpk=nc.dram_tensor("gpk", [1408, 256], I32, kind="ExternalInput").ap(),
        gsc=nc.dram_tensor("gsc", [128, 11 * 16], F16, kind="ExternalInput").ap(),
        dpk=nc.dram_tensor("dpk", [256, 704], I32, kind="ExternalInput").ap(),
        dsc=nc.dram_tensor("dsc", [128, 2 * 44], F16, kind="ExternalInput").ap(),
        w2pT=nc.dram_tensor("w2pT", [FF0, 256], BF16, kind="ExternalOutput").ap(),
        wguT=nc.dram_tensor("wguT", [DM, 1408], BF16, kind="ExternalOutput").ap(),
        wdnT=nc.dram_tensor("wdnT", [FI, 256], BF16, kind="ExternalOutput").ap())
    with ExitStack() as ctx:
        phase_E(nc, NcProxy(nc, "E_"), ctx.enter_context, io)
    return nc


def phase_E(nc, ncx, e, io):
    w2T = io["w2T"]; h128 = io["h128"]; idt = io["idt"]
    gpk = io["gpk"]; gsc = io["gsc"]; dpk = io["dpk"]; dsc = io["dsc"]
    w2pT = io["w2pT"]; wguT = io["wguT"]; wdnT = io["wdnT"]

    SC = float(2.0 ** -6.5)
    NTR = 11 * 16 + 2 * 44

    if True:
        w2s = e(ncx.sbuf_tensor("w2s", [128, 2 * 256], BF16))
        t1 = e(ncx.sbuf_tensor("t1", [128, 64 * 256], F32))
        tmph = e(ncx.sbuf_tensor("tmph", [128, 32 * 256], F32))
        w2o = e(ncx.sbuf_tensor("w2o", [128, 64 * 256], BF16))
        h_sb = e(ncx.sbuf_tensor("h_sb", [128, 128], BF16))
        idt_sb = e(ncx.sbuf_tensor("idt_sb", [128, 128], BF16))
        pk_sb = e(ncx.sbuf_tensor("pk_sb", [128, 11 * 256], I32))
        gsc_sb = e(ncx.sbuf_tensor("gsc_sb", [128, 11 * 16], F32))
        dpk_sb = e(ncx.sbuf_tensor("dpk_sb", [128, 2 * 704], I32))
        dsc_sb = e(ncx.sbuf_tensor("dsc_sb", [128, 2 * 44], F32))
        m_i = e(ncx.sbuf_tensor("m_i", [128, 704], I32))
        mf = e(ncx.sbuf_tensor("mf", [128, 704], F32))
        r1 = e(ncx.sbuf_tensor("r1", [128, 704], F32))
        r2 = e(ncx.sbuf_tensor("r2", [128, 704], F32))
        uu = e(ncx.sbuf_tensor("uu", [128, 704], F32))
        vv = e(ncx.sbuf_tensor("vv", [128, 704], F32))
        s31 = e(ncx.sbuf_tensor("s31", [128, 704], I32))
        wde = e(ncx.sbuf_tensor("wde", [128, 5632], F32))
        wdb = e(ncx.sbuf_tensor("wdb", [128, 5632], BF16))
        wtr = e(ncx.sbuf_tensor("wtr", [128, 128], BF16))
        p_f = e(ncx.psum_tensor("p_f", [128, 256], F32))
        p_tr = e(ncx.psum_tensor("p_tr", [128, 128], BF16))
        ld = e(ncx.semaphore("ld"))
        lds2 = e(ncx.semaphore("lds2"))
        ldwp = [e(ncx.semaphore("ldwA")), e(ncx.semaphore("ldwB"))]
        s_t = e(ncx.semaphore("s_t"))
        s_v = e(ncx.semaphore("s_v"))
        s_a = e(ncx.semaphore("s_a"))
        st = e(ncx.semaphore("st"))
        blk = e(nc.Block())

        @blk.gpsimd
        def _(g):
            g.dma_start(out=gsc_sb[:], in_=gsc[:]).then_inc(ld, 16)
            g.dma_start(out=dsc_sb[:], in_=dsc[:]).then_inc(ld, 16)
            for c in range(64):
                if c >= 2:
                    g.wait_ge(s_t, c - 1)
                b = c % 2
                g.dma_start(out=w2s[:, b * 256:(b + 1) * 256],
                            in_=w2T[c * 128:(c + 1) * 128, :]).then_inc(ldwp[c % 2], 16)

        @blk.sync
        def _(sy):
            sy.dma_start(out=h_sb[:], in_=h128[:]).then_inc(lds2, 16)
            sy.dma_start(out=idt_sb[:], in_=idt[:]).then_inc(lds2, 16)
            sy.dma_start(out=pk_sb[:].rearrange("p (c j) -> p c j", c=11),
                         in_=gpk.rearrange("(c p) j -> p c j", p=128)).then_inc(lds2, 16)
            sy.dma_start(out=dpk_sb[:].rearrange("p (c j) -> p c j", c=2),
                         in_=dpk.rearrange("(c p) j -> p c j", p=128)).then_inc(lds2, 16)

        @blk.tensor
        def _(t):
            t.wait_ge(ld, 16 * 2)
            t.wait_ge(lds2, 16 * 4)
            for c in range(64):
                t.wait_ge(ldwp[c % 2], 16 * (c // 2 + 1))
                if c >= 1:
                    t.wait_ge(s_a, c)
                b = c % 2
                mm = t.matmul(p_f[:], h_sb[:], w2s[:, b * 256:(b + 1) * 256],
                              start=True, stop=True)
                mm.then_inc(s_t, 1)
            ntr = 0
            for c in range(11):
                t.wait_ge(s_v, 2 + c)
                for k in range(16):
                    if ntr >= 1:
                        t.wait_ge(s_a, 64 + ntr)
                    mm = t.transpose(p_tr[:], wdb[:, k * 128:(k + 1) * 128], idt_sb[:])
                    mm.then_inc(s_t, 1)
                    ntr += 1
            for c in range(2):
                t.wait_ge(s_v, 13 + c)
                for k in range(44):
                    t.wait_ge(s_a, 64 + ntr)
                    mm = t.transpose(p_tr[:], wdb[:, k * 128:(k + 1) * 128], idt_sb[:])
                    mm.then_inc(s_t, 1)
                    ntr += 1

        @blk.scalar
        def _(a):
            for c in range(64):
                a.wait_ge(s_t, c + 1)
                a.drain()
                a.activation(t1[:, c * 256:(c + 1) * 256], p_f[:],
                             AF.Identity).then_inc(s_a, 1)
            for ntr in range(NTR):
                a.wait_ge(s_t, 64 + ntr + 1)
                if ntr >= 1:
                    a.wait_ge(st, 16 * (ntr + 1))
                a.drain()
                a.activation(wtr[:], p_tr[:], AF.Identity).then_inc(s_a, 1)

        @blk.vector
        def _(v):
            v.wait_ge(s_a, 64)
            for s in range(6):
                bit = 1 << s
                lo = [c for c in range(64) if not (c & bit)]
                v.drain()
                for j, c in enumerate(lo):
                    v.tensor_copy(tmph[:, j * 256:(j + 1) * 256],
                                  t1[:, c * 256:(c + 1) * 256])
                v.drain()
                for j, c in enumerate(lo):
                    cb = c | bit
                    v.tensor_add(t1[:, c * 256:(c + 1) * 256],
                                 t1[:, c * 256:(c + 1) * 256],
                                 t1[:, cb * 256:(cb + 1) * 256])
                v.drain()
                for j, c in enumerate(lo):
                    cb = c | bit
                    v.tensor_sub(t1[:, cb * 256:(cb + 1) * 256],
                                 tmph[:, j * 256:(j + 1) * 256],
                                 t1[:, cb * 256:(cb + 1) * 256])
            v.drain()
            v.tensor_scalar_mul(w2o[:], t1[:], SC).then_inc(s_v, 1)

            def dequant(pk_ap, n_in, sc_ap, n_grp, base_ms):
                wv = wde[:, 0:n_in].rearrange("p (j e) -> p e j", e=8)
                n8 = n_in // 8
                for p in range(8):
                    v.drain()
                    v.tensor_scalar(m_i[:, 0:n8], pk_ap, 4 * p, 7,
                                    ALU.logical_shift_right, ALU.bitwise_and)
                    v.drain()
                    v.tensor_single_scalar(mf[:, 0:n8], m_i[:, 0:n8], 0.0, ALU.add)
                    v.drain()
                    v.tensor_scalar(r1[:, 0:n8], mf[:, 0:n8], -4.0, 0.0,
                                    ALU.add, ALU.max)
                    v.tensor_scalar(s31[:, 0:n8], pk_ap, 28 - 4 * p,
                                    -2147483648, ALU.logical_shift_left,
                                    ALU.bitwise_and)
                    v.tensor_scalar(r2[:, 0:n8], mf[:, 0:n8], -6.0, 0.0,
                                    ALU.add, ALU.max)
                    v.drain()
                    v.tensor_add(uu[:, 0:n8], mf[:, 0:n8], r1[:, 0:n8])
                    v.drain()
                    v.scalar_tensor_tensor(vv[:, 0:n8], uu[:, 0:n8], 0.5,
                                           r2[:, 0:n8], ALU.mult, ALU.add)
                    v.drain()
                    v.tensor_tensor(wv[:, p, :].bitcast(I32),
                                    vv[:, 0:n8].bitcast(I32),
                                    s31[:, 0:n8], ALU.bitwise_xor)
                v.drain()
                for grp in range(n_grp):
                    v.tensor_scalar_mul(wdb[:, grp * 128:(grp + 1) * 128],
                                        wde[:, grp * 128:(grp + 1) * 128],
                                        sc_ap[:, grp:grp + 1])
                v.drain()
                v.engine_nop().then_inc(s_v, 1)

            for c in range(11):
                if c >= 1:
                    v.wait_ge(s_t, 64 + 16 * c)
                dequant(pk_sb[:, c * 256:(c + 1) * 256], 2048,
                        gsc_sb[:, c * 16:(c + 1) * 16], 16, None)
            for c in range(2):
                v.wait_ge(s_t, 64 + 176 + 44 * c)
                dequant(dpk_sb[:, c * 704:(c + 1) * 704], 5632,
                        dsc_sb[:, c * 44:(c + 1) * 44], 44, None)

        @blk.sync
        def _(sy):
            sy.wait_ge(s_v, 1)
            sy.dma_start(out=w2pT.rearrange("(c p) d -> p c d", p=128),
                         in_=w2o[:].rearrange("p (c d) -> p c d", c=64)).then_inc(st, 16)
            ntr = 0
            for c in range(11):
                for k in range(16):
                    sy.wait_ge(s_a, 64 + ntr + 1)
                    sy.dma_start(out=wguT[k * 128:(k + 1) * 128,
                                          c * 128:(c + 1) * 128],
                                 in_=wtr[:]).then_inc(st, 16)
                    ntr += 1
            for c in range(2):
                for k in range(44):
                    sy.wait_ge(s_a, 64 + ntr + 1)
                    sy.dma_start(out=wdnT[k * 128:(k + 1) * 128,
                                          c * 128:(c + 1) * 128],
                                 in_=wtr[:]).then_inc(st, 16)
                    ntr += 1
            sy.wait_ge(st, 16 * (1 + NTR))




def build_C():
    nc = bass.Bass("TRN2", target_bir_lowering=False)
    ya = nc.dram_tensor("ya", [TSL, DM], F32, kind="ExternalInput").ap()
    w2pT = nc.dram_tensor("w2pT", [FF0, DM], BF16, kind="ExternalInput").ap()
    io = dict(
        xa=nc.dram_tensor("xa", [TSL, DM], F32, kind="ExternalInput").ap(),
        w1T=nc.dram_tensor("w1T", [DM, FF0], F32, kind="ExternalInput").ap(),
        w3T=nc.dram_tensor("w3T", [DM, FF0], F32, kind="ExternalInput").ap(),
        idt=nc.dram_tensor("idt", [128, 128], BF16, kind="ExternalInput").ap(),
        z2o=nc.dram_tensor("z2o", [TSL, DM], F32, kind="ExternalOutput").ap(),
        z1f=nc.dram_tensor("z1f", [TSL, DM], F32, kind="Internal").ap(),
        hidT=nc.dram_tensor("hidT", [FF0, TSL], BF16, kind="Internal").ap(),
        ya_tile=lambda eng, tt: ya[tt * 128:(tt + 1) * 128, :].rearrange(
            "t (r d) -> t r d", r=8),
        w2p_tile=lambda hc, dc: w2pT[hc * 128:(hc + 1) * 128,
                                     dc * 512:(dc + 1) * 512].rearrange(
                                         "p (r d) -> p r d", r=2),
        z2tl=None, offs_sb=None)
    with ExitStack() as ctx:
        phase_C(nc, NcProxy(nc, "C_"), ctx.enter_context, io)
    return nc


def phase_C(nc, ncx, e, io):
    xa = io["xa"]; w1T = io["w1T"]; w3T = io["w3T"]; idt = io["idt"]
    z2o = io["z2o"]; z1f = io["z1f"]; hidT = io["hidT"]
    ya_tile = io["ya_tile"]; w2p_tile = io["w2p_tile"]; z2tl = io["z2tl"]

    NT = TSL // 128
    NK = DM // 128
    NH0 = FF0 // 128
    INV = 1.0 / DM

    if True:
        zt = e(ncx.sbuf_tensor("zt", [128, DM], F32))
        yt_in = e(ncx.sbuf_tensor("yt_in", [128, DM], F32))
        z1t = e(ncx.sbuf_tensor("z1t", [128, DM], F32))
        z1b = e(ncx.sbuf_tensor("z1b", [128, DM], BF16))
        mcol = e(ncx.sbuf_tensor("mcol", [128, 4], F32))
        eps_c = e(ncx.sbuf_tensor("eps_c", [128, 1], F32))
        idt_sb = e(ncx.sbuf_tensor("idt_sb", [128, 128], BF16))
        z1T = e(ncx.sbuf_tensor("z1T", [128, NK * TSL], BF16))
        w1_sb = e(ncx.sbuf_tensor("w1_sb", [128, 2 * NK * 128], BF16))
        w3_sb = e(ncx.sbuf_tensor("w3_sb", [128, 2 * NK * 128], BF16))
        if z2tl is not None:
            z2t_sb = w1_sb[:, 0:NK * 128]
        sg = e(ncx.sbuf_tensor("sg", [128, 512], BF16))
        hb = e(ncx.sbuf_tensor("hb", [128, 2 * 512], BF16))
        hw_sb = e(ncx.sbuf_tensor("hw_sb", [128, NH0 * 512], BF16))
        w2_sb = e(ncx.sbuf_tensor("w2_sb", [128, 8 * 512], BF16))
        acc3 = [e(ncx.sbuf_tensor(f"acc{i}", [128, DM], F32)) for i in range(3)]
        accs = acc3 + [yt_in]
        if z2tl is not None:
            z2b = z1b
            s_tr3 = e(ncx.semaphore("s_tr3"))
            s_te3 = e(ncx.semaphore("s_te3"))
            s_zb = e(ncx.semaphore("s_zb"))
            stz = e(ncx.semaphore("stz"))
        p_g = e(ncx.psum_tensor("p_g", [128, 2 * 512], F32))
        p_u = e(ncx.psum_tensor("p_u", [128, 2 * 512], F32))
        p_o = [e(ncx.psum_tensor(f"p_o{i}", [128, 512], F32)) for i in range(4)]
        p_tr = p_g[:, 0:64].bitcast(BF16)
        p_tr2 = p_g[:, 0:128].bitcast(BF16)
        ld = e(ncx.semaphore("ld"))
        ldi = e(ncx.semaphore("ldi"))
        ldwp = [e(ncx.semaphore("ldwA")), e(ncx.semaphore("ldwB"))]
        ldh = e(ncx.semaphore("ldh"))
        ldw2p = [e(ncx.semaphore("ldw2%d" % qq)) for qq in range(8)]
        ldz = e(ncx.semaphore("ldz"))
        s_xm = e(ncx.semaphore("s_xm"))
        s_sd = e(ncx.semaphore("s_sd"))
        s_z1 = e(ncx.semaphore("s_z1"))
        s_tr = e(ncx.semaphore("s_tr"))
        s_te = e(ncx.semaphore("s_te"))
        s_gu = e(ncx.semaphore("s_gu"))
        s_sg = e(ncx.semaphore("s_sg"))
        s_hb = e(ncx.semaphore("s_hb"))
        s_w2c = e(ncx.semaphore("s_w2c"))
        s_ev = e(ncx.semaphore("s_ev"))
        s_ln = e(ncx.semaphore("s_ln"))
        st = e(ncx.semaphore("st"))
        sthp = [e(ncx.semaphore("sthA")), e(ncx.semaphore("sthB"))]
        st2 = e(ncx.semaphore("st2"))
        blk = e(nc.Block())

        @blk.sync
        def _(sy):
            sy.dma_start(out=idt_sb[:], in_=idt[:]).then_inc(ldi, 16)
            for tt in range(NT):
                if tt >= 1:
                    sy.wait_ge(s_z1, tt)
                sy.dma_start(out=zt[:],
                             in_=xa[tt * 128:(tt + 1) * 128, :]).then_inc(ld, 16)
                sy.dma_start(out=yt_in[:].rearrange("p (r d) -> p r d", r=8),
                             in_=ya_tile(sy, tt)).then_inc(ld, 16)
                sy.wait_ge(s_z1, tt + 1)
                sy.dma_start(out=z1f[tt * 128:(tt + 1) * 128, :],
                             in_=z1t[:]).then_inc(st, 16)

        @blk.vector
        def _(v):
            v.memset(eps_c[:], 1e-5)
            for tt in range(NT):
                v.wait_ge(ld, 16 * (2 * tt + 2))
                v.tensor_add(zt[:], zt[:], yt_in[:])
                v.drain()
                v.tensor_reduce(mcol[:, 0:1], zt[:], AX.X, ALU.add)
                v.drain()
                v.tensor_scalar_mul(mcol[:, 0:1], mcol[:, 0:1], INV)
                v.drain()
                v.tensor_scalar(zt[:], zt[:], mcol[:, 0:1], None,
                                ALU.subtract).then_inc(s_xm, 1)
                v.wait_ge(s_sd, tt + 1)
                v.reciprocal(mcol[:, 2:3], mcol[:, 1:2])
                if tt >= 1:
                    v.wait_ge(st, 16 * tt)
                    v.wait_ge(s_tr, NK * tt)
                v.drain()
                v.tensor_scalar_mul(z1t[:], zt[:], mcol[:, 2:3])
                v.drain()
                v.tensor_copy(z1b[:], z1t[:]).then_inc(s_z1, 1)

        @blk.scalar
        def _(a):
            n = 0
            for tt in range(NT):
                a.wait_ge(s_xm, tt + 1)
                if tt >= 1:
                    a.wait_ge(st, 16 * tt)
                a.drain()
                a.activation(z1t[:], zt[:], AF.Square, accum_out=mcol[:, 3:4])
                a.drain()
                a.activation(mcol[:, 1:2], mcol[:, 3:4], AF.Sqrt, scale=INV,
                             bias=eps_c[:]).then_inc(s_sd, 1)
                for k in range(NK):
                    a.wait_ge(s_tr, n + 1)
                    a.drain()
                    a.activation(z1T[:, k * TSL + tt * 128:k * TSL + (tt + 1) * 128],
                                 p_tr, AF.Identity).then_inc(s_te, 1)
                    n += 1

        @blk.tensor
        def _(t):
            t.wait_ge(ldi, 16)
            n = 0
            for tt in range(NT):
                t.wait_ge(s_z1, tt + 1)
                for k in range(NK):
                    if n >= 1:
                        t.wait_ge(s_te, n)
                    mm = t.transpose(p_tr, z1b[:, k * 128:(k + 1) * 128],
                                     idt_sb[:])
                    mm.then_inc(s_tr, 1)
                    n += 1
            t.wait_ge(s_te, NT * NK)
            gi = 0
            for hc in range(NH0):
                t.wait_ge(ldwp[hc % 2], 32 * (hc // 2 + 1))
                hb_ = (hc % 2) * NK * 128
                for tcx in range(4):
                    gb = (gi % 2) * 512
                    if gi >= 2:
                        t.wait_ge(s_hb, gi - 1)
                    for k in range(NK):
                        t.matmul(p_g[:, gb:gb + 512],
                                 w1_sb[:, hb_ + k * 128:hb_ + (k + 1) * 128],
                                 z1T[:, k * TSL + tcx * 512:k * TSL + tcx * 512 + 512],
                                 start=(k == 0), stop=(k == NK - 1))
                    for k in range(NK):
                        mm = t.matmul(p_u[:, gb:gb + 512],
                                      w3_sb[:, hb_ + k * 128:hb_ + (k + 1) * 128],
                                      z1T[:, k * TSL + tcx * 512:k * TSL + tcx * 512 + 512],
                                      start=(k == 0), stop=(k == NK - 1))
                    mm.then_inc(s_gu, 1)
                    gi += 1
            for w in range(4):
                t.wait_ge(ldh, 16 * NH0 * (w + 1))
                for dc in range(4):
                    idx = w * 4 + dc
                    if idx > 0:
                        t.wait_ge(s_ev, idx)
                    t.wait_ge(ldz, 16 * (w + 1) * 4)
                    for hc in range(NH0):
                        widx = idx * NH0 + hc
                        t.wait_ge(ldw2p[widx % 8], 16 * (widx // 8 + 1))
                        wb = (widx % 8) * 512
                        for ttl in range(4):
                            mm = t.matmul(p_o[ttl][:],
                                          hw_sb[:, hc * 512 + ttl * 128:hc * 512 + (ttl + 1) * 128],
                                          w2_sb[:, wb:wb + 512],
                                          start=(hc == 0), stop=(hc == NH0 - 1))
                        mm.then_inc(s_w2c, 1)
                if z2tl is not None:
                    for ttl in range(4):
                        li = w * 4 + ttl
                        t.wait_ge(s_zb, li + 1)
                        for k in range(NK):
                            ti = li * NK + k
                            if ti >= 1:
                                t.wait_ge(s_te3, ti)
                            t.transpose(p_tr2[:, 0:128],
                                        z2b[:, k * 128:(k + 1) * 128],
                                        idt_sb[:]).then_inc(s_tr3, 1)

        @blk.gpsimd
        def _(g):
            for hc in range(NH0):
                if hc >= 2:
                    g.wait_ge(s_gu, 4 * (hc - 1))
                hb_ = (hc % 2) * NK * 128
                g.dma_start(out=w1_sb[:, hb_:hb_ + NK * 128].rearrange(
                                "p (k h) -> p k h", k=NK),
                            in_=w1T.rearrange("(k p) h -> p k h", p=128)[
                                :, :, hc * 128:(hc + 1) * 128]).then_inc(ldwp[hc % 2], 16)
                g.dma_start(out=w3_sb[:, hb_:hb_ + NK * 128].rearrange(
                                "p (k h) -> p k h", k=NK),
                            in_=w3T.rearrange("(k p) h -> p k h", p=128)[
                                :, :, hc * 128:(hc + 1) * 128]).then_inc(ldwp[hc % 2], 16)
            for w in range(4):
                for dc in range(4):
                    idx = w * 4 + dc
                    for hc in range(NH0):
                        widx = idx * NH0 + hc
                        if widx >= 8:
                            g.wait_ge(s_w2c, widx - 7)
                        wb = (widx % 8) * 512
                        g.dma_start(out=w2_sb[:, wb:wb + 512].rearrange(
                                        "p (r d) -> p r d", r=2),
                                    in_=w2p_tile(hc, dc)).then_inc(ldw2p[widx % 8], 16)

        @blk.scalar
        def _(a):
            gi = 0
            for hc in range(NH0):
                for tcx in range(4):
                    a.wait_ge(s_gu, gi + 1)
                    if gi >= 1:
                        a.wait_ge(s_hb, gi)
                    a.activation(sg[:], p_g[:, (gi % 2) * 512:(gi % 2) * 512 + 512],
                                 AF.Silu).then_inc(s_sg, 1)
                    gi += 1
            for li in range(NT):
                a.wait_ge(s_xm, NT + li + 1)
                if li >= 1:
                    a.wait_ge(st2, 16 * li)
                a.drain()
                a.activation(z1t[:], accs[li % 4][:], AF.Square,
                             accum_out=mcol[:, 3:4])
                a.drain()
                a.activation(mcol[:, 1:2], mcol[:, 3:4], AF.Sqrt, scale=INV,
                             bias=eps_c[:]).then_inc(s_sd, 1)

        @blk.vector
        def _(v):
            gi = 0
            for hc in range(NH0):
                for tcx in range(4):
                    v.wait_ge(s_sg, gi + 1)
                    if gi >= 2:
                        v.wait_ge(sthp[gi % 2], 16 * (gi // 2))
                    v.tensor_mul(hb[:, (gi % 2) * 512:(gi % 2 + 1) * 512],
                                 sg[:], p_u[:, (gi % 2) * 512:(gi % 2) * 512 + 512]
                                 ).then_inc(s_hb, 1)
                    gi += 1
            for w in range(4):
                for dc in range(4):
                    idx = w * 4 + dc
                    v.wait_ge(s_w2c, (idx + 1) * NH0)
                    v.wait_ge(ldz, 16 * (w + 1) * 4)
                    v.drain()
                    for ttl in range(4):
                        la = v.tensor_add(accs[ttl][:, dc * 512:(dc + 1) * 512],
                                          accs[ttl][:, dc * 512:(dc + 1) * 512],
                                          p_o[ttl][:])
                    la.then_inc(s_ev, 1)
                    v.drain()
                for ttl in range(4):
                    li = w * 4 + ttl
                    acc = accs[ttl]
                    v.tensor_reduce(mcol[:, 0:1], acc[:], AX.X, ALU.add)
                    v.drain()
                    v.tensor_scalar_mul(mcol[:, 0:1], mcol[:, 0:1], INV)
                    v.drain()
                    v.tensor_scalar(acc[:], acc[:], mcol[:, 0:1], None,
                                    ALU.subtract).then_inc(s_xm, 1)
                    v.wait_ge(s_sd, NT + li + 1)
                    v.reciprocal(mcol[:, 2:3], mcol[:, 1:2])
                    if li >= 1:
                        v.wait_ge(st2, 16 * li)
                    v.drain()
                    v.tensor_scalar_mul(z1t[:], acc[:], mcol[:, 2:3]).then_inc(s_ln, 1)
                    if z2tl is not None:
                        if li >= 1:
                            v.wait_ge(s_tr3, NK * li)
                        v.drain()
                        v.tensor_copy(z2b[:], z1t[:]).then_inc(s_zb, 1)

        @blk.sync
        def _(sy):
            gi = 0
            for hc in range(NH0):
                for tcx in range(4):
                    sy.wait_ge(s_hb, gi + 1)
                    sy.dma_start(out=hidT[hc * 128:(hc + 1) * 128,
                                          tcx * 512:(tcx + 1) * 512],
                                 in_=hb[:, (gi % 2) * 512:(gi % 2 + 1) * 512]
                                 ).then_inc(sthp[gi % 2], 16)
                    gi += 1
            sy.wait_ge(sthp[0], 16 * 128)
            sy.wait_ge(sthp[1], 16 * 128)
            sy.wait_ge(st, 16 * NT)
            for w in range(4):
                for hc in range(NH0):
                    if w >= 1:
                        sy.wait_ge(s_w2c, w * 4 * NH0)
                    sy.dma_start(out=hw_sb[:, hc * 512:(hc + 1) * 512],
                                 in_=hidT[hc * 128:(hc + 1) * 128,
                                          w * 512:(w + 1) * 512]).then_inc(ldh, 16)
                for ttl in range(4):
                    if w >= 1:
                        sy.wait_ge(s_ln, w * 4)
                    sy.dma_start(out=accs[ttl][:],
                                 in_=z1f[w * 512 + ttl * 128:w * 512 + (ttl + 1) * 128,
                                         :]).then_inc(ldz, 16)
                for ttl in range(4):
                    li = w * 4 + ttl
                    sy.wait_ge(s_ln, li + 1)
                    sy.dma_start(out=z2o[w * 512 + ttl * 128:w * 512 + (ttl + 1) * 128, :],
                                 in_=z1t[:]).then_inc(st2, 16)
                    if z2tl is not None:
                        li = w * 4 + ttl
                        sy.wait_ge(s_te3, NK * (li + 1))
                        sy.dma_start(out=z2tl.rearrange("(k p) t -> p k t", p=128)[
                                         :, :, li * 128:(li + 1) * 128],
                                     in_=z2t_sb[:].rearrange("p (k t) -> p k t", k=NK)
                                     ).then_inc(stz, 16)
            sy.wait_ge(st2, 16 * NT)
            if z2tl is not None:
                sy.wait_ge(stz, 16 * NT)




def build_B():
    nc = bass.Bass("TRN2", target_bir_lowering=False)
    io = dict(
        dtTs=nc.dram_tensor("dtTs", [256, TOK], F32, kind="ExternalInput").ap(),
        xTs=nc.dram_tensor("xTs", [256, TOK], F32, kind="ExternalInput").ap(),
        Ash=nc.dram_tensor("Ash", [128, 32], F32, kind="ExternalInput").ap(),
        Dsh=nc.dram_tensor("Dsh", [128, 2], F32, kind="ExternalInput").ap(),
        BT=nc.dram_tensor("BT", [NST, TOK], F32, kind="ExternalInput").ap(),
        CT=nc.dram_tensor("CT", [NST, TOK], F32, kind="ExternalInput").ap(),
        idt=nc.dram_tensor("idt", [128, 128], BF16, kind="ExternalInput").ap(),
        yTs=nc.dram_tensor("yTs", [256, TOK], F32, kind="ExternalOutput").ap(),
        ytr=None)
    with ExitStack() as ctx:
        phase_B(nc, NcProxy(nc, "B_"), ctx.enter_context, io)
    return nc


def phase_B(nc, ncx, e, io):
    dtTs = io["dtTs"]; xTs = io["xTs"]; Ash = io["Ash"]; Dsh = io["Dsh"]
    BT = io["BT"]; CT = io["CT"]; idt = io["idt"]; yTs = io["yTs"]
    ytr = io["ytr"]

    NTC = TOK // 512
    NITER = NTC * 2

    if True:
        btst = e(ncx.sbuf_tensor("btst", [1, 2 * NST * 512], BF16))
        ctst = e(ncx.sbuf_tensor("ctst", [1, 2 * NST * 512], BF16))
        alog_sb = e(ncx.sbuf_tensor("alog_sb", [128, 32], F32))
        a_sb = e(ncx.sbuf_tensor("a_sb", [128, 32], F32))
        d_sb = e(ncx.sbuf_tensor("d_sb", [128, 2], F32))
        ones1 = e(ncx.sbuf_tensor("ones1", [1, 128], BF16))
        idt_sb = e(ncx.sbuf_tensor("idt_sb", [128, 128], BF16))
        dt_t = e(ncx.sbuf_tensor("dt_t", [128, 2 * 512], F32))
        x_t = e(ncx.sbuf_tensor("x_t", [128, 2 * 512], F32))
        u_sb = e(ncx.sbuf_tensor("u_sb", [128, 512], F32))
        dA_sb = e(ncx.sbuf_tensor("dA_sb", [128, 2 * 512], F32))
        dbx_sb = e(ncx.sbuf_tensor("dbx_sb", [128, 2 * 512], F32))
        h_sb = e(ncx.sbuf_tensor("h_sb", [128, 2 * 512], F32))
        yt_bf = e(ncx.sbuf_tensor("yt_bf", [128, 2 * 512], BF16))
        yo_sb = e(ncx.sbuf_tensor("yo_sb", [128, 2 * 512], F32))
        state = e(ncx.sbuf_tensor("state", [128, 32], F32))
        if ytr is not None:
            ytr_sb = e(ncx.sbuf_tensor("ytr_sb", [128, 2 * 512], F32))
            if32 = e(ncx.sbuf_tensor("if32", [128, 128], F32))
            p_yt = e(ncx.psum_tensor("p_yt", [128, 128], F32))
            s_tr2 = e(ncx.semaphore("s_tr2"))
            s_te2 = e(ncx.semaphore("s_te2"))
        p_br = e(ncx.psum_tensor("p_br", [128, 2 * 512], F32))
        p_cr = e(ncx.psum_tensor("p_cr", [128, 3 * 512], F32))
        p_y = e(ncx.psum_tensor("p_y", [128, 2 * 512], F32))
        ld = e(ncx.semaphore("ld"))
        ldtp = [e(ncx.semaphore("ldt0")), e(ncx.semaphore("ldt1"))]
        ldbp = [e(ncx.semaphore("ldb0")), e(ncx.semaphore("ldb1"))]
        s_t = e(ncx.semaphore("s_t"))
        s_a = e(ncx.semaphore("s_a"))
        s_vb = e(ncx.semaphore("s_vb"))
        s_vs = e(ncx.semaphore("s_vs"))
        s_vy = e(ncx.semaphore("s_vy"))
        s_ty = e(ncx.semaphore("s_ty"))
        s_y = e(ncx.semaphore("s_y"))
        s_ai = e(ncx.semaphore("s_ai"))
        stpB = [e(ncx.semaphore("stB0")), e(ncx.semaphore("stB1"))]
        blk = e(nc.Block())

        @blk.sync
        def _(sy):
            sy.dma_start(out=alog_sb[:], in_=Ash[:]).then_inc(ld, 16)
            sy.dma_start(out=d_sb[:], in_=Dsh[:]).then_inc(ld, 16)
            sy.dma_start(out=idt_sb[:], in_=idt[:]).then_inc(ld, 16)

        @blk.gpsimd
        def _(g):
            for i in range(NITER):
                tc, gr = i // 2, i % 2
                if gr == 0:
                    if tc >= 2:
                        g.wait_ge(s_t, 32 * (tc - 1))
                    sb = slice((tc % 2) * NST * 512, (tc % 2 + 1) * NST * 512)
                    g.dma_start(out=btst[0:1, sb],
                                in_=BT[:, tc * 512:(tc + 1) * 512]).then_inc(ldbp[tc % 2], 16)
                    g.dma_start(out=ctst[0:1, sb],
                                in_=CT[:, tc * 512:(tc + 1) * 512]).then_inc(ldbp[tc % 2], 16)
                if i >= 2:
                    g.wait_ge(s_y, i - 1)
                b = i % 2
                g.dma_start(out=dt_t[:, b * 512:(b + 1) * 512],
                            in_=dtTs[gr * 128:(gr + 1) * 128,
                                     tc * 512:(tc + 1) * 512]).then_inc(ldtp[i % 2], 16)
                g.dma_start(out=x_t[:, b * 512:(b + 1) * 512],
                            in_=xTs[gr * 128:(gr + 1) * 128,
                                    tc * 512:(tc + 1) * 512]).then_inc(ldtp[i % 2], 16)

        @blk.tensor
        def _(t):
            t.wait_ge(s_ai, 1)
            nrep = 0
            for i in range(NITER):
                tc = i // 2
                t.wait_ge(ldbp[tc % 2], 32 * (tc // 2 + 1))
                base = (tc % 2) * NST * 512
                if i >= 2:
                    t.wait_ge(s_y, i - 1)
                for n in range(NST):
                    nb = nrep % 2
                    psl = slice(nb * 512, (nb + 1) * 512)
                    psl3 = slice((nrep % 3) * 512, (nrep % 3) * 512 + 512)
                    ssl = slice(base + n * 512, base + (n + 1) * 512)
                    if nrep >= 2:
                        t.wait_ge(s_vb, nrep - 1)
                    if nrep >= 3:
                        t.wait_ge(s_vy, nrep - 2)
                    t.matmul(p_br[:, psl], ones1[:], btst[0:1, ssl],
                             start=True, stop=True)
                    mm = t.matmul(p_cr[:, psl3], ones1[:], ctst[0:1, ssl],
                                  start=True, stop=True)
                    mm.then_inc(s_t, 1)
                    if n >= 2:
                        nn = nrep - 2
                        t.wait_ge(s_vy, nn + 1)
                        t.matmul(p_y[:, (i % 2) * 512:(i % 2 + 1) * 512],
                                 idt_sb[:], yt_bf[:, (nn % 2) * 512:(nn % 2 + 1) * 512],
                                 start=(nn % NST == 0), stop=False).then_inc(s_ty, 1)
                    nrep += 1
                for nn in range(nrep - 2, nrep):
                    t.wait_ge(s_vy, nn + 1)
                    t.matmul(p_y[:, (i % 2) * 512:(i % 2 + 1) * 512],
                             idt_sb[:], yt_bf[:, (nn % 2) * 512:(nn % 2 + 1) * 512],
                             start=False, stop=(nn % NST == NST - 1)).then_inc(s_ty, 1)
                if ytr is not None and i >= 1:
                    j = i - 1
                    t.wait_ge(s_y, j + 1)
                    for q in range(4):
                        ti = 4 * j + q
                        if ti >= 1:
                            t.wait_ge(s_te2, ti)
                        t.transpose(p_yt[:],
                                    yo_sb[:, (j % 2) * 512 + q * 128:(j % 2) * 512 + (q + 1) * 128],
                                    if32[:]).then_inc(s_tr2, 1)
            if ytr is not None:
                j = NITER - 1
                t.wait_ge(s_y, j + 1)
                for q in range(4):
                    ti = 4 * j + q
                    t.wait_ge(s_te2, ti)
                    t.transpose(p_yt[:],
                                yo_sb[:, (j % 2) * 512 + q * 128:(j % 2) * 512 + (q + 1) * 128],
                                if32[:]).then_inc(s_tr2, 1)

        @blk.scalar
        def _(a):
            a.wait_ge(ld, 48)
            a.activation(a_sb[:], alog_sb[:], AF.Exp).then_inc(s_a, 1)
            a.wait_ge(s_ai, 1)
            nrep = 0
            for i in range(NITER):
                for n in range(NST):
                    nb = nrep % 2
                    a.wait_ge(ldtp[i % 2], 32 * (i // 2 + 1))
                    if nrep >= 2:
                        a.wait_ge(s_vs, nrep - 1)
                    a.activation(dA_sb[:, nb * 512:(nb + 1) * 512],
                                 dt_t[:, (i % 2) * 512:(i % 2 + 1) * 512],
                                 AF.Exp,
                                 scale=a_sb[:, (i % 2) * 16 + n:(i % 2) * 16 + n + 1]
                                 ).then_inc(s_a, 1)
                    nrep += 1
                if ytr is not None and i >= 1:
                    j = i - 1
                    if j >= 2:
                        a.wait_ge(stpB[j % 2], 64 * (j // 2))
                    for q in range(4):
                        ti = 4 * j + q
                        a.wait_ge(s_tr2, ti + 1)
                        a.drain()
                        a.activation(ytr_sb[:, (j % 2) * 512 + q * 128:(j % 2) * 512 + (q + 1) * 128],
                                     p_yt[:],
                                     AF.Identity).then_inc(s_te2, 1)
            if ytr is not None:
                j = NITER - 1
                a.wait_ge(stpB[j % 2], 64 * (j // 2))
                for q in range(4):
                    ti = 4 * j + q
                    a.wait_ge(s_tr2, ti + 1)
                    a.drain()
                    a.activation(ytr_sb[:, (j % 2) * 512 + q * 128:(j % 2) * 512 + (q + 1) * 128],
                                 p_yt[:],
                                 AF.Identity).then_inc(s_te2, 1)

        @blk.vector
        def _(v):
            v.memset(ones1[:], 1.0)
            v.memset(state[:], 0.0)
            v.wait_ge(ld, 48)
            if ytr is not None:
                v.tensor_copy(if32[:], idt_sb[:])
            v.wait_ge(s_a, 1)
            v.tensor_scalar_mul(a_sb[:], a_sb[:], -1.0)
            v.drain()
            v.engine_nop().then_inc(s_ai, 1)
            nrep = 0
            for i in range(NITER):
                tc, gr = i // 2, i % 2
                b = i % 2
                bsl = slice(b * 512, (b + 1) * 512)
                v.wait_ge(ldtp[i % 2], 32 * (i // 2 + 1))
                v.drain()
                v.tensor_mul(u_sb[:], dt_t[:, bsl], x_t[:, bsl])
                v.drain()

                def em_dbx(n):
                    g = nrep0 + n
                    nb = g % 2
                    if g >= 2:
                        v.wait_ge(s_vs, g - 1)
                    v.wait_ge(s_t, g + 1)
                    v.tensor_mul(dbx_sb[:, nb * 512:(nb + 1) * 512], u_sb[:],
                                 p_br[:, nb * 512:(nb + 1) * 512]).then_inc(s_vb, 1)

                def em_scan(n):
                    g = nrep0 + n
                    nb = g % 2
                    scol = gr * 16 + n
                    v.wait_ge(s_vb, g + 1)
                    v.wait_ge(s_a, g + 2)
                    if g >= 2:
                        v.wait_ge(s_vy, g - 1)
                    init = 0.0 if tc % 16 == 0 else state[:, scol:scol + 1]
                    v.tensor_tensor_scan(h_sb[:, nb * 512:(nb + 1) * 512],
                                         dA_sb[:, nb * 512:(nb + 1) * 512],
                                         dbx_sb[:, nb * 512:(nb + 1) * 512],
                                         init, ALU.mult, ALU.add).then_inc(s_vs, 1)

                def em_tail(n):
                    g = nrep0 + n
                    nb = g % 2
                    scol = gr * 16 + n
                    v.wait_ge(s_vs, g + 1)
                    v.tensor_copy(state[:, scol:scol + 1],
                                  h_sb[:, nb * 512 + 511:nb * 512 + 512])
                    if g >= 2:
                        v.wait_ge(s_ty, g - 1)
                    v.tensor_mul(yt_bf[:, nb * 512:(nb + 1) * 512],
                                 h_sb[:, nb * 512:(nb + 1) * 512],
                                 p_cr[:, (g % 3) * 512:(g % 3) * 512 + 512]).then_inc(s_vy, 1)

                nrep0 = nrep
                em_dbx(0)
                em_dbx(1)
                em_scan(0)
                for n in range(2, NST):
                    em_dbx(n)
                    em_scan(n - 1)
                    em_tail(n - 2)
                em_scan(NST - 1)
                em_tail(NST - 2)
                em_tail(NST - 1)
                nrep += NST
                v.wait_ge(s_ty, nrep)
                if i >= 2:
                    if ytr is not None:
                        v.wait_ge(s_tr2, 4 * (i - 1))
                    else:
                        v.wait_ge(stpB[i % 2], 16 * (i // 2))
                v.drain()
                v.scalar_tensor_tensor(yo_sb[:, bsl], x_t[:, bsl],
                                       d_sb[:, gr:gr + 1],
                                       p_y[:, b * 512:(b + 1) * 512],
                                       ALU.mult, ALU.add).then_inc(s_y, 1)

        @blk.sync
        def _(sy):
            if ytr is None:
                for i in range(NITER):
                    tc, gr = i // 2, i % 2
                    b = i % 2
                    sy.wait_ge(s_y, i + 1)
                    sy.dma_start(out=yTs[gr * 128:(gr + 1) * 128,
                                         tc * 512:(tc + 1) * 512],
                                 in_=yo_sb[:, b * 512:(b + 1) * 512]).then_inc(stpB[i % 2], 16)
                sy.wait_ge(stpB[0], 16 * NITER // 2)
                sy.wait_ge(stpB[1], 16 * NITER // 2)
            else:
                for j in range(NITER):
                    tc, gr = j // 2, j % 2
                    for q in range(4):
                        ti = 4 * j + q
                        sy.wait_ge(s_te2, ti + 1)
                        sy.dma_start(out=ytr[tc * 512 + q * 128:tc * 512 + (q + 1) * 128,
                                             gr * 128:(gr + 1) * 128],
                                     in_=ytr_sb[:, (j % 2) * 512 + q * 128:(j % 2) * 512 + (q + 1) * 128]
                                     ).then_inc(stpB[j % 2], 16)
                sy.wait_ge(stpB[0], 64 * NITER // 2)
                sy.wait_ge(stpB[1], 64 * NITER // 2)




def build_D():
    nc = bass.Bass("TRN2", target_bir_lowering=False)
    z2T = nc.dram_tensor("z2T", [DM, SEQ], F32, kind="ExternalInput").ap()
    io = dict(
        wq=nc.dram_tensor("wq", [DM, 512], F32, kind="ExternalInput").ap(),
        wk=nc.dram_tensor("wk", [DM, 512], F32, kind="ExternalInput").ap(),
        wv=nc.dram_tensor("wv", [DM, 512], F32, kind="ExternalInput").ap(),
        masks=nc.dram_tensor("masks", [128, 5 * 512], BF16, kind="ExternalInput").ap(),
        Oc=nc.dram_tensor("Oc", [512, SEQ], BF16, kind="ExternalOutput").ap(),
        z2w_tile=lambda k, w4: z2T[k * 128:(k + 1) * 128,
                                   w4 * 2048:(w4 + 1) * 2048])
    with ExitStack() as ctx:
        phase_D(nc, NcProxy(nc, "D_"), ctx.enter_context, io)
    return nc


def phase_D(nc, ncx, e, io):
    wq = io["wq"]; wk = io["wk"]; wv = io["wv"]; masks = io["masks"]
    Oc = io["Oc"]; z2w_tile = io["z2w_tile"]

    NK = DM // 128
    NQC = SEQ // 512
    SCL = float(DH) ** -0.5

    if True:
        z2w = e(ncx.sbuf_tensor("z2w", [128, NK * 2048], BF16))
        wq_sb = e(ncx.sbuf_tensor("wq_sb", [128, NK * 128], BF16))
        wk_sb = e(ncx.sbuf_tensor("wk_sb", [128, NK * 128], BF16))
        wv_sb = e(ncx.sbuf_tensor("wv_sb", [128, NK * 128], BF16))
        QT = e(ncx.sbuf_tensor("QT", [128, SEQ], BF16))
        KT = e(ncx.sbuf_tensor("KT", [128, SEQ], BF16))
        V = e(ncx.sbuf_tensor("V", [128, SEQ], BF16))
        msk = e(ncx.sbuf_tensor("msk", [128, 5 * 512], BF16))
        pU = e(ncx.sbuf_tensor("pU", [128, 2 * 512], BF16))
        pM = e(ncx.sbuf_tensor("pM", [128, 4 * 512], BF16))
        rs_sb = e(ncx.sbuf_tensor("rs_sb", [1, 512], F32))
        rc_sb = e(ncx.sbuf_tensor("rc_sb", [1, 512], F32))
        vsum = e(ncx.sbuf_tensor("vsum", [128, 2 * 512], F32))
        ones_kf = e(ncx.sbuf_tensor("ones_kf", [128, 1], F32))
        rep_sb = e(ncx.sbuf_tensor("rep_sb", [128, 512], F32))
        Ob = e(ncx.sbuf_tensor("Ob", [128, 2 * 512], BF16))
        n20 = e(ncx.sbuf_tensor("n20", [128, 1], F32))
        ones_k = e(ncx.sbuf_tensor("ones_k", [128, 1], BF16))
        ones_r = e(ncx.sbuf_tensor("ones_r", [1, 128], F32))
        p_q = e(ncx.psum_tensor("p_q", [128, 2 * 512], F32))
        p_s = e(ncx.psum_tensor("p_s", [128, 2 * 512], F32))
        p_rs = e(ncx.psum_tensor("p_rs", [1, 512], F32))
        p_o = e(ncx.psum_tensor("p_o", [128, 512], F32))
        p_rep = e(ncx.psum_tensor("p_rep", [128, 512], F32))
        ld0 = e(ncx.semaphore("ld0"))
        ldz = e(ncx.semaphore("ldz"))
        ldw = e(ncx.semaphore("ldw"))
        s_qkv = e(ncx.semaphore("s_qkv"))
        s_qe = e(ncx.semaphore("s_qe"))
        s_s = e(ncx.semaphore("s_s"))
        s_p = e(ncx.semaphore("s_p"))
        s_pm = e(ncx.semaphore("s_pm"))
        s_av = e(ncx.semaphore("s_av"))
        s_vv = e(ncx.semaphore("s_vv"))
        s_vs = e(ncx.semaphore("s_vs"))
        s_rc = e(ncx.semaphore("s_rc"))
        s_rp = e(ncx.semaphore("s_rp"))
        s_oe = e(ncx.semaphore("s_oe"))
        s_vc = e(ncx.semaphore("s_vc"))
        stpD = [e(ncx.semaphore("stD0")), e(ncx.semaphore("stD1"))]
        blk = e(nc.Block())

        def sched():
            ai = 0
            diag_ai = []
            plan = []
            for h in range(4):
                for qc in range(NQC):
                    for kt in range(4 * qc + 4):
                        diag = kt >= 4 * qc
                        di = len(diag_ai) if diag else -1
                        if diag:
                            diag_ai.append(ai)
                        plan.append((h, qc, kt, diag, ai, di))
                        ai += 1
            return plan, diag_ai
        PLAN, DIAG_AI = sched()
        NAI = len(PLAN)

        @blk.sync
        def _(sy):
            sy.dma_start(out=msk[:], in_=masks[:]).then_inc(ld0, 16)
            oi = 0
            for h in range(4):
                for qc in range(NQC):
                    sy.wait_ge(s_oe, oi + 1)
                    sy.dma_start(out=Oc[h * 128:(h + 1) * 128,
                                        qc * 512:(qc + 1) * 512],
                                 in_=Ob[:, (oi % 2) * 512:(oi % 2 + 1) * 512]
                                 ).then_inc(stpD[oi % 2], 16)
                    oi += 1
            sy.wait_ge(stpD[0], 16 * 32)
            sy.wait_ge(stpD[1], 16 * 32)

        @blk.gpsimd
        def _(g):
            for h in range(4):
                if h >= 1:
                    g.wait_ge(s_oe, 16 * h)
                g.dma_start(out=wq_sb[:].rearrange("p (k j) -> p k j", k=NK),
                            in_=wq.rearrange("(k p) j -> p k j", p=128)[
                                :, :, h * 128:(h + 1) * 128]).then_inc(ldw, 16)
                g.dma_start(out=wk_sb[:].rearrange("p (k j) -> p k j", k=NK),
                            in_=wk.rearrange("(k p) j -> p k j", p=128)[
                                :, :, h * 128:(h + 1) * 128]).then_inc(ldw, 16)
                g.dma_start(out=wv_sb[:].rearrange("p (k j) -> p k j", k=NK),
                            in_=wv.rearrange("(k p) j -> p k j", p=128)[
                                :, :, h * 128:(h + 1) * 128]).then_inc(ldw, 16)
                for w4 in range(4):
                    if 4 * h + w4 >= 1:
                        g.wait_ge(s_qkv, 24 * (4 * h + w4))
                    for k in range(NK):
                        g.dma_start(out=z2w[:, k * 2048:(k + 1) * 2048],
                                    in_=z2w_tile(k, w4)).then_inc(ldz, 16)

        @blk.tensor
        def _(t):
            t.wait_ge(ld0, 16)
            t.wait_ge(s_vc, 1)
            qi = 0
            for h in range(4):
                t.wait_ge(ldw, 48 * (h + 1))
                for w4 in range(4):
                    t.wait_ge(ldz, 256 * (4 * h + w4 + 1))
                    for which, wsb in ((0, wq_sb), (1, wk_sb)):
                        for tcq in range(4):
                            if qi >= 2:
                                t.wait_ge(s_qe, qi - 1)
                            pb = (qi % 2) * 512
                            for k in range(NK):
                                mm = t.matmul(
                                    p_q[:, pb:pb + 512],
                                    wsb[:, k * 128:(k + 1) * 128],
                                    z2w[:, k * 2048 + tcq * 512:k * 2048 + tcq * 512 + 512],
                                    start=(k == 0), stop=(k == NK - 1))
                            mm.then_inc(s_qkv, 1)
                            qi += 1
                    for tt in range(16):
                        if qi >= 2:
                            t.wait_ge(s_qe, qi - 1)
                        pb = (qi % 2) * 512
                        for k in range(NK):
                            mm = t.matmul(
                                p_q[:, pb:pb + 128],
                                z2w[:, k * 2048 + tt * 128:k * 2048 + (tt + 1) * 128],
                                wv_sb[:, k * 128:(k + 1) * 128],
                                start=(k == 0), stop=(k == NK - 1))
                        mm.then_inc(s_qkv, 1)
                        qi += 1
                for qc in range(NQC):
                    gqc = h * NQC + qc
                    if gqc >= 1:
                        t.wait_ge(s_oe, gqc)
                    nkt = 4 * qc + 4
                    base = AI_BASE[(h, qc)]

                    def emit_scores(kt):
                        ai = base + kt
                        if ai >= 2:
                            t.wait_ge(s_p, ai - 1)
                        sb_ = (ai % 2) * 512
                        t.matmul(p_s[:, sb_:sb_ + 512],
                                 KT[:, kt * 128:(kt + 1) * 128],
                                 QT[:, qc * 512:(qc + 1) * 512],
                                 start=True, stop=True).then_inc(s_s, 1)

                    def emit_av(kt):
                        ai = base + kt
                        if kt >= 4 * qc:
                            di = DI_OF[ai]
                            t.wait_ge(s_pm, di + 1)
                            rhs = pM[:, (di % 4) * 512:(di % 4 + 1) * 512]
                        else:
                            t.wait_ge(s_p, ai + 1)
                            rhs = pU[:, (ai % 2) * 512:(ai % 2 + 1) * 512]
                        mm = t.matmul(p_o[:], V[:, kt * 128:(kt + 1) * 128], rhs,
                                      start=(kt == 0), stop=(kt == nkt - 1))
                        mm.then_inc(s_av, 1)

                    emit_scores(0)
                    for kt in range(1, nkt):
                        emit_scores(kt)
                        emit_av(kt - 1)
                    emit_av(nkt - 1)
                    t.wait_ge(s_vs, gqc + 1)
                    t.matmul(p_rs[:], ones_kf[:], vsum[:, 0:512],
                             start=True, stop=True).then_inc(s_s, 1)
                    t.wait_ge(s_rc, gqc + 1)
                    t.matmul(p_rep[:], ones_r[:], rc_sb[:],
                             start=True, stop=True).then_inc(s_rp, 1)

        @blk.scalar
        def _(a):
            qi = 0
            for h in range(4):
                for w4 in range(4):
                    for which in range(2):
                        for tcq in range(4):
                            a.wait_ge(s_qkv, qi + 1)
                            dst = (QT if which == 0 else KT)[
                                :, w4 * 2048 + tcq * 512:w4 * 2048 + tcq * 512 + 512]
                            pb = (qi % 2) * 512
                            if which == 0:
                                a.activation(dst, p_q[:, pb:pb + 512],
                                             AF.Identity, scale=SCL).then_inc(s_qe, 1)
                            else:
                                a.activation(dst, p_q[:, pb:pb + 512],
                                             AF.Identity).then_inc(s_qe, 1)
                            qi += 1
                    for tt in range(16):
                        a.wait_ge(s_qkv, qi + 1)
                        pb = (qi % 2) * 512
                        a.activation(V[:, (w4 * 16 + tt) * 128:(w4 * 16 + tt + 1) * 128],
                                     p_q[:, pb:pb + 128],
                                     AF.Identity).then_inc(s_qe, 1)
                        qi += 1
                for qc in range(NQC):
                    nkt = 4 * qc + 4
                    for kt in range(nkt):
                        ai = AI_BASE[(h, qc)] + kt + (h * NQC + qc)
                        ai0 = AI_BASE[(h, qc)] + kt
                        a.wait_ge(s_s, ai + 1)
                        if ai0 >= 2:
                            a.wait_ge(s_av, ai0 - 1)
                            a.wait_ge(s_vv, ai0 - 1)
                        sb_ = (ai0 % 2) * 512
                        a.activation(pU[:, (ai0 % 2) * 512:(ai0 % 2 + 1) * 512],
                                     p_s[:, sb_:sb_ + 512], AF.Exp,
                                     bias=n20[:]).then_inc(s_p, 1)

        @blk.vector
        def _(v):
            v.memset(n20[:], -20.0)
            v.memset(ones_k[:], 1.0)
            v.memset(ones_kf[:], 1.0)
            v.memset(ones_r[:], 1.0).then_inc(s_vc, 1)
            v.wait_ge(ld0, 16)
            for h in range(4):
                for qc in range(NQC):
                    gqc = h * NQC + qc
                    nkt = 4 * qc + 4
                    for kt in range(nkt):
                        ai = AI_BASE[(h, qc)] + kt
                        sp_ai = ai + gqc
                        diag = kt >= 4 * qc
                        v.wait_ge(s_p, ai + 1)
                        if diag:
                            di = DI_OF[ai]
                            r = kt - 4 * qc
                            mc = 4 if (qc == 0 and kt == 0) else r
                            if di >= 4:
                                v.wait_ge(s_av, DIAG_AI[di - 4] + 1)
                            v.tensor_mul(pM[:, (di % 4) * 512:(di % 4 + 1) * 512],
                                         pU[:, (ai % 2) * 512:(ai % 2 + 1) * 512],
                                         msk[:, mc * 512:(mc + 1) * 512]).then_inc(s_pm, 1)
                            v.drain()
                            src_ap = pM[:, (di % 4) * 512:(di % 4 + 1) * 512]
                        else:
                            src_ap = pU[:, (ai % 2) * 512:(ai % 2 + 1) * 512]
                        half = slice((kt % 2) * 512, (kt % 2) * 512 + 512)
                        if kt < 2:
                            if gqc >= 1 and kt == 0:
                                v.wait_ge(s_s, AI_BASE[(h, qc)] + gqc)
                            v.tensor_copy(vsum[:, half], src_ap).then_inc(s_vv, 1)
                        else:
                            v.wait_ge(s_vv, AI_BASE[(h, qc)] + kt - 1)
                            v.tensor_add(vsum[:, half], vsum[:, half],
                                         src_ap).then_inc(s_vv, 1)
                        if kt == nkt - 1:
                            v.drain()
                            v.tensor_add(vsum[:, 0:512], vsum[:, 0:512],
                                         vsum[:, 512:1024])
                            v.drain()
                            v.engine_nop().then_inc(s_vs, 1)
                    v.wait_ge(s_s, AI_BASE[(h, qc)] + nkt + gqc + 1)
                    v.drain()
                    v.tensor_copy(rs_sb[:], p_rs[:])
                    v.drain()
                    v.reciprocal(rc_sb[:], rs_sb[:]).then_inc(s_rc, 1)
                    v.wait_ge(s_rp, gqc + 1)
                    v.drain()
                    v.tensor_copy(rep_sb[:], p_rep[:])
                    ai_last = AI_BASE[(h, qc)] + nkt - 1
                    v.wait_ge(s_av, ai_last + 1)
                    if gqc >= 2:
                        v.wait_ge(stpD[gqc % 2], 16 * (gqc // 2))
                    v.drain()
                    v.tensor_mul(Ob[:, (gqc % 2) * 512:(gqc % 2 + 1) * 512],
                                 p_o[:], rep_sb[:]).then_inc(s_oe, 1)


AI_BASE = {}
DI_OF = {}
def _init_attn_sched():
    ai = 0
    di = 0
    for h in range(4):
        for qc in range(16):
            AI_BASE[(h, qc)] = ai
            for kt in range(4 * qc + 4):
                if kt >= 4 * qc:
                    DI_OF[ai] = di
                    di += 1
                ai += 1
_init_attn_sched()




def build_F():
    nc = bass.Bass("TRN2", target_bir_lowering=False)
    OTs = nc.dram_tensor("OTs", [DM, TSL], BF16, kind="ExternalInput").ap()
    wguT = nc.dram_tensor("wguT", [DM, 2 * FI], BF16, kind="ExternalInput").ap()
    wdnT = nc.dram_tensor("wdnT", [FI, DM], BF16, kind="ExternalInput").ap()
    io = dict(
        z2s=nc.dram_tensor("z2s", [TSL, DM], F32, kind="ExternalInput").ap(),
        wout=nc.dram_tensor("wout", [DM, DM], F32, kind="ExternalInput").ap(),
        boutr=nc.dram_tensor("boutr", [128, DM], F32, kind="ExternalInput").ap(),
        idt=nc.dram_tensor("idt", [128, 128], BF16, kind="ExternalInput").ap(),
        outp=nc.dram_tensor("outp", [TSL, DM], F32, kind="ExternalOutput").ap(),
        x3f=nc.dram_tensor("x3f", [TSL, DM], F32, kind="Internal").ap(),
        x3Tf=nc.dram_tensor("x3Tf", [DM, TSL], BF16, kind="Internal").ap(),
        ot_tile=lambda eng, tt: OTs.rearrange("(k p) t -> p k t", p=128)[
            :, :, tt * 128:(tt + 1) * 128],
        wg_tile=lambda pc: wguT.rearrange("(k p) o -> p k o", p=128)[
            :, :, pc * 128:(pc + 1) * 128],
        wu_tile=lambda pc: wguT.rearrange("(k p) o -> p k o", p=128)[
            :, :, FI + pc * 128:FI + (pc + 1) * 128],
        wd_tile=lambda ic, dc: wdnT[ic * 128:(ic + 1) * 128,
                                    dc * 512:(dc + 1) * 512].rearrange(
                                        "p (r d) -> p r d", r=2))
    with ExitStack() as ctx:
        phase_F(nc, NcProxy(nc, "F_"), ctx.enter_context, io)
    return nc


def phase_F(nc, ncx, e, io):
    cut = io.get("cut")
    z2s = io["z2s"]; wout = io["wout"]; boutr = io["boutr"]; idt = io["idt"]
    outp = io["outp"]; x3f = io["x3f"]; x3Tf = io["x3Tf"]
    ot_tile = io["ot_tile"]; wg_tile = io["wg_tile"]; wu_tile = io["wu_tile"]
    wd_tile = io["wd_tile"]

    NT = TSL // 128
    NK = DM // 128
    NPC = FI // 128
    INV = 1.0 / DM
    import os as _osf
    NWN = int(_osf.environ.get("MEGA_FNWN", "4"))
    WL = [int(s) for s in _osf.environ.get(
        "MEGA_FWL", ",".join(str(i) for i in range(NWN))).split(",")]

    if True:
        big = e(ncx.sbuf_tensor("big", [128, NK * DM], BF16))
        x3w = e(ncx.sbuf_tensor("x3w", [128, NK * 512], BF16))
        x3ev = e(ncx.sbuf_tensor("x3ev", [128, NK * 128], BF16))
        otile = e(ncx.sbuf_tensor("otile", [128, NK * 128], BF16))
        z2t = e(ncx.sbuf_tensor("z2t", [128, DM], F32))
        xrd = e(ncx.sbuf_tensor("xrd", [128, DM], F32))
        x3b = e(ncx.sbuf_tensor("x3b", [128, DM], BF16))
        accs = [e(ncx.sbuf_tensor(f"acc{i}", [128, DM], F32)) for i in range(4)]
        outt = e(ncx.sbuf_tensor("outt", [128, 2 * DM], F32))
        wg_sb = e(ncx.sbuf_tensor("wg_sb", [128, 2 * NK * 128], BF16))
        wu_sb = e(ncx.sbuf_tensor("wu_sb", [128, 2 * NK * 128], BF16))
        wd_sb = e(ncx.sbuf_tensor("wd_sb", [128, 8 * 512], BF16))
        sg = e(ncx.sbuf_tensor("sg", [128, 512], BF16))
        mcol = e(ncx.sbuf_tensor("mcol", [128, 4], F32))
        eps5 = e(ncx.sbuf_tensor("eps5", [128, 1], F32))
        eps6 = e(ncx.sbuf_tensor("eps6", [128, 1], F32))
        idt_sb = e(ncx.sbuf_tensor("idt_sb", [128, 128], BF16))
        p_op = e(ncx.psum_tensor("p_op", [128, 512], F32))
        p_tr = e(ncx.psum_tensor("p_tr", [128, 128], BF16))
        p_g = e(ncx.psum_tensor("p_g", [128, 512], F32))
        p_u = e(ncx.psum_tensor("p_u", [128, 512], F32))
        p_d = [e(ncx.psum_tensor(f"p_d{i}", [128, 512], F32)) for i in range(4)]
        ld0 = e(ncx.semaphore("ld0"))
        ldwt = e(ncx.semaphore("ldwt"))
        ld1 = e(ncx.semaphore("ld1"))
        ldgp = [e(ncx.semaphore("ldgA")), e(ncx.semaphore("ldgB"))]
        lddp = [e(ncx.semaphore("ldd%d" % qq)) for qq in range(8)]
        ldx = e(ncx.semaphore("ldx"))
        s_op = e(ncx.semaphore("s_op"))
        s_ev1 = e(ncx.semaphore("s_ev1"))
        s_e1 = e(ncx.semaphore("s_e1"))
        s_xm = e(ncx.semaphore("s_xm"))
        s_sd = e(ncx.semaphore("s_sd"))
        s_tr = e(ncx.semaphore("s_tr"))
        s_te = e(ncx.semaphore("s_te"))
        s_gu = e(ncx.semaphore("s_gu"))
        s_sg = e(ncx.semaphore("s_sg"))
        s_hb = e(ncx.semaphore("s_hb"))
        s_dc = e(ncx.semaphore("s_dc"))
        s_ev2 = e(ncx.semaphore("s_ev2"))
        s_rm = e(ncx.semaphore("s_rm"))
        s_fin = e(ncx.semaphore("s_fin"))
        st1 = e(ncx.semaphore("st1"))
        stx = e(ncx.semaphore("stx"))
        ldxw = e(ncx.semaphore("ldxw"))
        st2p = [e(ncx.semaphore("st2A")), e(ncx.semaphore("st2B"))]
        blk = e(nc.Block())

        @blk.sync
        def _(sy):
            sy.dma_start(out=idt_sb[:], in_=idt[:]).then_inc(ld0, 16)
            sy.dma_start(out=xrd[:], in_=boutr[:]).then_inc(ld0, 16)
            if cut == "ld":
                sy.wait_ge(ld0, 32)
                return
            if cut == "ld2":
                for tt in range(NT):
                    sy.dma_start(out=otile[:].rearrange("p (k t) -> p k t", k=NK),
                                 in_=ot_tile(sy, tt)).then_inc(ld1, 16)
                    sy.dma_start(out=z2t[:],
                                 in_=z2s[tt * 128:(tt + 1) * 128, :]).then_inc(ld1, 16)
                sy.wait_ge(ld1, 32 * NT)
                return
            for tt in range(NT):
                if tt >= 1:
                    sy.wait_ge(s_e1, tt)
                sy.dma_start(out=otile[:].rearrange("p (k t) -> p k t", k=NK),
                             in_=ot_tile(sy, tt)).then_inc(ld1, 16)
                sy.dma_start(out=z2t[:],
                             in_=z2s[tt * 128:(tt + 1) * 128, :]).then_inc(ld1, 16)
                sy.wait_ge(s_e1, tt + 1)
                if cut in ("op", "mm"):
                    continue
                sy.dma_start(out=x3f[tt * 128:(tt + 1) * 128, :],
                             in_=accs[0][:]).then_inc(st1, 16)
                sy.wait_ge(s_te, NK * (tt + 1))
                sy.dma_start(out=x3Tf.rearrange("(k p) t -> p k t", p=128)[
                                 :, :, tt * 128:(tt + 1) * 128],
                             in_=x3ev[:].rearrange("p (k t) -> p k t", k=NK)
                             ).then_inc(stx, 16)
            if cut in ("op", "mm"):
                return
            sy.wait_ge(st1, 16 * NT)
            sy.wait_ge(stx, 16 * NT)
            if cut == "p1":
                return
            li = 0
            for idx, wn in enumerate(WL):
                if idx >= 1:
                    sy.wait_ge(s_gu, idx * NPC)
                sy.dma_start(out=x3w[:].rearrange("p (k t) -> p k t", k=NK),
                             in_=x3Tf.rearrange("(k p) t -> p k t", p=128)[
                                 :, :, wn * 512:(wn + 1) * 512]).then_inc(ldxw, 16)
                for ttl in range(4):
                    if li >= 1:
                        sy.wait_ge(s_xm, NT + li)
                    sy.dma_start(out=xrd[:],
                                 in_=x3f[wn * 512 + ttl * 128:wn * 512 + (ttl + 1) * 128, :]
                                 ).then_inc(ldx, 16)
                    sy.wait_ge(s_fin, li + 1)
                    sy.dma_start(out=outp[wn * 512 + ttl * 128:wn * 512 + (ttl + 1) * 128, :],
                                 in_=outt[:, (li % 2) * DM:(li % 2 + 1) * DM]
                                 ).then_inc(st2p[li % 2], 16)
                    li += 1
            sy.wait_ge(st2p[0], 16 * len(WL) * 2)
            sy.wait_ge(st2p[1], 16 * len(WL) * 2)

        @blk.gpsimd
        def _(g):
            g.dma_start(out=big[:].rearrange("p (k d) -> p k d", k=NK),
                        in_=wout.rearrange("(k p) d -> p k d", p=128)).then_inc(ldwt, 16)
            if cut in ("ld", "ld2"):
                g.wait_ge(ldwt, 16)
                return
            if cut in ("p1", "op", "mm"):
                return
            for idx, wn in enumerate(WL):
                for pc in range(NPC):
                    gi = idx * NPC + pc
                    if gi >= 2:
                        g.wait_ge(s_gu, gi - 1)
                    bb = (gi % 2) * NK * 128
                    g.dma_start(out=wg_sb[:, bb:bb + NK * 128].rearrange(
                                    "p (k j) -> p k j", k=NK),
                                in_=wg_tile(pc)).then_inc(ldgp[gi % 2], 16)
                    g.dma_start(out=wu_sb[:, bb:bb + NK * 128].rearrange(
                                    "p (k j) -> p k j", k=NK),
                                in_=wu_tile(pc)).then_inc(ldgp[gi % 2], 16)
                for dc in range(4):
                    for ic in range(NPC):
                        widx = (idx * 4 + dc) * NPC + ic
                        if widx >= 8:
                            g.wait_ge(s_dc, widx - 7)
                        wb = (widx % 8) * 512
                        g.dma_start(out=wd_sb[:, wb:wb + 512].rearrange(
                                        "p (r d) -> p r d", r=2),
                                    in_=wd_tile(ic, dc)).then_inc(lddp[widx % 8], 16)

        @blk.tensor
        def _(t):
            if cut in ("ld", "ld2"):
                return
            t.wait_ge(ld0, 32)
            t.wait_ge(ldwt, 16)
            n = 0
            for tt in range(NT):
                t.wait_ge(ld1, 32 * (tt + 1))
                for dc in range(4):
                    if tt + dc >= 1:
                        t.wait_ge(s_ev1, 4 * tt + dc)
                    for k in range(NK):
                        mm = t.matmul(p_op[:],
                                      otile[:, k * 128:(k + 1) * 128],
                                      big[:, k * DM + dc * 512:k * DM + dc * 512 + 512],
                                      start=(k == 0), stop=(k == NK - 1))
                    mm.then_inc(s_op, 1)
                if cut in ("op", "mm"):
                    continue
                t.wait_ge(s_e1, tt + 1)
                for k in range(NK):
                    if n >= 1:
                        t.wait_ge(s_te, n)
                    mm = t.transpose(p_tr[:], x3b[:, k * 128:(k + 1) * 128],
                                     idt_sb[:])
                    mm.then_inc(s_tr, 1)
                    n += 1
            if cut in ("p1", "op", "mm"):
                return
            for idx, wn in enumerate(WL):
                t.wait_ge(ldxw, 16 * (idx + 1))
                for pc in range(NPC):
                    gi = idx * NPC + pc
                    t.wait_ge(ldgp[gi % 2], 32 * (gi // 2 + 1))
                    if gi >= 1:
                        t.wait_ge(s_hb, gi)
                    bb = (gi % 2) * NK * 128
                    for k in range(NK):
                        t.matmul(p_g[:], wg_sb[:, bb + k * 128:bb + (k + 1) * 128],
                                 x3w[:, k * 512:(k + 1) * 512],
                                 start=(k == 0), stop=(k == NK - 1))
                    for k in range(NK):
                        mm = t.matmul(p_u[:], wu_sb[:, bb + k * 128:bb + (k + 1) * 128],
                                      x3w[:, k * 512:(k + 1) * 512],
                                      start=(k == 0), stop=(k == NK - 1))
                    mm.then_inc(s_gu, 1)
                for dc in range(4):
                    wdi = idx * 4 + dc
                    if wdi >= 1:
                        t.wait_ge(s_ev2, wdi)
                    for ic in range(NPC):
                        widx = wdi * NPC + ic
                        t.wait_ge(lddp[widx % 8], 16 * (widx // 8 + 1))
                        t.wait_ge(s_hb, idx * NPC + ic + 1)
                        wb = (widx % 8) * 512
                        for ttl in range(4):
                            mm = t.matmul(p_d[ttl][:],
                                          big[:, ic * 512 + ttl * 128:ic * 512 + (ttl + 1) * 128],
                                          wd_sb[:, wb:wb + 512],
                                          start=(ic == 0), stop=(ic == NPC - 1))
                        mm.then_inc(s_dc, 1)

        @blk.scalar
        def _(a):
            if cut in ("ld", "ld2", "op", "mm"):
                return
            n = 0
            for tt in range(NT):
                a.wait_ge(s_xm, tt + 1)
                if tt >= 1:
                    a.wait_ge(st1, 16 * tt)
                a.drain()
                a.activation(z2t[:], accs[0][:], AF.Square, accum_out=mcol[:, 3:4])
                a.drain()
                a.activation(mcol[:, 1:2], mcol[:, 3:4], AF.Sqrt, scale=INV,
                             bias=eps5[:]).then_inc(s_sd, 1)
                if tt >= 1:
                    a.wait_ge(stx, 16 * tt)
                for k in range(NK):
                    a.wait_ge(s_tr, n + 1)
                    a.drain()
                    a.activation(x3ev[:, k * 128:(k + 1) * 128],
                                 p_tr[:], AF.Identity).then_inc(s_te, 1)
                    n += 1
            if cut == "p1":
                return
            for idx, wn in enumerate(WL):
                for pc in range(NPC):
                    gi = idx * NPC + pc
                    a.wait_ge(s_gu, gi + 1)
                    if gi >= 1:
                        a.wait_ge(s_hb, gi)
                    a.drain()
                    a.activation(sg[:], p_g[:], AF.Silu).then_inc(s_sg, 1)
                for ttl in range(4):
                    li = idx * 4 + ttl
                    a.wait_ge(s_ev2, 4 * (idx + 1))
                    a.wait_ge(s_fin, li)
                    a.drain()
                    a.activation(z2t[:], accs[ttl][:], AF.Square,
                                 accum_out=mcol[:, 3:4])
                    a.drain()
                    a.activation(mcol[:, 1:2], mcol[:, 3:4], AF.Sqrt, scale=1.0 / DM,
                                 bias=eps6[:]).then_inc(s_rm, 1)
                    a.wait_ge(s_xm, NT + li + 1)
                    a.drain()
                    a.activation(z2t[:], accs[ttl][:], AF.Square,
                                 accum_out=mcol[:, 3:4])
                    a.drain()
                    a.activation(mcol[:, 1:2], mcol[:, 3:4], AF.Sqrt, scale=INV,
                                 bias=eps5[:]).then_inc(s_sd, 1)

        @blk.vector
        def _(v):
            if cut in ("ld", "ld2"):
                return
            v.memset(eps5[:], 1e-5)
            v.memset(eps6[:], 1e-6)
            for tt in range(NT):
                acc = accs[0]
                if tt >= 1 and cut not in ("op", "mm"):
                    v.wait_ge(st1, 16 * tt)
                for dc in range(4):
                    v.wait_ge(s_op, 4 * tt + dc + 1)
                    v.drain()
                    if cut == "mm":
                        v.engine_nop().then_inc(s_ev1, 1)
                        continue
                    v.tensor_add(acc[:, dc * 512:(dc + 1) * 512], p_op[:],
                                 xrd[:, dc * 512:(dc + 1) * 512]).then_inc(s_ev1, 1)
                v.wait_ge(ld1, 32 * (tt + 1))
                v.drain()
                if cut == "mm":
                    v.engine_nop().then_inc(s_e1, 1)
                    continue
                v.tensor_add(acc[:], acc[:], z2t[:])
                v.drain()
                if cut == "op":
                    v.engine_nop().then_inc(s_e1, 1)
                    continue
                v.tensor_reduce(mcol[:, 0:1], acc[:], AX.X, ALU.add)
                v.drain()
                v.tensor_scalar_mul(mcol[:, 0:1], mcol[:, 0:1], INV)
                v.drain()
                v.tensor_scalar(acc[:], acc[:], mcol[:, 0:1], None,
                                ALU.subtract).then_inc(s_xm, 1)
                v.wait_ge(s_sd, tt + 1)
                v.reciprocal(mcol[:, 2:3], mcol[:, 1:2])
                if tt >= 1:
                    v.wait_ge(s_tr, NK * tt)
                v.drain()
                v.tensor_scalar_mul(acc[:], acc[:], mcol[:, 2:3])
                v.drain()
                v.tensor_copy(x3b[:], acc[:]).then_inc(s_e1, 1)
            if cut in ("p1", "op", "mm"):
                return
            for idx, wn in enumerate(WL):
                for pc in range(NPC):
                    gi = idx * NPC + pc
                    v.wait_ge(s_sg, gi + 1)
                    if idx >= 1 and pc == 0:
                        v.wait_ge(s_dc, idx * 4 * NPC)
                    v.drain()
                    v.tensor_mul(big[:, pc * 512:(pc + 1) * 512], sg[:], p_u[:]
                                 ).then_inc(s_hb, 1)
                for dc in range(4):
                    wdi = idx * 4 + dc
                    v.wait_ge(s_dc, (wdi + 1) * NPC)
                    v.drain()
                    for ttl in range(4):
                        la = v.tensor_copy(accs[ttl][:, dc * 512:(dc + 1) * 512],
                                           p_d[ttl][:])
                    la.then_inc(s_ev2, 1)
                    v.drain()
                for ttl in range(4):
                    li = idx * 4 + ttl
                    acc = accs[ttl]
                    v.wait_ge(s_rm, li + 1)
                    v.reciprocal(mcol[:, 2:3], mcol[:, 1:2])
                    v.drain()
                    v.tensor_scalar_mul(acc[:], acc[:], mcol[:, 2:3])
                    v.wait_ge(ldx, 16 * (li + 1))
                    v.drain()
                    v.tensor_add(acc[:], acc[:], xrd[:])
                    v.drain()
                    v.tensor_reduce(mcol[:, 0:1], acc[:], AX.X, ALU.add)
                    v.drain()
                    v.tensor_scalar_mul(mcol[:, 0:1], mcol[:, 0:1], INV)
                    v.drain()
                    v.tensor_scalar(acc[:], acc[:], mcol[:, 0:1], None,
                                    ALU.subtract).then_inc(s_xm, 1)
                    v.wait_ge(s_sd, NT + li + 1)
                    v.reciprocal(mcol[:, 2:3], mcol[:, 1:2])
                    if li >= 2:
                        v.wait_ge(st2p[li % 2], 16 * (li // 2))
                    v.drain()
                    v.tensor_scalar_mul(outt[:, (li % 2) * DM:(li % 2 + 1) * DM],
                                        acc[:], mcol[:, 2:3]).then_inc(s_fin, 1)


def phase_Ap(nc, ncx, e, io):
    xTf = io["xTf"]; wxT = io["wxT"]; wdtTs = io["wdtTs"]; bdt2 = io["bdt2"]
    dtTs = io["dtTs"]; BTCT = io["BTCT"]

    NK = DM // 128
    NTC = TOK // 512

    if True:
        xb = e(ncx.sbuf_tensor("xb", [128, 2 * NK * 512], BF16))
        wx_sb = e(ncx.sbuf_tensor("wx_sb", [128, NK * 96], BF16))
        wdt_sb = e(ncx.sbuf_tensor("wdt_sb", [64, 256], BF16))
        bdt_sb = e(ncx.sbuf_tensor("bdt_sb", [128, 2], F32))
        bc_sb = e(ncx.sbuf_tensor("bc_sb", [32, TOK], F32))
        dtx_b = e(ncx.sbuf_tensor("dtx_b", [64, 2 * 512], BF16))
        t_z = e(ncx.sbuf_tensor("t_z", [128, 2 * 512], F32))
        t_a = e(ncx.sbuf_tensor("t_a", [128, 512], F32))
        t_e = e(ncx.sbuf_tensor("t_e", [128, 512], F32))
        t_l = e(ncx.sbuf_tensor("t_l", [128, 2 * 512], F32))
        t_r = e(ncx.sbuf_tensor("t_r", [128, 512], F32))
        dt_out = e(ncx.sbuf_tensor("dt_out", [128, 2 * 512], F32))
        one_c = e(ncx.sbuf_tensor("one_c", [128, 1], F32))
        p_xp = e(ncx.psum_tensor("p_xp", [96, 2 * 512], F32))
        p_dt = e(ncx.psum_tensor("p_dt", [128, 2 * 512], F32))
        ld = e(ncx.semaphore("ld"))
        lds = e(ncx.semaphore("lds"))
        ldxp = [e(ncx.semaphore("ldx0")), e(ncx.semaphore("ldx1"))]
        s_xp = e(ncx.semaphore("s_xp"))
        s_dt = e(ncx.semaphore("s_dt"))
        s_dtx = e(ncx.semaphore("s_dtx"))
        s_spZ = e(ncx.semaphore("s_spZ"))
        s_spL = e(ncx.semaphore("s_spL"))
        s_vr = e(ncx.semaphore("s_vr"))
        s_out = e(ncx.semaphore("s_out"))
        stp = [e(ncx.semaphore("st0")), e(ncx.semaphore("st1"))]
        stbc = e(ncx.semaphore("stbc"))
        blk = e(nc.Block())

        @blk.gpsimd
        def _(g):
            g.dma_start(out=wx_sb[:].rearrange("p (k j) -> p k j", k=NK),
                        in_=wxT.rearrange("(k p) j -> p k j", p=128)).then_inc(ld, 16)
            g.dma_start(out=wdt_sb[:], in_=wdtTs[:]).then_inc(ld, 16)
            for tc in range(NTC):
                if tc >= 2:
                    g.wait_ge(s_xp, tc - 1)
                b = (tc % 2) * NK * 512
                for k in range(NK):
                    g.dma_start(out=xb[:, b + k * 512:b + (k + 1) * 512],
                                in_=xTf[k * 128:(k + 1) * 128,
                                        tc * 512:(tc + 1) * 512]).then_inc(ldxp[tc % 2], 16)

        @blk.sync
        def _(sy):
            sy.dma_start(out=bdt_sb[:], in_=bdt2[:]).then_inc(lds, 16)
            gg = 0
            for tc in range(NTC):
                for gr in range(2):
                    sy.wait_ge(s_out, gg + 1)
                    sy.dma_start(out=dtTs[gr * 128:(gr + 1) * 128,
                                          tc * 512:(tc + 1) * 512],
                                 in_=dt_out[:, (gg % 2) * 512:(gg % 2 + 1) * 512]
                                 ).then_inc(stp[gg % 2], 16)
                    gg += 1
            sy.wait_ge(s_dtx, NTC)
            sy.dma_start(out=BTCT[:], in_=bc_sb[:]).then_inc(stbc, 16)
            sy.wait_ge(stp[0], 16 * NTC)
            sy.wait_ge(stp[1], 16 * NTC)
            sy.wait_ge(stbc, 16)

        @blk.tensor
        def _(t):
            t.wait_ge(ld, 32)
            gg = 0
            for tc in range(NTC):
                b = (tc % 2) * NK * 512
                pb = (tc % 2) * 512
                t.wait_ge(ldxp[tc % 2], 256 * (tc // 2 + 1))
                if tc >= 2:
                    t.wait_ge(s_dtx, tc - 1)
                for k in range(NK):
                    mm = t.matmul(p_xp[:, pb:pb + 512],
                                  wx_sb[:, k * 96:(k + 1) * 96],
                                  xb[:, b + k * 512:b + (k + 1) * 512],
                                  start=(k == 0), stop=(k == NK - 1))
                mm.then_inc(s_xp, 1)
                t.wait_ge(s_dtx, tc + 1)
                for gr in range(2):
                    if gg >= 2:
                        t.wait_ge(s_spZ, gg - 1)
                    t.matmul(p_dt[:, (gg % 2) * 512:(gg % 2 + 1) * 512],
                             wdt_sb[:, gr * 128:(gr + 1) * 128],
                             dtx_b[:, (tc % 2) * 512:(tc % 2 + 1) * 512],
                             start=True, stop=True).then_inc(s_dt, 1)
                    gg += 1

        @blk.scalar
        def _(a):
            a.wait_ge(ld, 32)
            a.wait_ge(lds, 16)
            gg = 0
            for tc in range(NTC):
                for gr in range(2):
                    zsl = slice((gg % 2) * 512, (gg % 2 + 1) * 512)
                    a.wait_ge(s_dt, gg + 1)
                    if gg >= 2:
                        a.wait_ge(s_vr, gg - 1)
                    a.activation(t_z[:, zsl], p_dt[:, zsl], AF.Identity,
                                 bias=bdt_sb[:, gr:gr + 1]).then_inc(s_spZ, 1)
                    a.drain()
                    a.activation(t_a[:], t_z[:, zsl], AF.Abs)
                    a.drain()
                    a.activation(t_e[:], t_a[:], AF.Exp, scale=-1.0)
                    a.drain()
                    a.activation(t_l[:, zsl], t_e[:], AF.Ln,
                                 bias=one_c[:]).then_inc(s_spL, 1)
                    gg += 1

        @blk.vector
        def _(v):
            v.memset(one_c[:], 1.0)
            gg = 0
            for tc in range(NTC):
                pb = (tc % 2) * 512
                v.wait_ge(s_xp, tc + 1)
                v.tensor_copy(bc_sb[:, tc * 512:(tc + 1) * 512], p_xp[64:96, pb:pb + 512])
                v.tensor_copy(dtx_b[:, pb:pb + 512],
                              p_xp[0:64, pb:pb + 512]).then_inc(s_dtx, 1)
                for gr in range(2):
                    zsl = slice((gg % 2) * 512, (gg % 2 + 1) * 512)
                    v.wait_ge(s_spZ, gg + 1)
                    v.tensor_scalar(t_r[:], t_z[:, zsl], 0.0, None,
                                    ALU.max).then_inc(s_vr, 1)
                    v.wait_ge(s_spL, gg + 1)
                    if gg >= 2:
                        v.wait_ge(stp[gg % 2], 16 * (gg // 2))
                    v.drain()
                    v.tensor_add(dt_out[:, zsl], t_r[:], t_l[:, zsl]).then_inc(s_out, 1)
                    gg += 1


def phase_Z(nc, ncx, e, io):
    src_f = io["src"]; dst_T = io["dst"]; idt = io["idt"]
    NT = TSL // 128
    NK = DM // 128
    if True:
        xt = e(ncx.sbuf_tensor("xt", [128, 2 * DM], BF16))
        ev = e(ncx.sbuf_tensor("ev", [128, 2 * NK * 128], BF16))
        idt_sb = e(ncx.sbuf_tensor("idt_sb", [128, 128], BF16))
        p_t = e(ncx.psum_tensor("p_t", [128, 128], BF16))
        ld = e(ncx.semaphore("ld"))
        ldi = e(ncx.semaphore("ldi"))
        s_tr = e(ncx.semaphore("s_tr"))
        s_te = e(ncx.semaphore("s_te"))
        st = e(ncx.semaphore("st"))
        blk = e(nc.Block())

        @blk.gpsimd
        def _(g):
            g.dma_start(out=idt_sb[:], in_=idt[:]).then_inc(ldi, 16)
            for tt in range(NT):
                if tt >= 2:
                    g.wait_ge(s_tr, NK * (tt - 1))
                g.dma_start(out=xt[:, (tt % 2) * DM:(tt % 2 + 1) * DM],
                            in_=src_f[tt * 128:(tt + 1) * 128, :]).then_inc(ld, 16)

        @blk.tensor
        def _(t):
            t.wait_ge(ldi, 16)
            n = 0
            for tt in range(NT):
                t.wait_ge(ld, 16 * (tt + 1))
                for k in range(NK):
                    if n >= 1:
                        t.wait_ge(s_te, n)
                    t.transpose(p_t[:],
                                xt[:, (tt % 2) * DM + k * 128:(tt % 2) * DM + (k + 1) * 128],
                                idt_sb[:]).then_inc(s_tr, 1)
                    n += 1

        @blk.scalar
        def _(a):
            n = 0
            for tt in range(NT):
                if tt >= 2:
                    a.wait_ge(st, 16 * (tt - 1))
                for k in range(NK):
                    a.wait_ge(s_tr, n + 1)
                    a.drain()
                    a.activation(ev[:, (tt % 2) * NK * 128 + k * 128:
                                    (tt % 2) * NK * 128 + (k + 1) * 128],
                                 p_t[:], AF.Identity).then_inc(s_te, 1)
                    n += 1

        @blk.sync
        def _(sy):
            for tt in range(NT):
                sy.wait_ge(s_te, NK * (tt + 1))
                sy.dma_start(out=dst_T.rearrange("(k p) t -> p k t", p=128)[
                                 :, :, tt * 128:(tt + 1) * 128],
                             in_=ev[:, (tt % 2) * NK * 128:(tt % 2 + 1) * NK * 128]
                             .rearrange("p (k t) -> p k t", k=NK)).then_inc(st, 16)
            sy.wait_ge(st, 16 * NT)


def build_mega(stop_after=None):
    def _stopper(nc, outp, xa):
        with ExitStack() as c2:
            ee = c2.enter_context
            sbq = ee(nc.sbuf_tensor("stop_sb", [128, DM], F32))
            ldq = ee(nc.semaphore("stop_ld"))
            stq = ee(nc.semaphore("stop_st"))
            bq = ee(nc.Block())
            @bq.sync
            def _(sy):
                sy.dma_start(out=sbq[:], in_=xa[0:128, :]).then_inc(ldq, 16)
                sy.wait_ge(ldq, 16)
                sy.dma_start(out=outp[0:128, :], in_=sbq[:]).then_inc(stq, 16)
                sy.wait_ge(stq, 16)
    nc = bass.Bass("TRN2", target_bir_lowering=False)
    F16 = mybir.dt.float16
    dram = nc.dram_tensor

    xTf = dram("xTf", [DM, TOK], F32, kind="ExternalInput").ap()
    wxT = dram("wxT", [DM, 96], F32, kind="ExternalInput").ap()
    wdtTs = dram("wdtTs", [DTR, 256], F32, kind="ExternalInput").ap()
    bdt2 = dram("bdt2", [128, 2], F32, kind="ExternalInput").ap()
    xTsh = dram("xTsh", [256, TOK], F32, kind="ExternalInput").ap()
    Ash = dram("Ash", [128, 32], F32, kind="ExternalInput").ap()
    Dsh = dram("Dsh", [128, 2], F32, kind="ExternalInput").ap()
    w2T = dram("w2T", [FF0, 256], F32, kind="ExternalInput").ap()
    h128 = dram("h128", [128, 128], BF16, kind="ExternalInput").ap()
    idt = dram("idt", [128, 128], BF16, kind="ExternalInput").ap()
    gpk = dram("gpk", [1408, 256], I32, kind="ExternalInput").ap()
    gsc = dram("gsc", [128, 11 * 16], F16, kind="ExternalInput").ap()
    dpk = dram("dpk", [256, 704], I32, kind="ExternalInput").ap()
    dsc = dram("dsc", [128, 2 * 44], F16, kind="ExternalInput").ap()
    xa = dram("xa", [TSL, DM], F32, kind="ExternalInput").ap()
    w1T = dram("w1T", [DM, FF0], F32, kind="ExternalInput").ap()
    w3T = dram("w3T", [DM, FF0], F32, kind="ExternalInput").ap()
    wq = dram("wq", [DM, 512], F32, kind="ExternalInput").ap()
    wk = dram("wk", [DM, 512], F32, kind="ExternalInput").ap()
    wv = dram("wv", [DM, 512], F32, kind="ExternalInput").ap()
    masks = dram("masks", [128, 5 * 512], BF16, kind="ExternalInput").ap()
    wout = dram("wout", [DM, DM], F32, kind="ExternalInput").ap()
    boutr = dram("boutr", [128, DM], F32, kind="ExternalInput").ap()
    offs = dram("offs", [1, 34], I32, kind="ExternalInput").ap()
    outp = dram("outp", [TSL, DM], F32, kind="ExternalOutput").ap()

    dtTs = dram("dtTs", [256, TOK], F32).ap()
    BTCT = dram("BTCT", [32, TOK], F32).ap()
    wp2in = dram("wp2in", [FF0, 256], BF16)
    wpgin = dram("wpgin", [DM, 1408], BF16)
    wpdin = dram("wpdin", [FI, 256], BF16)
    wp2g = dram("wp2g", [8 * FF0, 256], BF16)
    wpgg = dram("wpgg", [8 * DM, 1408], BF16)
    wpdg = dram("wpdg", [8 * FI, 256], BF16)
    ylocal = dram("ylocal", [TOK, 256], F32)
    ygath = dram("ygath", [8 * TOK, 256], F32)
    ylo = dram("ylo", [TSL, DM], F32).ap()
    OTl = dram("OTl", [DM, TSL], BF16).ap()
    z1f = dram("z1f", [TSL, DM], F32).ap()
    hidT = dram("hidT", [FF0, TSL], BF16).ap()
    z2loc = dram("z2loc", [TSL, DM], F32).ap()
    z2tl = dram("z2tl", [DM, TSL], BF16)
    z2g = dram("z2g", [4 * DM, TSL], BF16)
    Oloc = dram("Oloc", [512, SEQ], BF16)
    Og = dram("Og", [4 * 512, SEQ], BF16)
    x3f = dram("x3f", [TSL, DM], F32).ap()
    x3Tf = dram("x3Tf", [DM, TSL], BF16).ap()

    import os as _os
    RG8 = [list(range(8))]
    RG44 = [[0, 1, 2, 3], [4, 5, 6, 7]]

    with ExitStack() as octx:
        oe = octx.enter_context
        offs_sb = oe(nc.sbuf_tensor("offs_sb", [1, 34], I32))
        s_ccw = oe(nc.semaphore("s_ccw"))
        s_ccy = oe(nc.semaphore("s_ccy"))
        s_ccz = oe(nc.semaphore("s_ccz"))
        s_cco = oe(nc.semaphore("s_cco"))
        s_yl = oe(nc.semaphore("s_yl"))
        s_ol = oe(nc.semaphore("s_ol"))
        ld0m = oe(nc.semaphore("ld0m"))
        barsems = [oe(nc.semaphore("barS%d" % i)) for i in range(20)]
        swsems = [oe(nc.semaphore("sw%d" % i)) for i in range(40)]
        sw_alloc = [swsems, 0]
        clear_from = max(h.num for h in swsems) + 1
        dma_pool = [oe(nc.semaphore("pd%d" % i)) for i in range(12)]
        cmp_pool = [oe(nc.semaphore("pc%d" % i)) for i in range(20)]
        nbar = [0]
        SWN = dict(
            E_={"ld", "ldwA", "ldwB"},
            A_={"ld", "ldx0", "ldx1"},
            B_={"ldt0", "ldt1", "ldb0", "ldb1"},
            C_={"ldwA", "ldwB"} | {"ldw2%d" % qq for qq in range(8)},
            Z_={"ld", "ldi"},
            D_={"ldz", "ldw"},
            F_={"ldwt", "ldgA", "ldgB"} | {"ldd%d" % qq for qq in range(8)},
        )

        def PX(pfx):
            return NcProxy(nc, pfx, dma_pool=dma_pool, cmp_pool=cmp_pool,
                           sw_names=SWN[pfx], sw_alloc=sw_alloc)

        def bar():
            sb = barsems[nbar[0]]
            nbar[0] += 1
            _barrier(nc, NcProxy(nc, "M_"), oe, nbar[0], sb=sb,
                     clear_from=clear_from)

        with nc.Block() as b0:
            @b0.sync
            def _(sy):
                sy.dma_start(out=offs_sb[:], in_=offs[:]).then_inc(ld0m, 16)
                sy.wait_ge(ld0m, 16)

        _fonly = bool(int(_os.environ.get("MEGA_FONLY", "0")))
        if _fonly:
            _fcut0 = _os.environ.get("MEGA_FCUT") or None
            with nc.Block() as binit:
                @binit.gpsimd
                def _(g):
                    g.dma_start(out=OTl, in_=xa.rearrange("a b -> b a") if False
                                else OTl, ).then_inc(ld0m, 16) if False else None
                    g.dma_start(out=OTl, in_=xa).then_inc(ld0m, 16)
                    g.wait_ge(ld0m, 32)
            bar()
            with ExitStack() as ctx:
                phase_F(nc, PX("F_"), ctx.enter_context, dict(
                    z2s=xa, wout=wout, boutr=boutr, idt=idt, outp=outp,
                    x3f=x3f, x3Tf=x3Tf,
                    ot_tile=lambda eng, tt: OTl.rearrange("(k p) t -> p k t", p=128)[
                        :, :, tt * 128:(tt + 1) * 128],
                    wg_tile=None, wu_tile=None, wd_tile=None, cut=_fcut0))
            if _fcut0:
                bar()
                _stopper(nc, outp, xa)
            return nc
        with ExitStack() as ctx:
            phase_E(nc, PX("E_"), ctx.enter_context, dict(
                w2T=w2T, h128=h128, idt=idt, gpk=gpk, gsc=gsc, dpk=dpk, dsc=dsc,
                w2pT=wp2in.ap(), wguT=wpgin.ap(), wdnT=wpdin.ap()))
        bar()
        if stop_after == "E":
            _stopper(nc, outp, xa)
            return nc
        with nc.Block() as bw:
            @bw.gpsimd
            def _(g):
                g.collective_compute("AllGather", ALU.bypass, replica_groups=RG8,
                                     ins=[wp2in.ap().opt()], outs=[wp2g.ap().opt()]
                                     ).then_inc(s_ccw, 1)
                g.collective_compute("AllGather", ALU.bypass, replica_groups=RG8,
                                     ins=[wpgin.ap().opt()], outs=[wpgg.ap().opt()]
                                     ).then_inc(s_ccw, 1)
                g.collective_compute("AllGather", ALU.bypass, replica_groups=RG8,
                                     ins=[wpdin.ap().opt()], outs=[wpdg.ap().opt()]
                                     ).then_inc(s_ccw, 1)

        with ExitStack() as ctx:
            phase_Ap(nc, PX("A_"), ctx.enter_context, dict(
                xTf=xTf, wxT=wxT, wdtTs=wdtTs, bdt2=bdt2, dtTs=dtTs, BTCT=BTCT))
        bar()
        if stop_after == "Ap":
            _stopper(nc, outp, xa)
            return nc

        _noytr = bool(int(_os.environ.get("MEGA_NOYTR", "0")))
        ysc = dram("yscdbg", [256, TOK], F32).ap() if _noytr else None
        with ExitStack() as ctx:
            phase_B(nc, PX("B_"), ctx.enter_context, dict(
                dtTs=dtTs, xTs=xTsh, Ash=Ash, Dsh=Dsh,
                BT=BTCT[0:16, :], CT=BTCT[16:32, :], idt=idt,
                yTs=ysc, ytr=None if _noytr else ylocal.ap()))
        bar()
        if stop_after == "B":
            _stopper(nc, outp, xa)
            return nc
        yvr = ygath.ap().rearrange("(r t) d -> r t d", r=8)
        with nc.Block() as by:
            @by.gpsimd
            def _(g):
                g.collective_compute("AllGather", ALU.bypass, replica_groups=RG8,
                                     ins=[ylocal.ap().opt()], outs=[ygath.ap().opt()]
                                     ).then_inc(s_ccy, 1)
                g.wait_ge(s_ccy, 1)
                g.wait_ge(s_ccw, 3)
                ry = g.value_load(offs_sb[0:1, 0:1])
                H = TSL // 2
                for r8 in range(8):
                    for hh in range(2):
                        g.dma_start(
                            out=ylo[hh * H:(hh + 1) * H, r8 * 256:(r8 + 1) * 256],
                            in_=yvr[r8, ds(ry + hh * H, H), :]).then_inc(s_yl, 16)
                g.wait_ge(s_yl, 16 * 16)
        bar()
        if stop_after == "AGy":
            _stopper(nc, outp, xa)
            return nc

        def ya_tile(eng, tt):
            return ylo[tt * 128:(tt + 1) * 128, :].rearrange(
                "t (r d) -> t r d", r=8)

        def w2p_tile(hc, dc):
            return wp2g.ap().rearrange("(r h) d -> h r d", r=8)[
                hc * 128:(hc + 1) * 128, 2 * dc:2 * dc + 2, :]

        with ExitStack() as ctx:
            phase_C(nc, PX("C_"), ctx.enter_context, dict(
                xa=xa, w1T=w1T, w3T=w3T, idt=idt, z2o=z2loc, z1f=z1f, hidT=hidT,
                ya_tile=ya_tile, w2p_tile=w2p_tile, z2tl=None,
                offs_sb=offs_sb))
        bar()
        with ExitStack() as ctx:
            phase_Z(nc, PX("Z_"), ctx.enter_context, dict(
                src=z2loc, dst=z2tl.ap(), idt=idt))
        bar()
        if stop_after == "C":
            _stopper(nc, outp, xa)
            return nc
        with nc.Block() as bz:
            @bz.gpsimd
            def _(g):
                g.collective_compute("AllGather", ALU.bypass, replica_groups=RG44,
                                     ins=[z2tl.ap().opt()], outs=[z2g.ap().opt()]
                                     ).then_inc(s_ccz, 1)
                g.wait_ge(s_ccz, 1)
        bar()

        def z2w_tile(k, w4):
            return z2g.ap()[w4 * DM + k * 128:w4 * DM + (k + 1) * 128, :]

        with ExitStack() as ctx:
            phase_D(nc, PX("D_"), ctx.enter_context, dict(
                wq=wq, wk=wk, wv=wv, masks=masks, Oc=Oloc.ap(),
                z2w_tile=z2w_tile))
        bar()
        if stop_after == "D":
            _stopper(nc, outp, xa)
            return nc
        with nc.Block() as bo:
            @bo.gpsimd
            def _(g):
                g.collective_compute("AllGather", ALU.bypass, replica_groups=RG44,
                                     ins=[Oloc.ap().opt()], outs=[Og.ap().opt()]
                                     ).then_inc(s_cco, 1)
                g.wait_ge(s_cco, 1)
            @bo.sync
            def _(sy):
                sy.wait_ge(s_cco, 1)
                ro = sy.value_load(offs_sb[0:1, 1:2])
                for q4 in range(4):
                    sy.dma_start(
                        out=OTl[q4 * 512:(q4 + 1) * 512, :],
                        in_=Og.ap()[q4 * 512:(q4 + 1) * 512, ds(ro, TSL)]
                        ).then_inc(s_ol, 16)
                sy.wait_ge(s_ol, 16 * 4)
        bar()
        if stop_after == "AGO":
            _stopper(nc, outp, xa)
            return nc

        def ot_tile(eng, tt):
            return OTl.rearrange("(k p) t -> p k t", p=128)[
                :, :, tt * 128:(tt + 1) * 128]

        def wg_tile(pc):
            r, lc = pc // 11, pc % 11
            return wpgg.ap().rearrange("(r k p) o -> r p k o", r=8, p=128)[
                r, :, :, lc * 128:(lc + 1) * 128]

        def wu_tile(pc):
            cc = 44 + pc
            r, lc = cc // 11, cc % 11
            return wpgg.ap().rearrange("(r k p) o -> r p k o", r=8, p=128)[
                r, :, :, lc * 128:(lc + 1) * 128]

        def wd_tile(ic, dc):
            return wpdg.ap().rearrange("(r i) d -> i r d", r=8)[
                ic * 128:(ic + 1) * 128, 2 * dc:2 * dc + 2, :]

        _fcut = _os.environ.get("MEGA_FCUT") or None
        with ExitStack() as ctx:
            phase_F(nc, PX("F_"), ctx.enter_context, dict(
                z2s=z2loc, wout=wout, boutr=boutr, idt=idt, outp=outp,
                x3f=x3f, x3Tf=x3Tf, ot_tile=ot_tile, wg_tile=wg_tile,
                wu_tile=wu_tile, wd_tile=wd_tile, cut=_fcut))
        if _fcut:
            bar()
            _stopper(nc, outp, xa)
    return nc


_BUILT = {}


def _get(name, fn):
    if name not in _BUILT:
        _BUILT[name] = fn()
    return _BUILT[name]


def _hadamard128():
    H = np.array([[1.0]], dtype=np.float32)
    while H.shape[0] < 128:
        H = np.block([[H, H], [H, -H]])
    return H


def kernel_unfused(**ins):
    x = np.ascontiguousarray(ins["L0_attn_x"].reshape(TOK, DM))
    xT = np.ascontiguousarray(x.T)
    wxT = np.ascontiguousarray(ins["L0_attn_W_x"].T)
    wdtT = np.ascontiguousarray(ins["L0_attn_W_dt"].T)
    bdt = np.ascontiguousarray(ins["L0_attn_b_dt"].reshape(16, 128).T)
    idt = np.eye(128, dtype=BF)
    h128 = np.ascontiguousarray(_hadamard128().astype(BF))
    cores = list(range(NCORE))

    W2T = np.ascontiguousarray(ins["L0_ffn_W2"].T)
    gup = ins["L1_ffn_gate_up_packed"]
    gscT = np.ascontiguousarray(ins["L1_ffn_gate_up_scales"].T)
    dpk = ins["L1_ffn_down_packed"]
    dscT = np.ascontiguousarray(ins["L1_ffn_down_scales"].T)
    ncE = _get("E", build_E)
    in_maps = []
    for c in range(NCORE):
        dsl = slice(c * 256, (c + 1) * 256)
        gpk_c = np.ascontiguousarray(np.concatenate(
            [gup[c * 704:(c + 1) * 704],
             gup[FI + c * 704:FI + (c + 1) * 704]], axis=0))
        gsc_c = np.ascontiguousarray(np.concatenate(
            [gscT[c * 704:(c + 1) * 704],
             gscT[FI + c * 704:FI + (c + 1) * 704]], axis=0)
            .reshape(11, 128, 16).transpose(1, 0, 2).reshape(128, 176))
        dpk_c = np.ascontiguousarray(dpk[c * 256:(c + 1) * 256])
        dsc_c = np.ascontiguousarray(dscT[c * 256:(c + 1) * 256]
                                     .reshape(2, 128, 44).transpose(1, 0, 2)
                                     .reshape(128, 88))
        in_maps.append(dict(w2T=np.ascontiguousarray(W2T[:, dsl]), h128=h128,
                            idt=idt, gpk=gpk_c, gsc=gsc_c, dpk=dpk_c, dsc=dsc_c))
    resE = run_bass_kernel_spmd(ncE, in_maps, core_ids=cores)
    w2pT = np.concatenate([r["w2pT"] for r in resE.results], axis=1)
    wguT = np.zeros((DM, 2 * FI), dtype=BF)
    wdnT = np.concatenate([r["wdnT"] for r in resE.results], axis=1)
    for c in range(NCORE):
        blkc = resE.results[c]["wguT"]
        wguT[:, c * 704:(c + 1) * 704] = blkc[:, 0:704]
        wguT[:, FI + c * 704:FI + (c + 1) * 704] = blkc[:, 704:1408]

    ncA = _get("A", build_A)
    in_maps = []
    for c in range(NCORE):
        in_maps.append(dict(
            xT=np.ascontiguousarray(xT[:, c * TSL:(c + 1) * TSL]),
            wxT=wxT, wdtT=wdtT, bdt=bdt))
    resA = run_bass_kernel_spmd(ncA, in_maps, core_ids=cores)
    dtT = np.concatenate([r["dtT"] for r in resA.results], axis=1)
    bcs = np.concatenate([r["bc"] for r in resA.results], axis=1)
    BT = np.ascontiguousarray(bcs[0:16])
    CT = np.ascontiguousarray(bcs[16:32])

    A_log = ins["L0_attn_A_log"]
    Dv = ins["L0_attn_D"]
    ncB = _get("B", build_B)
    in_maps = []
    for c in range(NCORE):
        rs = slice(c * 256, (c + 1) * 256)
        in_maps.append(dict(
            dtTs=np.ascontiguousarray(dtT[rs]),
            xTs=np.ascontiguousarray(xT[rs]),
            Ash=np.ascontiguousarray(
                A_log[rs].reshape(2, 128, NST).transpose(1, 0, 2).reshape(128, 32)),
            Dsh=np.ascontiguousarray(Dv[rs].reshape(2, 128).T),
            BT=BT, CT=CT, idt=idt))
    resB = run_bass_kernel_spmd(ncB, in_maps, core_ids=cores)
    yT = np.concatenate([r["yTs"] for r in resB.results], axis=0)
    y = np.ascontiguousarray(yT.T)

    w1T_h = np.ascontiguousarray(ins["L0_ffn_W1"].T)
    w3T_h = np.ascontiguousarray(ins["L0_ffn_W3"].T)
    ncC = _get("C", build_C)
    in_maps = []
    for c in range(NCORE):
        ts = slice(c * TSL, (c + 1) * TSL)
        in_maps.append(dict(xa=np.ascontiguousarray(x[ts]),
                            ya=np.ascontiguousarray(y[ts]),
                            w1T=w1T_h, w3T=w3T_h, w2pT=w2pT, idt=idt))
    resC = run_bass_kernel_spmd(ncC, in_maps, core_ids=cores)
    z2 = np.concatenate([r["z2o"] for r in resC.results], axis=0)

    wqkv = ins["L1_attn_w_qkv"]
    masks_h = _attn_masks()
    ncD = _get("D", build_D)
    in_maps = []
    for c in range(NCORE):
        beta = c // 4
        h0 = (c % 4) * 4
        z2T_b = np.ascontiguousarray(z2[beta * SEQ:(beta + 1) * SEQ].T)
        in_maps.append(dict(
            z2T=z2T_b,
            wq=np.ascontiguousarray(wqkv[:, h0 * DH:(h0 + 4) * DH]),
            wk=np.ascontiguousarray(wqkv[:, DM + h0 * DH:DM + (h0 + 4) * DH]),
            wv=np.ascontiguousarray(wqkv[:, 2 * DM + h0 * DH:2 * DM + (h0 + 4) * DH]),
            masks=masks_h))
    resD = run_bass_kernel_spmd(ncD, in_maps, core_ids=cores)
    OT = [np.concatenate([resD.results[b * 4 + j]["Oc"] for j in range(4)], axis=0)
          for b in range(2)]
    OTfull = np.concatenate(OT, axis=1)

    wout_h = np.ascontiguousarray(ins["L1_attn_w_out"])
    boutr_h = np.ascontiguousarray(np.tile(ins["L1_attn_b_out"][None, :], (128, 1)))
    ncF = _get("F", build_F)
    in_maps = []
    for c in range(NCORE):
        ts = slice(c * TSL, (c + 1) * TSL)
        in_maps.append(dict(
            OTs=np.ascontiguousarray(OTfull[:, ts]),
            z2s=np.ascontiguousarray(z2[ts]),
            wout=wout_h, boutr=boutr_h, wguT=wguT, wdnT=wdnT, idt=idt))
    resF = run_bass_kernel_spmd(ncF, in_maps, core_ids=cores)
    out = np.concatenate([r["outp"] for r in resF.results], axis=0)
    return out.reshape(2, SEQ, DM)


def _attn_masks():
    m = np.zeros((128, 5 * 512), dtype=BF)
    kk = np.arange(128)[:, None]
    qq = np.arange(512)[None, :]
    for r in range(4):
        m[:, r * 512:(r + 1) * 512] = (kk + r * 128 <= qq).astype(BF)
    corner = ((kk <= qq) | (qq < 128)).astype(BF)
    m[:, 4 * 512:5 * 512] = corner
    return m


def kernel(**ins):
    x = np.ascontiguousarray(ins["L0_attn_x"].reshape(TOK, DM))
    xT = np.ascontiguousarray(x.T)
    wxT = np.ascontiguousarray(ins["L0_attn_W_x"].T)
    wdtT = np.ascontiguousarray(ins["L0_attn_W_dt"].T)
    bdt_full = ins["L0_attn_b_dt"]
    A_log = ins["L0_attn_A_log"]
    Dv = ins["L0_attn_D"]
    W2T = np.ascontiguousarray(ins["L0_ffn_W2"].T)
    gup = ins["L1_ffn_gate_up_packed"]
    gscT = np.ascontiguousarray(ins["L1_ffn_gate_up_scales"].T)
    dpk = ins["L1_ffn_down_packed"]
    dscT = np.ascontiguousarray(ins["L1_ffn_down_scales"].T)
    w1T_h = np.ascontiguousarray(ins["L0_ffn_W1"].T)
    w3T_h = np.ascontiguousarray(ins["L0_ffn_W3"].T)
    wqkv = ins["L1_attn_w_qkv"]
    wout_h = np.ascontiguousarray(ins["L1_attn_w_out"])
    boutr_h = np.ascontiguousarray(np.tile(ins["L1_attn_b_out"][None, :], (128, 1)))
    idt = np.eye(128, dtype=BF)
    h128 = np.ascontiguousarray(_hadamard128().astype(BF))
    masks_h = _attn_masks()

    ncM = _get("M", build_mega)
    in_maps = []
    for c in range(NCORE):
        rs = slice(c * 256, (c + 1) * 256)
        ts = slice(c * TSL, (c + 1) * TSL)
        h0 = (c % 4) * 4
        gpk_c = np.ascontiguousarray(gup[c * 1408:(c + 1) * 1408])
        gsc_c = np.ascontiguousarray(
            gscT[c * 1408:(c + 1) * 1408]
            .reshape(11, 128, 16).transpose(1, 0, 2).reshape(128, 176))
        dpk_c = np.ascontiguousarray(dpk[c * 256:(c + 1) * 256])
        dsc_c = np.ascontiguousarray(dscT[c * 256:(c + 1) * 256]
                                     .reshape(2, 128, 44).transpose(1, 0, 2)
                                     .reshape(128, 88))
        in_maps.append(dict(
            xTf=xT,
            wxT=wxT,
            wdtTs=np.ascontiguousarray(wdtT[:, rs]),
            bdt2=np.ascontiguousarray(bdt_full[rs].reshape(2, 128).T),
            xTsh=np.ascontiguousarray(xT[rs]),
            Ash=np.ascontiguousarray(
                A_log[rs].reshape(2, 128, NST).transpose(1, 0, 2).reshape(128, 32)),
            Dsh=np.ascontiguousarray(Dv[rs].reshape(2, 128).T),
            w2T=np.ascontiguousarray(W2T[:, rs]),
            h128=h128, idt=idt,
            gpk=gpk_c, gsc=gsc_c, dpk=dpk_c, dsc=dsc_c,
            xa=np.ascontiguousarray(x[ts]),
            w1T=w1T_h, w3T=w3T_h,
            wq=np.ascontiguousarray(wqkv[:, h0 * DH:(h0 + 4) * DH]),
            wk=np.ascontiguousarray(wqkv[:, DM + h0 * DH:DM + (h0 + 4) * DH]),
            wv=np.ascontiguousarray(wqkv[:, 2 * DM + h0 * DH:2 * DM + (h0 + 4) * DH]),
            masks=masks_h,
            wout=wout_h, boutr=boutr_h,
            offs=np.array([[c * TSL, (c % 4) * TSL]
                           + [(c % 4) * TSL + tt * 128 for tt in range(16)]
                           + [c * TSL + tt * 128 for tt in range(16)]],
                          dtype=np.int32)))
    res = run_bass_kernel_spmd(ncM, in_maps, core_ids=list(range(NCORE)))
    out = np.concatenate([r["outp"] for r in res.results], axis=0)
    return out.reshape(2, SEQ, DM)


# version 06
# speedup vs baseline: 1.0302x; 1.0302x over previous
import numpy as np
import ml_dtypes
from contextlib import ExitStack

import concourse.bass as bass
import concourse.mybir as mybir
from concourse.bass_utils import run_bass_kernel_spmd
from concourse.bass import ds

F32 = mybir.dt.float32
BF16 = mybir.dt.bfloat16
I32 = mybir.dt.int32
AF = mybir.ActivationFunctionType
ALU = mybir.AluOpType
AX = mybir.AxisListType
BF = ml_dtypes.bfloat16

NCORE = 8
DM = 2048
TOK = 16384
TSL = TOK // NCORE
NST = 16
DTR = 64
SEQ = 8192
NH = 16
DH = 128
FF0 = 8192
FI = 5632



class NcProxy:
    def __init__(self, nc, pfx, dma_pool=None, cmp_pool=None,
                 sw_names=None, sw_alloc=None):
        self.nc = nc
        self.pfx = pfx
        self.dma_pool = dma_pool
        self.cmp_pool = cmp_pool
        self.sw_names = sw_names or set()
        self.sw_alloc = sw_alloc
        self.ndma = 0
        self.ncmp = 0

    def sbuf_tensor(self, name, shape, dtype):
        return self.nc.sbuf_tensor(self.pfx + name, shape, dtype)

    def psum_tensor(self, name, shape, dtype):
        return self.nc.psum_tensor(self.pfx + name, shape, dtype)

    def semaphore(self, name):
        if self.dma_pool is None:
            return self.nc.semaphore(self.pfx + name)
        import contextlib

        @contextlib.contextmanager
        def _fake(handle):
            yield handle
        if name in self.sw_names:
            h = self.sw_alloc[0][self.sw_alloc[1]]
            self.sw_alloc[1] += 1
        elif name.startswith("s_"):
            h = self.cmp_pool[self.ncmp]
            self.ncmp += 1
        else:
            h = self.dma_pool[self.ndma]
            self.ndma += 1
        return _fake(h)


def _barrier(nc, ncx, e, k, sb=None, clear_from=None):
    if sb is None:
        sb = e(ncx.semaphore(f"bar{k}"))
    with nc.Block() as blk:
        for engname in ("tensor", "scalar", "vector", "gpsimd"):
            def mk(engname=engname):
                def _(eng):
                    eng.drain().then_inc(sb, 1)
                    eng.wait_ge(sb, 6 if clear_from is not None else 5)
                return _
            getattr(blk, engname)(mk())

        @blk.sync
        def _(sy):
            sy.drain().then_inc(sb, 1)
            sy.wait_ge(sb, 5)
            if clear_from is not None:
                sy.sem_clear(range(clear_from, 256))
                sy.nop().then_inc(sb, 1)
                sy.wait_ge(sb, 6)
    return sb


def build_A():
    nc = bass.Bass("TRN2", target_bir_lowering=False)
    xT = nc.dram_tensor("xT", [DM, TSL], F32, kind="ExternalInput").ap()
    wxT = nc.dram_tensor("wxT", [DM, 96], F32, kind="ExternalInput").ap()
    wdtT = nc.dram_tensor("wdtT", [DTR, DM], F32, kind="ExternalInput").ap()
    bdt = nc.dram_tensor("bdt", [128, 16], F32, kind="ExternalInput").ap()
    dtT = nc.dram_tensor("dtT", [DM, TSL], F32, kind="ExternalOutput").ap()
    bc = nc.dram_tensor("bc", [32, TSL], F32, kind="ExternalOutput").ap()

    NK = DM // 128
    NTC = TSL // 512

    with ExitStack() as ctx:
        e = ctx.enter_context
        xb = e(nc.sbuf_tensor("xb", [128, NK * TSL], BF16))
        wx_sb = e(nc.sbuf_tensor("wx_sb", [128, NK * 96], BF16))
        wdt_sb = e(nc.sbuf_tensor("wdt_sb", [DTR, DM], BF16))
        bdt_sb = e(nc.sbuf_tensor("bdt_sb", [128, 16], F32))
        xp_sb = e(nc.sbuf_tensor("xp_sb", [96, TSL], F32))
        dtx_b = e(nc.sbuf_tensor("dtx_b", [DTR, TSL], BF16))
        t_z = e(nc.sbuf_tensor("t_z", [128, 2 * 512], F32))
        t_a = e(nc.sbuf_tensor("t_a", [128, 512], F32))
        t_e = e(nc.sbuf_tensor("t_e", [128, 512], F32))
        t_l = e(nc.sbuf_tensor("t_l", [128, 2 * 512], F32))
        t_r = e(nc.sbuf_tensor("t_r", [128, 512], F32))
        dt_out = e(nc.sbuf_tensor("dt_out", [128, 2 * 512], F32))
        one_c = e(nc.sbuf_tensor("one_c", [128, 1], F32))
        p_xp = e(nc.psum_tensor("p_xp", [96, 512], F32))
        p_dt = e(nc.psum_tensor("p_dt", [128, 2 * 512], F32))
        ld = e(nc.semaphore("ld"))
        lds = e(nc.semaphore("lds"))
        s_xp = e(nc.semaphore("s_xp"))
        s_dt = e(nc.semaphore("s_dt"))
        s_dtx = e(nc.semaphore("s_dtx"))
        s_spZ = e(nc.semaphore("s_spZ"))
        s_spL = e(nc.semaphore("s_spL"))
        s_vr = e(nc.semaphore("s_vr"))
        s_out = e(nc.semaphore("s_out"))
        stp = [e(nc.semaphore("st0")), e(nc.semaphore("st1"))]
        blk = e(nc.Block())

        NLD = 16 + 16 + 1 + 1 + 1

        @blk.gpsimd
        def _(g):
            for k in range(NK):
                g.dma_start(out=xb[:, k * TSL:(k + 1) * TSL],
                            in_=xT[k * 128:(k + 1) * 128, :]).then_inc(ld, 16)
            for k in range(NK):
                g.dma_start(out=wx_sb[:, k * 96:(k + 1) * 96],
                            in_=wxT[k * 128:(k + 1) * 128, :]).then_inc(ld, 16)
            g.dma_start(out=wdt_sb[:], in_=wdtT[:]).then_inc(ld, 16)

        @blk.sync
        def _(sy):
            sy.dma_start(out=bdt_sb[:], in_=bdt[:]).then_inc(lds, 16)

        LD_ALL = 16 * (NK + NK + 1)

        @blk.tensor
        def _(t):
            t.wait_ge(ld, LD_ALL)
            g = 0
            for tc in range(NTC):
                cs = slice(tc * 512, (tc + 1) * 512)
                if tc > 0:
                    t.wait_ge(s_dtx, tc)
                for k in range(NK):
                    mm = t.matmul(p_xp[:, :], wx_sb[:, k * 96:(k + 1) * 96],
                                  xb[:, k * TSL + tc * 512: k * TSL + tc * 512 + 512],
                                  start=(k == 0), stop=(k == NK - 1))
                mm.then_inc(s_xp, 1)
                t.wait_ge(s_dtx, tc + 1)
                for k in range(NK):
                    if g >= 2:
                        t.wait_ge(s_spZ, g - 1)
                    b = g % 2
                    t.matmul(p_dt[:, b * 512:(b + 1) * 512],
                             wdt_sb[:, k * 128:(k + 1) * 128], dtx_b[:, cs],
                             start=True, stop=True).then_inc(s_dt, 1)
                    g += 1

        @blk.scalar
        def _(a):
            a.wait_ge(lds, 16)
            g = 0
            for tc in range(NTC):
                for k in range(NK):
                    b = g % 2
                    zsl = slice(b * 512, (b + 1) * 512)
                    a.wait_ge(s_dt, g + 1)
                    if g >= 2:
                        a.wait_ge(s_vr, g - 1)
                    a.activation(t_z[:, zsl], p_dt[:, zsl], AF.Identity,
                                 bias=bdt_sb[:, k:k + 1]).then_inc(s_spZ, 1)
                    a.drain()
                    a.activation(t_a[:], t_z[:, zsl], AF.Abs)
                    a.drain()
                    a.activation(t_e[:], t_a[:], AF.Exp, scale=-1.0)
                    a.drain()
                    a.activation(t_l[:, zsl], t_e[:], AF.Ln,
                                 bias=one_c[:]).then_inc(s_spL, 1)
                    g += 1

        @blk.vector
        def _(v):
            v.memset(one_c[:], 1.0)
            g = 0
            for tc in range(NTC):
                cs = slice(tc * 512, (tc + 1) * 512)
                v.wait_ge(s_xp, tc + 1)
                v.tensor_copy(xp_sb[:, cs], p_xp[:, :])
                v.tensor_copy(dtx_b[:, cs], p_xp[0:64, :]).then_inc(s_dtx, 1)
                for k in range(NK):
                    b = g % 2
                    zsl = slice(b * 512, (b + 1) * 512)
                    v.wait_ge(s_spZ, g + 1)
                    v.tensor_scalar(t_r[:], t_z[:, zsl], 0.0, None,
                                    ALU.max).then_inc(s_vr, 1)
                    v.wait_ge(s_spL, g + 1)
                    if g >= 2:
                        v.wait_ge(stp[g % 2], 16 * (g // 2))
                    v.drain()
                    v.tensor_add(dt_out[:, zsl], t_r[:], t_l[:, zsl]).then_inc(s_out, 1)
                    g += 1

        @blk.sync
        def _(sy):
            g = 0
            for tc in range(NTC):
                for k in range(NK):
                    b = g % 2
                    sy.wait_ge(s_out, g + 1)
                    sy.dma_start(out=dtT[k * 128:(k + 1) * 128,
                                         tc * 512:(tc + 1) * 512],
                                 in_=dt_out[:, b * 512:(b + 1) * 512]).then_inc(stp[g % 2], 16)
                    g += 1
            sy.wait_ge(s_dtx, NTC)
            sy.dma_start(out=bc[:], in_=xp_sb[64:96, :]).then_inc(stp[0], 16)
            sy.wait_ge(stp[0], 16 * (NTC * NK // 2 + 1))
            sy.wait_ge(stp[1], 16 * (NTC * NK // 2))
    return nc


def build_E():
    nc = bass.Bass("TRN2", target_bir_lowering=False)
    F16 = mybir.dt.float16
    io = dict(
        w2T=nc.dram_tensor("w2T", [FF0, 256], F32, kind="ExternalInput").ap(),
        h128=nc.dram_tensor("h128", [128, 128], BF16, kind="ExternalInput").ap(),
        idt=nc.dram_tensor("idt", [128, 128], BF16, kind="ExternalInput").ap(),
        gpk=nc.dram_tensor("gpk", [1408, 256], I32, kind="ExternalInput").ap(),
        gsc=nc.dram_tensor("gsc", [128, 11 * 16], F16, kind="ExternalInput").ap(),
        dpk=nc.dram_tensor("dpk", [256, 704], I32, kind="ExternalInput").ap(),
        dsc=nc.dram_tensor("dsc", [128, 2 * 44], F16, kind="ExternalInput").ap(),
        w2pT=nc.dram_tensor("w2pT", [FF0, 256], BF16, kind="ExternalOutput").ap(),
        wguT=nc.dram_tensor("wguT", [DM, 1408], BF16, kind="ExternalOutput").ap(),
        wdnT=nc.dram_tensor("wdnT", [FI, 256], BF16, kind="ExternalOutput").ap())
    with ExitStack() as ctx:
        phase_E(nc, NcProxy(nc, "E_"), ctx.enter_context, io)
    return nc


def phase_E(nc, ncx, e, io):
    w2T = io["w2T"]; h128 = io["h128"]; idt = io["idt"]
    gpk = io["gpk"]; gsc = io["gsc"]; dpk = io["dpk"]; dsc = io["dsc"]
    w2pT = io["w2pT"]; wguT = io["wguT"]; wdnT = io["wdnT"]

    SC = float(2.0 ** -6.5)
    NTR = 11 * 16 + 2 * 44

    if True:
        w2s = e(ncx.sbuf_tensor("w2s", [128, 2 * 256], BF16))
        t1 = e(ncx.sbuf_tensor("t1", [128, 64 * 256], F32))
        tmph = e(ncx.sbuf_tensor("tmph", [128, 32 * 256], F32))
        w2o = e(ncx.sbuf_tensor("w2o", [128, 64 * 256], BF16))
        h_sb = e(ncx.sbuf_tensor("h_sb", [128, 128], BF16))
        idt_sb = e(ncx.sbuf_tensor("idt_sb", [128, 128], BF16))
        pk_sb = e(ncx.sbuf_tensor("pk_sb", [128, 11 * 256], I32))
        gsc_sb = e(ncx.sbuf_tensor("gsc_sb", [128, 11 * 16], F32))
        dpk_sb = e(ncx.sbuf_tensor("dpk_sb", [128, 2 * 704], I32))
        dsc_sb = e(ncx.sbuf_tensor("dsc_sb", [128, 2 * 44], F32))
        m_i = e(ncx.sbuf_tensor("m_i", [128, 704], I32))
        mf = e(ncx.sbuf_tensor("mf", [128, 704], F32))
        r1 = e(ncx.sbuf_tensor("r1", [128, 704], F32))
        r2 = e(ncx.sbuf_tensor("r2", [128, 704], F32))
        uu = e(ncx.sbuf_tensor("uu", [128, 704], F32))
        vv = e(ncx.sbuf_tensor("vv", [128, 704], F32))
        s31 = e(ncx.sbuf_tensor("s31", [128, 704], I32))
        wde = e(ncx.sbuf_tensor("wde", [128, 5632], F32))
        wdb = e(ncx.sbuf_tensor("wdb", [128, 5632], BF16))
        wtr = e(ncx.sbuf_tensor("wtr", [128, 128], BF16))
        p_f = e(ncx.psum_tensor("p_f", [128, 256], F32))
        p_tr = e(ncx.psum_tensor("p_tr", [128, 128], BF16))
        ld = e(ncx.semaphore("ld"))
        lds2 = e(ncx.semaphore("lds2"))
        ldwp = [e(ncx.semaphore("ldwA")), e(ncx.semaphore("ldwB"))]
        s_t = e(ncx.semaphore("s_t"))
        s_v = e(ncx.semaphore("s_v"))
        s_a = e(ncx.semaphore("s_a"))
        st = e(ncx.semaphore("st"))
        blk = e(nc.Block())

        @blk.gpsimd
        def _(g):
            g.dma_start(out=gsc_sb[:], in_=gsc[:]).then_inc(ld, 16)
            g.dma_start(out=dsc_sb[:], in_=dsc[:]).then_inc(ld, 16)
            for c in range(64):
                if c >= 2:
                    g.wait_ge(s_t, c - 1)
                b = c % 2
                g.dma_start(out=w2s[:, b * 256:(b + 1) * 256],
                            in_=w2T[c * 128:(c + 1) * 128, :]).then_inc(ldwp[c % 2], 16)

        @blk.sync
        def _(sy):
            sy.dma_start(out=h_sb[:], in_=h128[:]).then_inc(lds2, 16)
            sy.dma_start(out=idt_sb[:], in_=idt[:]).then_inc(lds2, 16)
            sy.dma_start(out=pk_sb[:].rearrange("p (c j) -> p c j", c=11),
                         in_=gpk.rearrange("(c p) j -> p c j", p=128)).then_inc(lds2, 16)
            sy.dma_start(out=dpk_sb[:].rearrange("p (c j) -> p c j", c=2),
                         in_=dpk.rearrange("(c p) j -> p c j", p=128)).then_inc(lds2, 16)

        @blk.tensor
        def _(t):
            t.wait_ge(ld, 16 * 2)
            t.wait_ge(lds2, 16 * 4)
            for c in range(64):
                t.wait_ge(ldwp[c % 2], 16 * (c // 2 + 1))
                if c >= 1:
                    t.wait_ge(s_a, c)
                b = c % 2
                mm = t.matmul(p_f[:], h_sb[:], w2s[:, b * 256:(b + 1) * 256],
                              start=True, stop=True)
                mm.then_inc(s_t, 1)
            ntr = 0
            for c in range(11):
                t.wait_ge(s_v, 2 + c)
                for k in range(16):
                    if ntr >= 1:
                        t.wait_ge(s_a, 64 + ntr)
                    mm = t.transpose(p_tr[:], wdb[:, k * 128:(k + 1) * 128], idt_sb[:])
                    mm.then_inc(s_t, 1)
                    ntr += 1
            for c in range(2):
                t.wait_ge(s_v, 13 + c)
                for k in range(44):
                    t.wait_ge(s_a, 64 + ntr)
                    mm = t.transpose(p_tr[:], wdb[:, k * 128:(k + 1) * 128], idt_sb[:])
                    mm.then_inc(s_t, 1)
                    ntr += 1

        @blk.scalar
        def _(a):
            for c in range(64):
                a.wait_ge(s_t, c + 1)
                a.drain()
                a.activation(t1[:, c * 256:(c + 1) * 256], p_f[:],
                             AF.Identity).then_inc(s_a, 1)
            for ntr in range(NTR):
                a.wait_ge(s_t, 64 + ntr + 1)
                if ntr >= 1:
                    a.wait_ge(st, 16 * (ntr + 1))
                a.drain()
                a.activation(wtr[:], p_tr[:], AF.Identity).then_inc(s_a, 1)

        @blk.vector
        def _(v):
            v.wait_ge(s_a, 64)
            for s in range(6):
                bit = 1 << s
                lo = [c for c in range(64) if not (c & bit)]
                v.drain()
                for j, c in enumerate(lo):
                    v.tensor_copy(tmph[:, j * 256:(j + 1) * 256],
                                  t1[:, c * 256:(c + 1) * 256])
                v.drain()
                for j, c in enumerate(lo):
                    cb = c | bit
                    v.tensor_add(t1[:, c * 256:(c + 1) * 256],
                                 t1[:, c * 256:(c + 1) * 256],
                                 t1[:, cb * 256:(cb + 1) * 256])
                v.drain()
                for j, c in enumerate(lo):
                    cb = c | bit
                    v.tensor_sub(t1[:, cb * 256:(cb + 1) * 256],
                                 tmph[:, j * 256:(j + 1) * 256],
                                 t1[:, cb * 256:(cb + 1) * 256])
            v.drain()
            v.tensor_scalar_mul(w2o[:], t1[:], SC).then_inc(s_v, 1)

            def dequant(pk_ap, n_in, sc_ap, n_grp, base_ms):
                wv = wde[:, 0:n_in].rearrange("p (j e) -> p e j", e=8)
                n8 = n_in // 8
                for p in range(8):
                    v.drain()
                    v.tensor_scalar(m_i[:, 0:n8], pk_ap, 4 * p, 7,
                                    ALU.logical_shift_right, ALU.bitwise_and)
                    v.drain()
                    v.tensor_single_scalar(mf[:, 0:n8], m_i[:, 0:n8], 0.0, ALU.add)
                    v.drain()
                    v.tensor_scalar(r1[:, 0:n8], mf[:, 0:n8], -4.0, 0.0,
                                    ALU.add, ALU.max)
                    v.tensor_scalar(s31[:, 0:n8], pk_ap, 28 - 4 * p,
                                    -2147483648, ALU.logical_shift_left,
                                    ALU.bitwise_and)
                    v.tensor_scalar(r2[:, 0:n8], mf[:, 0:n8], -6.0, 0.0,
                                    ALU.add, ALU.max)
                    v.drain()
                    v.tensor_add(uu[:, 0:n8], mf[:, 0:n8], r1[:, 0:n8])
                    v.drain()
                    v.scalar_tensor_tensor(vv[:, 0:n8], uu[:, 0:n8], 0.5,
                                           r2[:, 0:n8], ALU.mult, ALU.add)
                    v.drain()
                    v.tensor_tensor(wv[:, p, :].bitcast(I32),
                                    vv[:, 0:n8].bitcast(I32),
                                    s31[:, 0:n8], ALU.bitwise_xor)
                v.drain()
                for grp in range(n_grp):
                    v.tensor_scalar_mul(wdb[:, grp * 128:(grp + 1) * 128],
                                        wde[:, grp * 128:(grp + 1) * 128],
                                        sc_ap[:, grp:grp + 1])
                v.drain()
                v.engine_nop().then_inc(s_v, 1)

            for c in range(11):
                if c >= 1:
                    v.wait_ge(s_t, 64 + 16 * c)
                dequant(pk_sb[:, c * 256:(c + 1) * 256], 2048,
                        gsc_sb[:, c * 16:(c + 1) * 16], 16, None)
            for c in range(2):
                v.wait_ge(s_t, 64 + 176 + 44 * c)
                dequant(dpk_sb[:, c * 704:(c + 1) * 704], 5632,
                        dsc_sb[:, c * 44:(c + 1) * 44], 44, None)

        @blk.sync
        def _(sy):
            sy.wait_ge(s_v, 1)
            sy.dma_start(out=w2pT.rearrange("(c p) d -> p c d", p=128),
                         in_=w2o[:].rearrange("p (c d) -> p c d", c=64)).then_inc(st, 16)
            ntr = 0
            for c in range(11):
                for k in range(16):
                    sy.wait_ge(s_a, 64 + ntr + 1)
                    sy.dma_start(out=wguT[k * 128:(k + 1) * 128,
                                          c * 128:(c + 1) * 128],
                                 in_=wtr[:]).then_inc(st, 16)
                    ntr += 1
            for c in range(2):
                for k in range(44):
                    sy.wait_ge(s_a, 64 + ntr + 1)
                    sy.dma_start(out=wdnT[k * 128:(k + 1) * 128,
                                          c * 128:(c + 1) * 128],
                                 in_=wtr[:]).then_inc(st, 16)
                    ntr += 1
            sy.wait_ge(st, 16 * (1 + NTR))




def build_C():
    nc = bass.Bass("TRN2", target_bir_lowering=False)
    ya = nc.dram_tensor("ya", [TSL, DM], F32, kind="ExternalInput").ap()
    w2pT = nc.dram_tensor("w2pT", [FF0, DM], BF16, kind="ExternalInput").ap()
    io = dict(
        xa=nc.dram_tensor("xa", [TSL, DM], F32, kind="ExternalInput").ap(),
        w1T=nc.dram_tensor("w1T", [DM, FF0], F32, kind="ExternalInput").ap(),
        w3T=nc.dram_tensor("w3T", [DM, FF0], F32, kind="ExternalInput").ap(),
        idt=nc.dram_tensor("idt", [128, 128], BF16, kind="ExternalInput").ap(),
        z2o=nc.dram_tensor("z2o", [TSL, DM], F32, kind="ExternalOutput").ap(),
        z1f=nc.dram_tensor("z1f", [TSL, DM], F32, kind="Internal").ap(),
        hidT=nc.dram_tensor("hidT", [FF0, TSL], BF16, kind="Internal").ap(),
        ya_tile=lambda eng, tt: ya[tt * 128:(tt + 1) * 128, :].rearrange(
            "t (r d) -> t r d", r=8),
        w2p_tile=lambda hc, dc: w2pT[hc * 128:(hc + 1) * 128,
                                     dc * 512:(dc + 1) * 512].rearrange(
                                         "p (r d) -> p r d", r=2),
        z2tl=None, offs_sb=None)
    with ExitStack() as ctx:
        phase_C(nc, NcProxy(nc, "C_"), ctx.enter_context, io)
    return nc


def phase_C(nc, ncx, e, io):
    xa = io["xa"]; w1T = io["w1T"]; w3T = io["w3T"]; idt = io["idt"]
    z2o = io["z2o"]; z1f = io["z1f"]; hidT = io["hidT"]
    ya_tile = io["ya_tile"]; w2p_tile = io["w2p_tile"]; z2tl = io["z2tl"]

    NT = TSL // 128
    NK = DM // 128
    NH0 = FF0 // 128
    INV = 1.0 / DM

    if True:
        zt = e(ncx.sbuf_tensor("zt", [128, DM], F32))
        yt_in = e(ncx.sbuf_tensor("yt_in", [128, DM], F32))
        z1t = e(ncx.sbuf_tensor("z1t", [128, DM], F32))
        z1b = e(ncx.sbuf_tensor("z1b", [128, DM], BF16))
        mcol = e(ncx.sbuf_tensor("mcol", [128, 4], F32))
        eps_c = e(ncx.sbuf_tensor("eps_c", [128, 1], F32))
        idt_sb = e(ncx.sbuf_tensor("idt_sb", [128, 128], BF16))
        z1T = e(ncx.sbuf_tensor("z1T", [128, NK * TSL], BF16))
        w1_sb = e(ncx.sbuf_tensor("w1_sb", [128, 2 * NK * 128], BF16))
        w3_sb = e(ncx.sbuf_tensor("w3_sb", [128, 2 * NK * 128], BF16))
        if z2tl is not None:
            z2t_sb = w1_sb[:, 0:NK * 128]
        sg = e(ncx.sbuf_tensor("sg", [128, 512], BF16))
        hb = e(ncx.sbuf_tensor("hb", [128, 2 * 512], BF16))
        hw_sb = e(ncx.sbuf_tensor("hw_sb", [128, NH0 * 512], BF16))
        w2_sb = e(ncx.sbuf_tensor("w2_sb", [128, 8 * 512], BF16))
        acc3 = [e(ncx.sbuf_tensor(f"acc{i}", [128, DM], F32)) for i in range(3)]
        accs = acc3 + [yt_in]
        if z2tl is not None:
            z2b = z1b
            s_tr3 = e(ncx.semaphore("s_tr3"))
            s_te3 = e(ncx.semaphore("s_te3"))
            s_zb = e(ncx.semaphore("s_zb"))
            stz = e(ncx.semaphore("stz"))
        p_g = e(ncx.psum_tensor("p_g", [128, 2 * 512], F32))
        p_u = e(ncx.psum_tensor("p_u", [128, 2 * 512], F32))
        p_o = [e(ncx.psum_tensor(f"p_o{i}", [128, 512], F32)) for i in range(4)]
        p_tr = p_g[:, 0:64].bitcast(BF16)
        p_tr2 = p_g[:, 0:128].bitcast(BF16)
        ld = e(ncx.semaphore("ld"))
        ldi = e(ncx.semaphore("ldi"))
        ldwp = [e(ncx.semaphore("ldwA")), e(ncx.semaphore("ldwB"))]
        ldh = e(ncx.semaphore("ldh"))
        ldw2p = [e(ncx.semaphore("ldw2%d" % qq)) for qq in range(8)]
        ldz = e(ncx.semaphore("ldz"))
        s_xm = e(ncx.semaphore("s_xm"))
        s_sd = e(ncx.semaphore("s_sd"))
        s_z1 = e(ncx.semaphore("s_z1"))
        s_tr = e(ncx.semaphore("s_tr"))
        s_te = e(ncx.semaphore("s_te"))
        s_gu = e(ncx.semaphore("s_gu"))
        s_sg = e(ncx.semaphore("s_sg"))
        s_hb = e(ncx.semaphore("s_hb"))
        s_w2c = e(ncx.semaphore("s_w2c"))
        s_ev = e(ncx.semaphore("s_ev"))
        s_ln = e(ncx.semaphore("s_ln"))
        st = e(ncx.semaphore("st"))
        sthp = [e(ncx.semaphore("sthA")), e(ncx.semaphore("sthB"))]
        st2 = e(ncx.semaphore("st2"))
        blk = e(nc.Block())

        @blk.sync
        def _(sy):
            sy.dma_start(out=idt_sb[:], in_=idt[:]).then_inc(ldi, 16)
            for tt in range(NT):
                if tt >= 1:
                    sy.wait_ge(s_z1, tt)
                sy.dma_start(out=zt[:],
                             in_=xa[tt * 128:(tt + 1) * 128, :]).then_inc(ld, 16)
                sy.dma_start(out=yt_in[:].rearrange("p (r d) -> p r d", r=8),
                             in_=ya_tile(sy, tt)).then_inc(ld, 16)
                sy.wait_ge(s_z1, tt + 1)
                sy.dma_start(out=z1f[tt * 128:(tt + 1) * 128, :],
                             in_=z1t[:]).then_inc(st, 16)

        @blk.vector
        def _(v):
            v.memset(eps_c[:], 1e-5)
            for tt in range(NT):
                v.wait_ge(ld, 16 * (2 * tt + 2))
                v.tensor_add(zt[:], zt[:], yt_in[:])
                v.drain()
                v.tensor_reduce(mcol[:, 0:1], zt[:], AX.X, ALU.add)
                v.drain()
                v.tensor_scalar_mul(mcol[:, 0:1], mcol[:, 0:1], INV)
                v.drain()
                v.tensor_scalar(zt[:], zt[:], mcol[:, 0:1], None,
                                ALU.subtract).then_inc(s_xm, 1)
                v.wait_ge(s_sd, tt + 1)
                v.reciprocal(mcol[:, 2:3], mcol[:, 1:2])
                if tt >= 1:
                    v.wait_ge(st, 16 * tt)
                    v.wait_ge(s_tr, NK * tt)
                v.drain()
                v.tensor_scalar_mul(z1t[:], zt[:], mcol[:, 2:3])
                v.drain()
                v.tensor_copy(z1b[:], z1t[:]).then_inc(s_z1, 1)

        @blk.scalar
        def _(a):
            n = 0
            for tt in range(NT):
                a.wait_ge(s_xm, tt + 1)
                if tt >= 1:
                    a.wait_ge(st, 16 * tt)
                a.drain()
                a.activation(z1t[:], zt[:], AF.Square, accum_out=mcol[:, 3:4])
                a.drain()
                a.activation(mcol[:, 1:2], mcol[:, 3:4], AF.Sqrt, scale=INV,
                             bias=eps_c[:]).then_inc(s_sd, 1)
                for k in range(NK):
                    a.wait_ge(s_tr, n + 1)
                    a.drain()
                    a.activation(z1T[:, k * TSL + tt * 128:k * TSL + (tt + 1) * 128],
                                 p_tr, AF.Identity).then_inc(s_te, 1)
                    n += 1

        @blk.tensor
        def _(t):
            t.wait_ge(ldi, 16)
            n = 0
            for tt in range(NT):
                t.wait_ge(s_z1, tt + 1)
                for k in range(NK):
                    if n >= 1:
                        t.wait_ge(s_te, n)
                    mm = t.transpose(p_tr, z1b[:, k * 128:(k + 1) * 128],
                                     idt_sb[:])
                    mm.then_inc(s_tr, 1)
                    n += 1
            t.wait_ge(s_te, NT * NK)
            gi = 0
            for hc in range(NH0):
                t.wait_ge(ldwp[hc % 2], 32 * (hc // 2 + 1))
                hb_ = (hc % 2) * NK * 128
                for tcx in range(4):
                    gb = (gi % 2) * 512
                    if gi >= 2:
                        t.wait_ge(s_hb, gi - 1)
                    for k in range(NK):
                        t.matmul(p_g[:, gb:gb + 512],
                                 w1_sb[:, hb_ + k * 128:hb_ + (k + 1) * 128],
                                 z1T[:, k * TSL + tcx * 512:k * TSL + tcx * 512 + 512],
                                 start=(k == 0), stop=(k == NK - 1))
                    for k in range(NK):
                        mm = t.matmul(p_u[:, gb:gb + 512],
                                      w3_sb[:, hb_ + k * 128:hb_ + (k + 1) * 128],
                                      z1T[:, k * TSL + tcx * 512:k * TSL + tcx * 512 + 512],
                                      start=(k == 0), stop=(k == NK - 1))
                    mm.then_inc(s_gu, 1)
                    gi += 1
            for w in range(4):
                t.wait_ge(ldh, 16 * NH0 * (w + 1))
                for dc in range(4):
                    idx = w * 4 + dc
                    if idx > 0:
                        t.wait_ge(s_ev, idx)
                    t.wait_ge(ldz, 16 * (w + 1) * 4)
                    for hc in range(NH0):
                        widx = idx * NH0 + hc
                        t.wait_ge(ldw2p[widx % 8], 16 * (widx // 8 + 1))
                        wb = (widx % 8) * 512
                        for ttl in range(4):
                            mm = t.matmul(p_o[ttl][:],
                                          hw_sb[:, hc * 512 + ttl * 128:hc * 512 + (ttl + 1) * 128],
                                          w2_sb[:, wb:wb + 512],
                                          start=(hc == 0), stop=(hc == NH0 - 1))
                        mm.then_inc(s_w2c, 1)
                if z2tl is not None:
                    for ttl in range(4):
                        li = w * 4 + ttl
                        t.wait_ge(s_zb, li + 1)
                        for k in range(NK):
                            ti = li * NK + k
                            if ti >= 1:
                                t.wait_ge(s_te3, ti)
                            t.transpose(p_tr2[:, 0:128],
                                        z2b[:, k * 128:(k + 1) * 128],
                                        idt_sb[:]).then_inc(s_tr3, 1)

        @blk.gpsimd
        def _(g):
            for hc in range(NH0):
                if hc >= 2:
                    g.wait_ge(s_gu, 4 * (hc - 1))
                hb_ = (hc % 2) * NK * 128
                g.dma_start(out=w1_sb[:, hb_:hb_ + NK * 128].rearrange(
                                "p (k h) -> p k h", k=NK),
                            in_=w1T.rearrange("(k p) h -> p k h", p=128)[
                                :, :, hc * 128:(hc + 1) * 128]).then_inc(ldwp[hc % 2], 16)
                g.dma_start(out=w3_sb[:, hb_:hb_ + NK * 128].rearrange(
                                "p (k h) -> p k h", k=NK),
                            in_=w3T.rearrange("(k p) h -> p k h", p=128)[
                                :, :, hc * 128:(hc + 1) * 128]).then_inc(ldwp[hc % 2], 16)
            for w in range(4):
                for dc in range(4):
                    idx = w * 4 + dc
                    for hc in range(NH0):
                        widx = idx * NH0 + hc
                        if widx >= 8:
                            g.wait_ge(s_w2c, widx - 7)
                        wb = (widx % 8) * 512
                        g.dma_start(out=w2_sb[:, wb:wb + 512].rearrange(
                                        "p (r d) -> p r d", r=2),
                                    in_=w2p_tile(hc, dc)).then_inc(ldw2p[widx % 8], 16)

        @blk.scalar
        def _(a):
            gi = 0
            for hc in range(NH0):
                for tcx in range(4):
                    a.wait_ge(s_gu, gi + 1)
                    if gi >= 1:
                        a.wait_ge(s_hb, gi)
                    a.activation(sg[:], p_g[:, (gi % 2) * 512:(gi % 2) * 512 + 512],
                                 AF.Silu).then_inc(s_sg, 1)
                    gi += 1
            a.wait_ge(sthp[0], 16 * 128)
            a.wait_ge(sthp[1], 16 * 128)
            li = 0
            for w in range(4):
                if w >= 1:
                    a.wait_ge(s_w2c, w * 4 * NH0)
                for hc in range(NH0):
                    a.dma_start(out=hw_sb[:, hc * 512:(hc + 1) * 512],
                                in_=hidT[hc * 128:(hc + 1) * 128,
                                         w * 512:(w + 1) * 512]).then_inc(ldh, 16)
                for ttl in range(4):
                    a.wait_ge(s_xm, NT + li + 1)
                    if li >= 1:
                        a.wait_ge(st2, 16 * li)
                    a.drain()
                    a.activation(z1t[:], accs[li % 4][:], AF.Square,
                                 accum_out=mcol[:, 3:4])
                    a.drain()
                    a.activation(mcol[:, 1:2], mcol[:, 3:4], AF.Sqrt, scale=INV,
                                 bias=eps_c[:]).then_inc(s_sd, 1)
                    li += 1

        @blk.vector
        def _(v):
            gi = 0
            for hc in range(NH0):
                for tcx in range(4):
                    v.wait_ge(s_sg, gi + 1)
                    if gi >= 2:
                        v.wait_ge(sthp[gi % 2], 16 * (gi // 2))
                    v.tensor_mul(hb[:, (gi % 2) * 512:(gi % 2 + 1) * 512],
                                 sg[:], p_u[:, (gi % 2) * 512:(gi % 2) * 512 + 512]
                                 ).then_inc(s_hb, 1)
                    gi += 1
            for w in range(4):
                for dc in range(4):
                    idx = w * 4 + dc
                    v.wait_ge(s_w2c, (idx + 1) * NH0)
                    v.wait_ge(ldz, 16 * (w + 1) * 4)
                    v.drain()
                    for ttl in range(4):
                        la = v.tensor_add(accs[ttl][:, dc * 512:(dc + 1) * 512],
                                          accs[ttl][:, dc * 512:(dc + 1) * 512],
                                          p_o[ttl][:])
                    la.then_inc(s_ev, 1)
                    v.drain()
                for ttl in range(4):
                    li = w * 4 + ttl
                    acc = accs[ttl]
                    v.tensor_reduce(mcol[:, 0:1], acc[:], AX.X, ALU.add)
                    v.drain()
                    v.tensor_scalar_mul(mcol[:, 0:1], mcol[:, 0:1], INV)
                    v.drain()
                    v.tensor_scalar(acc[:], acc[:], mcol[:, 0:1], None,
                                    ALU.subtract).then_inc(s_xm, 1)
                    v.wait_ge(s_sd, NT + li + 1)
                    v.reciprocal(mcol[:, 2:3], mcol[:, 1:2])
                    if li >= 1:
                        v.wait_ge(st2, 16 * li)
                    v.drain()
                    v.tensor_scalar_mul(z1t[:], acc[:], mcol[:, 2:3]).then_inc(s_ln, 1)
                    if z2tl is not None:
                        if li >= 1:
                            v.wait_ge(s_tr3, NK * li)
                        v.drain()
                        v.tensor_copy(z2b[:], z1t[:]).then_inc(s_zb, 1)

        @blk.sync
        def _(sy):
            gi = 0
            for hc in range(NH0):
                for tcx in range(4):
                    sy.wait_ge(s_hb, gi + 1)
                    sy.dma_start(out=hidT[hc * 128:(hc + 1) * 128,
                                          tcx * 512:(tcx + 1) * 512],
                                 in_=hb[:, (gi % 2) * 512:(gi % 2 + 1) * 512]
                                 ).then_inc(sthp[gi % 2], 16)
                    gi += 1
            sy.wait_ge(st, 16 * NT)
            for w in range(4):
                for ttl in range(4):
                    if w >= 1:
                        sy.wait_ge(s_ln, w * 4)
                    sy.dma_start(out=accs[ttl][:],
                                 in_=z1f[w * 512 + ttl * 128:w * 512 + (ttl + 1) * 128,
                                         :]).then_inc(ldz, 16)
                for ttl in range(4):
                    li = w * 4 + ttl
                    sy.wait_ge(s_ln, li + 1)
                    sy.dma_start(out=z2o[w * 512 + ttl * 128:w * 512 + (ttl + 1) * 128, :],
                                 in_=z1t[:]).then_inc(st2, 16)
                    if z2tl is not None:
                        li = w * 4 + ttl
                        sy.wait_ge(s_te3, NK * (li + 1))
                        sy.dma_start(out=z2tl.rearrange("(k p) t -> p k t", p=128)[
                                         :, :, li * 128:(li + 1) * 128],
                                     in_=z2t_sb[:].rearrange("p (k t) -> p k t", k=NK)
                                     ).then_inc(stz, 16)
            sy.wait_ge(st2, 16 * NT)
            if z2tl is not None:
                sy.wait_ge(stz, 16 * NT)




def build_B():
    nc = bass.Bass("TRN2", target_bir_lowering=False)
    io = dict(
        dtTs=nc.dram_tensor("dtTs", [256, TOK], F32, kind="ExternalInput").ap(),
        xTs=nc.dram_tensor("xTs", [256, TOK], F32, kind="ExternalInput").ap(),
        Ash=nc.dram_tensor("Ash", [128, 32], F32, kind="ExternalInput").ap(),
        Dsh=nc.dram_tensor("Dsh", [128, 2], F32, kind="ExternalInput").ap(),
        BT=nc.dram_tensor("BT", [NST, TOK], F32, kind="ExternalInput").ap(),
        CT=nc.dram_tensor("CT", [NST, TOK], F32, kind="ExternalInput").ap(),
        idt=nc.dram_tensor("idt", [128, 128], BF16, kind="ExternalInput").ap(),
        yTs=nc.dram_tensor("yTs", [256, TOK], F32, kind="ExternalOutput").ap(),
        ytr=None)
    with ExitStack() as ctx:
        phase_B(nc, NcProxy(nc, "B_"), ctx.enter_context, io)
    return nc


def phase_B(nc, ncx, e, io):
    dtTs = io["dtTs"]; xTs = io["xTs"]; Ash = io["Ash"]; Dsh = io["Dsh"]
    BT = io["BT"]; CT = io["CT"]; idt = io["idt"]; yTs = io["yTs"]
    ytr = io["ytr"]

    NTC = TOK // 512
    NITER = NTC * 2

    if True:
        btst = e(ncx.sbuf_tensor("btst", [1, 2 * NST * 512], BF16))
        ctst = e(ncx.sbuf_tensor("ctst", [1, 2 * NST * 512], BF16))
        alog_sb = e(ncx.sbuf_tensor("alog_sb", [128, 32], F32))
        a_sb = e(ncx.sbuf_tensor("a_sb", [128, 32], F32))
        d_sb = e(ncx.sbuf_tensor("d_sb", [128, 2], F32))
        ones1 = e(ncx.sbuf_tensor("ones1", [1, 128], BF16))
        idt_sb = e(ncx.sbuf_tensor("idt_sb", [128, 128], BF16))
        dt_t = e(ncx.sbuf_tensor("dt_t", [128, 2 * 512], F32))
        x_t = e(ncx.sbuf_tensor("x_t", [128, 2 * 512], F32))
        u_sb = e(ncx.sbuf_tensor("u_sb", [128, 512], F32))
        dA_sb = e(ncx.sbuf_tensor("dA_sb", [128, 2 * 512], F32))
        dbx_sb = e(ncx.sbuf_tensor("dbx_sb", [128, 2 * 512], F32))
        h_sb = e(ncx.sbuf_tensor("h_sb", [128, 2 * 512], F32))
        yt_bf = e(ncx.sbuf_tensor("yt_bf", [128, 2 * 512], BF16))
        yo_sb = e(ncx.sbuf_tensor("yo_sb", [128, 2 * 512], F32))
        state = e(ncx.sbuf_tensor("state", [128, 32], F32))
        if ytr is not None:
            ytr_sb = e(ncx.sbuf_tensor("ytr_sb", [128, 2 * 512], F32))
            if32 = e(ncx.sbuf_tensor("if32", [128, 128], F32))
            p_yt = e(ncx.psum_tensor("p_yt", [128, 128], F32))
            s_tr2 = e(ncx.semaphore("s_tr2"))
            s_te2 = e(ncx.semaphore("s_te2"))
        p_br = e(ncx.psum_tensor("p_br", [128, 2 * 512], F32))
        p_cr = e(ncx.psum_tensor("p_cr", [128, 3 * 512], F32))
        p_y = e(ncx.psum_tensor("p_y", [128, 2 * 512], F32))
        ld = e(ncx.semaphore("ld"))
        ldtp = [e(ncx.semaphore("ldt0")), e(ncx.semaphore("ldt1"))]
        ldbp = [e(ncx.semaphore("ldb0")), e(ncx.semaphore("ldb1"))]
        s_t = e(ncx.semaphore("s_t"))
        s_a = e(ncx.semaphore("s_a"))
        s_vb = e(ncx.semaphore("s_vb"))
        s_vs = e(ncx.semaphore("s_vs"))
        s_vy = e(ncx.semaphore("s_vy"))
        s_ty = e(ncx.semaphore("s_ty"))
        s_y = e(ncx.semaphore("s_y"))
        s_ai = e(ncx.semaphore("s_ai"))
        stpB = [e(ncx.semaphore("stB0")), e(ncx.semaphore("stB1"))]
        blk = e(nc.Block())

        @blk.sync
        def _(sy):
            sy.dma_start(out=alog_sb[:], in_=Ash[:]).then_inc(ld, 16)
            sy.dma_start(out=d_sb[:], in_=Dsh[:]).then_inc(ld, 16)
            sy.dma_start(out=idt_sb[:], in_=idt[:]).then_inc(ld, 16)

        @blk.gpsimd
        def _(g):
            for i in range(NITER):
                tc, gr = i // 2, i % 2
                if gr == 0:
                    if tc >= 2:
                        g.wait_ge(s_t, 32 * (tc - 1))
                    sb = slice((tc % 2) * NST * 512, (tc % 2 + 1) * NST * 512)
                    g.dma_start(out=btst[0:1, sb],
                                in_=BT[:, tc * 512:(tc + 1) * 512]).then_inc(ldbp[tc % 2], 16)
                    g.dma_start(out=ctst[0:1, sb],
                                in_=CT[:, tc * 512:(tc + 1) * 512]).then_inc(ldbp[tc % 2], 16)
                if i >= 2:
                    g.wait_ge(s_y, i - 1)
                b = i % 2
                g.dma_start(out=dt_t[:, b * 512:(b + 1) * 512],
                            in_=dtTs[gr * 128:(gr + 1) * 128,
                                     tc * 512:(tc + 1) * 512]).then_inc(ldtp[i % 2], 16)
                g.dma_start(out=x_t[:, b * 512:(b + 1) * 512],
                            in_=xTs[gr * 128:(gr + 1) * 128,
                                    tc * 512:(tc + 1) * 512]).then_inc(ldtp[i % 2], 16)

        @blk.tensor
        def _(t):
            t.wait_ge(s_ai, 1)
            nrep = 0
            for i in range(NITER):
                tc = i // 2
                t.wait_ge(ldbp[tc % 2], 32 * (tc // 2 + 1))
                base = (tc % 2) * NST * 512
                if i >= 2:
                    t.wait_ge(s_y, i - 1)
                for n in range(NST):
                    nb = nrep % 2
                    psl = slice(nb * 512, (nb + 1) * 512)
                    psl3 = slice((nrep % 3) * 512, (nrep % 3) * 512 + 512)
                    ssl = slice(base + n * 512, base + (n + 1) * 512)
                    if nrep >= 2:
                        t.wait_ge(s_vb, nrep - 1)
                    if nrep >= 3:
                        t.wait_ge(s_vy, nrep - 2)
                    t.matmul(p_br[:, psl], ones1[:], btst[0:1, ssl],
                             start=True, stop=True)
                    mm = t.matmul(p_cr[:, psl3], ones1[:], ctst[0:1, ssl],
                                  start=True, stop=True)
                    mm.then_inc(s_t, 1)
                    if n >= 2:
                        nn = nrep - 2
                        t.wait_ge(s_vy, nn + 1)
                        t.matmul(p_y[:, (i % 2) * 512:(i % 2 + 1) * 512],
                                 idt_sb[:], yt_bf[:, (nn % 2) * 512:(nn % 2 + 1) * 512],
                                 start=(nn % NST == 0), stop=False).then_inc(s_ty, 1)
                    nrep += 1
                for nn in range(nrep - 2, nrep):
                    t.wait_ge(s_vy, nn + 1)
                    t.matmul(p_y[:, (i % 2) * 512:(i % 2 + 1) * 512],
                             idt_sb[:], yt_bf[:, (nn % 2) * 512:(nn % 2 + 1) * 512],
                             start=False, stop=(nn % NST == NST - 1)).then_inc(s_ty, 1)
                if ytr is not None and i >= 1:
                    j = i - 1
                    t.wait_ge(s_y, j + 1)
                    for q in range(4):
                        ti = 4 * j + q
                        if ti >= 1:
                            t.wait_ge(s_te2, ti)
                        t.transpose(p_yt[:],
                                    yo_sb[:, (j % 2) * 512 + q * 128:(j % 2) * 512 + (q + 1) * 128],
                                    if32[:]).then_inc(s_tr2, 1)
            if ytr is not None:
                j = NITER - 1
                t.wait_ge(s_y, j + 1)
                for q in range(4):
                    ti = 4 * j + q
                    t.wait_ge(s_te2, ti)
                    t.transpose(p_yt[:],
                                yo_sb[:, (j % 2) * 512 + q * 128:(j % 2) * 512 + (q + 1) * 128],
                                if32[:]).then_inc(s_tr2, 1)

        @blk.scalar
        def _(a):
            a.wait_ge(ld, 48)
            a.activation(a_sb[:], alog_sb[:], AF.Exp).then_inc(s_a, 1)
            a.wait_ge(s_ai, 1)
            nrep = 0
            for i in range(NITER):
                for n in range(NST):
                    nb = nrep % 2
                    a.wait_ge(ldtp[i % 2], 32 * (i // 2 + 1))
                    if nrep >= 2:
                        a.wait_ge(s_vs, nrep - 1)
                    a.activation(dA_sb[:, nb * 512:(nb + 1) * 512],
                                 dt_t[:, (i % 2) * 512:(i % 2 + 1) * 512],
                                 AF.Exp,
                                 scale=a_sb[:, (i % 2) * 16 + n:(i % 2) * 16 + n + 1]
                                 ).then_inc(s_a, 1)
                    nrep += 1
                if ytr is not None and i >= 1:
                    j = i - 1
                    if j >= 2:
                        a.wait_ge(stpB[j % 2], 64 * (j // 2))
                    for q in range(4):
                        ti = 4 * j + q
                        a.wait_ge(s_tr2, ti + 1)
                        a.drain()
                        a.activation(ytr_sb[:, (j % 2) * 512 + q * 128:(j % 2) * 512 + (q + 1) * 128],
                                     p_yt[:],
                                     AF.Identity).then_inc(s_te2, 1)
            if ytr is not None:
                j = NITER - 1
                a.wait_ge(stpB[j % 2], 64 * (j // 2))
                for q in range(4):
                    ti = 4 * j + q
                    a.wait_ge(s_tr2, ti + 1)
                    a.drain()
                    a.activation(ytr_sb[:, (j % 2) * 512 + q * 128:(j % 2) * 512 + (q + 1) * 128],
                                 p_yt[:],
                                 AF.Identity).then_inc(s_te2, 1)

        @blk.vector
        def _(v):
            v.memset(ones1[:], 1.0)
            v.memset(state[:], 0.0)
            v.wait_ge(ld, 48)
            if ytr is not None:
                v.tensor_copy(if32[:], idt_sb[:])
            v.wait_ge(s_a, 1)
            v.tensor_scalar_mul(a_sb[:], a_sb[:], -1.0)
            v.drain()
            v.engine_nop().then_inc(s_ai, 1)
            nrep = 0
            for i in range(NITER):
                tc, gr = i // 2, i % 2
                b = i % 2
                bsl = slice(b * 512, (b + 1) * 512)
                v.wait_ge(ldtp[i % 2], 32 * (i // 2 + 1))
                v.drain()
                v.tensor_mul(u_sb[:], dt_t[:, bsl], x_t[:, bsl])
                v.drain()

                def em_dbx(n):
                    g = nrep0 + n
                    nb = g % 2
                    if g >= 2:
                        v.wait_ge(s_vs, g - 1)
                    v.wait_ge(s_t, g + 1)
                    v.tensor_mul(dbx_sb[:, nb * 512:(nb + 1) * 512], u_sb[:],
                                 p_br[:, nb * 512:(nb + 1) * 512]).then_inc(s_vb, 1)

                def em_scan(n):
                    g = nrep0 + n
                    nb = g % 2
                    scol = gr * 16 + n
                    v.wait_ge(s_vb, g + 1)
                    v.wait_ge(s_a, g + 2)
                    if g >= 2:
                        v.wait_ge(s_vy, g - 1)
                    init = 0.0 if tc % 16 == 0 else state[:, scol:scol + 1]
                    v.tensor_tensor_scan(h_sb[:, nb * 512:(nb + 1) * 512],
                                         dA_sb[:, nb * 512:(nb + 1) * 512],
                                         dbx_sb[:, nb * 512:(nb + 1) * 512],
                                         init, ALU.mult, ALU.add).then_inc(s_vs, 1)

                def em_tail(n):
                    g = nrep0 + n
                    nb = g % 2
                    scol = gr * 16 + n
                    v.wait_ge(s_vs, g + 1)
                    v.tensor_copy(state[:, scol:scol + 1],
                                  h_sb[:, nb * 512 + 511:nb * 512 + 512])
                    if g >= 2:
                        v.wait_ge(s_ty, g - 1)
                    v.tensor_mul(yt_bf[:, nb * 512:(nb + 1) * 512],
                                 h_sb[:, nb * 512:(nb + 1) * 512],
                                 p_cr[:, (g % 3) * 512:(g % 3) * 512 + 512]).then_inc(s_vy, 1)

                nrep0 = nrep
                em_dbx(0)
                em_dbx(1)
                em_scan(0)
                for n in range(2, NST):
                    em_dbx(n)
                    em_scan(n - 1)
                    em_tail(n - 2)
                em_scan(NST - 1)
                em_tail(NST - 2)
                em_tail(NST - 1)
                nrep += NST
                v.wait_ge(s_ty, nrep)
                if i >= 2:
                    if ytr is not None:
                        v.wait_ge(s_tr2, 4 * (i - 1))
                    else:
                        v.wait_ge(stpB[i % 2], 16 * (i // 2))
                v.drain()
                v.scalar_tensor_tensor(yo_sb[:, bsl], x_t[:, bsl],
                                       d_sb[:, gr:gr + 1],
                                       p_y[:, b * 512:(b + 1) * 512],
                                       ALU.mult, ALU.add).then_inc(s_y, 1)

        @blk.sync
        def _(sy):
            if ytr is None:
                for i in range(NITER):
                    tc, gr = i // 2, i % 2
                    b = i % 2
                    sy.wait_ge(s_y, i + 1)
                    sy.dma_start(out=yTs[gr * 128:(gr + 1) * 128,
                                         tc * 512:(tc + 1) * 512],
                                 in_=yo_sb[:, b * 512:(b + 1) * 512]).then_inc(stpB[i % 2], 16)
                sy.wait_ge(stpB[0], 16 * NITER // 2)
                sy.wait_ge(stpB[1], 16 * NITER // 2)
            else:
                for j in range(NITER):
                    tc, gr = j // 2, j % 2
                    for q in range(4):
                        ti = 4 * j + q
                        sy.wait_ge(s_te2, ti + 1)
                        sy.dma_start(out=ytr[tc * 512 + q * 128:tc * 512 + (q + 1) * 128,
                                             gr * 128:(gr + 1) * 128],
                                     in_=ytr_sb[:, (j % 2) * 512 + q * 128:(j % 2) * 512 + (q + 1) * 128]
                                     ).then_inc(stpB[j % 2], 16)
                sy.wait_ge(stpB[0], 64 * NITER // 2)
                sy.wait_ge(stpB[1], 64 * NITER // 2)




def build_D():
    nc = bass.Bass("TRN2", target_bir_lowering=False)
    z2T = nc.dram_tensor("z2T", [DM, SEQ], F32, kind="ExternalInput").ap()
    io = dict(
        wq=nc.dram_tensor("wq", [DM, 512], F32, kind="ExternalInput").ap(),
        wk=nc.dram_tensor("wk", [DM, 512], F32, kind="ExternalInput").ap(),
        wv=nc.dram_tensor("wv", [DM, 512], F32, kind="ExternalInput").ap(),
        masks=nc.dram_tensor("masks", [128, 5 * 512], BF16, kind="ExternalInput").ap(),
        Oc=nc.dram_tensor("Oc", [512, SEQ], BF16, kind="ExternalOutput").ap(),
        z2w_tile=lambda k, w8: z2T[k * 128:(k + 1) * 128,
                                   w8 * 1024:(w8 + 1) * 1024])
    with ExitStack() as ctx:
        phase_D(nc, NcProxy(nc, "D_"), ctx.enter_context, io)
    return nc


def phase_D(nc, ncx, e, io):
    wq = io["wq"]; wk = io["wk"]; wv = io["wv"]; masks = io["masks"]
    Oc = io["Oc"]; z2w_tile = io["z2w_tile"]

    NK = DM // 128
    NQC = SEQ // 512
    SCL = float(DH) ** -0.5

    if True:
        z2w = e(ncx.sbuf_tensor("z2w", [128, 2 * NK * 1024], BF16))
        wq_sb = e(ncx.sbuf_tensor("wq_sb", [128, NK * 128], BF16))
        wk_sb = e(ncx.sbuf_tensor("wk_sb", [128, NK * 128], BF16))
        wv_sb = e(ncx.sbuf_tensor("wv_sb", [128, NK * 128], BF16))
        QT = e(ncx.sbuf_tensor("QT", [128, SEQ], BF16))
        KT = e(ncx.sbuf_tensor("KT", [128, SEQ], BF16))
        V = e(ncx.sbuf_tensor("V", [128, SEQ], BF16))
        msk = e(ncx.sbuf_tensor("msk", [128, 5 * 512], BF16))
        pU = e(ncx.sbuf_tensor("pU", [128, 2 * 512], BF16))
        pM = e(ncx.sbuf_tensor("pM", [128, 4 * 512], BF16))
        rs_sb = e(ncx.sbuf_tensor("rs_sb", [1, 512], F32))
        rc_sb = e(ncx.sbuf_tensor("rc_sb", [1, 512], F32))
        vsum = e(ncx.sbuf_tensor("vsum", [128, 2 * 512], F32))
        ones_kf = e(ncx.sbuf_tensor("ones_kf", [128, 1], F32))
        rep_sb = e(ncx.sbuf_tensor("rep_sb", [128, 512], F32))
        Ob = e(ncx.sbuf_tensor("Ob", [128, 2 * 512], BF16))
        n20 = e(ncx.sbuf_tensor("n20", [128, 1], F32))
        ones_k = e(ncx.sbuf_tensor("ones_k", [128, 1], BF16))
        ones_r = e(ncx.sbuf_tensor("ones_r", [1, 128], F32))
        p_q = e(ncx.psum_tensor("p_q", [128, 2 * 512], F32))
        p_s = e(ncx.psum_tensor("p_s", [128, 2 * 512], F32))
        p_rs = e(ncx.psum_tensor("p_rs", [1, 512], F32))
        p_o = e(ncx.psum_tensor("p_o", [128, 512], F32))
        p_rep = e(ncx.psum_tensor("p_rep", [128, 512], F32))
        ld0 = e(ncx.semaphore("ld0"))
        ldz = e(ncx.semaphore("ldz"))
        ldw = e(ncx.semaphore("ldw"))
        s_qkv = e(ncx.semaphore("s_qkv"))
        s_qe = e(ncx.semaphore("s_qe"))
        s_s = e(ncx.semaphore("s_s"))
        s_p = e(ncx.semaphore("s_p"))
        s_pm = e(ncx.semaphore("s_pm"))
        s_av = e(ncx.semaphore("s_av"))
        s_vv = e(ncx.semaphore("s_vv"))
        s_vs = e(ncx.semaphore("s_vs"))
        s_rc = e(ncx.semaphore("s_rc"))
        s_rp = e(ncx.semaphore("s_rp"))
        s_oe = e(ncx.semaphore("s_oe"))
        s_vc = e(ncx.semaphore("s_vc"))
        stpD = [e(ncx.semaphore("stD0")), e(ncx.semaphore("stD1"))]
        blk = e(nc.Block())

        def sched():
            ai = 0
            diag_ai = []
            plan = []
            for h in range(4):
                for qc in range(NQC):
                    for kt in range(4 * qc + 4):
                        diag = kt >= 4 * qc
                        di = len(diag_ai) if diag else -1
                        if diag:
                            diag_ai.append(ai)
                        plan.append((h, qc, kt, diag, ai, di))
                        ai += 1
            return plan, diag_ai
        PLAN, DIAG_AI = sched()
        NAI = len(PLAN)

        @blk.sync
        def _(sy):
            sy.dma_start(out=msk[:], in_=masks[:]).then_inc(ld0, 16)
            oi = 0
            for h in range(4):
                for qc in range(NQC):
                    sy.wait_ge(s_oe, oi + 1)
                    sy.dma_start(out=Oc[h * 128:(h + 1) * 128,
                                        qc * 512:(qc + 1) * 512],
                                 in_=Ob[:, (oi % 2) * 512:(oi % 2 + 1) * 512]
                                 ).then_inc(stpD[oi % 2], 16)
                    oi += 1
            sy.wait_ge(stpD[0], 16 * 32)
            sy.wait_ge(stpD[1], 16 * 32)

        @blk.gpsimd
        def _(g):
            for h in range(4):
                if h >= 1:
                    g.wait_ge(s_qkv, 96 * h)
                g.dma_start(out=wq_sb[:].rearrange("p (k j) -> p k j", k=NK),
                            in_=wq.rearrange("(k p) j -> p k j", p=128)[
                                :, :, h * 128:(h + 1) * 128]).then_inc(ldw, 16)
                g.dma_start(out=wk_sb[:].rearrange("p (k j) -> p k j", k=NK),
                            in_=wk.rearrange("(k p) j -> p k j", p=128)[
                                :, :, h * 128:(h + 1) * 128]).then_inc(ldw, 16)
                g.dma_start(out=wv_sb[:].rearrange("p (k j) -> p k j", k=NK),
                            in_=wv.rearrange("(k p) j -> p k j", p=128)[
                                :, :, h * 128:(h + 1) * 128]).then_inc(ldw, 16)
                for w8 in range(8):
                    gw = 8 * h + w8
                    if gw >= 2:
                        g.wait_ge(s_qkv, 12 * (gw - 1))
                    zb = (gw % 2) * NK * 1024
                    for k in range(NK):
                        g.dma_start(out=z2w[:, zb + k * 1024:zb + (k + 1) * 1024],
                                    in_=z2w_tile(k, w8)).then_inc(ldz, 16)

        @blk.tensor
        def _(t):
            t.wait_ge(ld0, 16)
            t.wait_ge(s_vc, 1)
            qi = 0
            for h in range(4):
                t.wait_ge(ldw, 48 * (h + 1))
                for w8 in range(8):
                    gw = 8 * h + w8
                    t.wait_ge(ldz, 256 * (gw + 1))
                    zb = (gw % 2) * NK * 1024
                    for which, wsb in ((0, wq_sb), (1, wk_sb)):
                        for tcq in range(2):
                            if qi >= 2:
                                t.wait_ge(s_qe, qi - 1)
                            pb = (qi % 2) * 512
                            for k in range(NK):
                                mm = t.matmul(
                                    p_q[:, pb:pb + 512],
                                    wsb[:, k * 128:(k + 1) * 128],
                                    z2w[:, zb + k * 1024 + tcq * 512:zb + k * 1024 + tcq * 512 + 512],
                                    start=(k == 0), stop=(k == NK - 1))
                            mm.then_inc(s_qkv, 1)
                            qi += 1
                    for tt in range(8):
                        if qi >= 2:
                            t.wait_ge(s_qe, qi - 1)
                        pb = (qi % 2) * 512
                        for k in range(NK):
                            mm = t.matmul(
                                p_q[:, pb:pb + 128],
                                z2w[:, zb + k * 1024 + tt * 128:zb + k * 1024 + (tt + 1) * 128],
                                wv_sb[:, k * 128:(k + 1) * 128],
                                start=(k == 0), stop=(k == NK - 1))
                        mm.then_inc(s_qkv, 1)
                        qi += 1
                for qc in range(NQC):
                    gqc = h * NQC + qc
                    if gqc >= 1:
                        t.wait_ge(s_oe, gqc)
                    nkt = 4 * qc + 4
                    base = AI_BASE[(h, qc)]

                    def emit_scores(kt):
                        ai = base + kt
                        if ai >= 2:
                            t.wait_ge(s_p, ai - 1)
                        sb_ = (ai % 2) * 512
                        t.matmul(p_s[:, sb_:sb_ + 512],
                                 KT[:, kt * 128:(kt + 1) * 128],
                                 QT[:, qc * 512:(qc + 1) * 512],
                                 start=True, stop=True).then_inc(s_s, 1)

                    def emit_av(kt):
                        ai = base + kt
                        if kt >= 4 * qc:
                            di = DI_OF[ai]
                            t.wait_ge(s_pm, di + 1)
                            rhs = pM[:, (di % 4) * 512:(di % 4 + 1) * 512]
                        else:
                            t.wait_ge(s_p, ai + 1)
                            rhs = pU[:, (ai % 2) * 512:(ai % 2 + 1) * 512]
                        mm = t.matmul(p_o[:], V[:, kt * 128:(kt + 1) * 128], rhs,
                                      start=(kt == 0), stop=(kt == nkt - 1))
                        mm.then_inc(s_av, 1)

                    emit_scores(0)
                    for kt in range(1, nkt):
                        emit_scores(kt)
                        emit_av(kt - 1)
                    emit_av(nkt - 1)
                    t.wait_ge(s_vs, gqc + 1)
                    t.matmul(p_rs[:], ones_kf[:], vsum[:, 0:512],
                             start=True, stop=True).then_inc(s_s, 1)
                    t.wait_ge(s_rc, gqc + 1)
                    t.matmul(p_rep[:], ones_r[:], rc_sb[:],
                             start=True, stop=True).then_inc(s_rp, 1)

        @blk.scalar
        def _(a):
            qi = 0
            for h in range(4):
                for w8 in range(8):
                    for which in range(2):
                        for tcq in range(2):
                            a.wait_ge(s_qkv, qi + 1)
                            dst = (QT if which == 0 else KT)[
                                :, w8 * 1024 + tcq * 512:w8 * 1024 + tcq * 512 + 512]
                            pb = (qi % 2) * 512
                            if which == 0:
                                a.activation(dst, p_q[:, pb:pb + 512],
                                             AF.Identity, scale=SCL).then_inc(s_qe, 1)
                            else:
                                a.activation(dst, p_q[:, pb:pb + 512],
                                             AF.Identity).then_inc(s_qe, 1)
                            qi += 1
                    for tt in range(8):
                        a.wait_ge(s_qkv, qi + 1)
                        pb = (qi % 2) * 512
                        a.activation(V[:, (w8 * 8 + tt) * 128:(w8 * 8 + tt + 1) * 128],
                                     p_q[:, pb:pb + 128],
                                     AF.Identity).then_inc(s_qe, 1)
                        qi += 1
                for qc in range(NQC):
                    nkt = 4 * qc + 4
                    for kt in range(nkt):
                        ai = AI_BASE[(h, qc)] + kt + (h * NQC + qc)
                        ai0 = AI_BASE[(h, qc)] + kt
                        a.wait_ge(s_s, ai + 1)
                        if ai0 >= 2:
                            a.wait_ge(s_av, ai0 - 1)
                            a.wait_ge(s_vv, ai0 - 1)
                        sb_ = (ai0 % 2) * 512
                        a.activation(pU[:, (ai0 % 2) * 512:(ai0 % 2 + 1) * 512],
                                     p_s[:, sb_:sb_ + 512], AF.Exp,
                                     bias=n20[:]).then_inc(s_p, 1)

        @blk.vector
        def _(v):
            v.memset(n20[:], -20.0)
            v.memset(ones_k[:], 1.0)
            v.memset(ones_kf[:], 1.0)
            v.memset(ones_r[:], 1.0).then_inc(s_vc, 1)
            v.wait_ge(ld0, 16)
            for h in range(4):
                for qc in range(NQC):
                    gqc = h * NQC + qc
                    nkt = 4 * qc + 4
                    for kt in range(nkt):
                        ai = AI_BASE[(h, qc)] + kt
                        sp_ai = ai + gqc
                        diag = kt >= 4 * qc
                        v.wait_ge(s_p, ai + 1)
                        if diag:
                            di = DI_OF[ai]
                            r = kt - 4 * qc
                            mc = 4 if (qc == 0 and kt == 0) else r
                            if di >= 4:
                                v.wait_ge(s_av, DIAG_AI[di - 4] + 1)
                            v.tensor_mul(pM[:, (di % 4) * 512:(di % 4 + 1) * 512],
                                         pU[:, (ai % 2) * 512:(ai % 2 + 1) * 512],
                                         msk[:, mc * 512:(mc + 1) * 512]).then_inc(s_pm, 1)
                            v.drain()
                            src_ap = pM[:, (di % 4) * 512:(di % 4 + 1) * 512]
                        else:
                            src_ap = pU[:, (ai % 2) * 512:(ai % 2 + 1) * 512]
                        half = slice((kt % 2) * 512, (kt % 2) * 512 + 512)
                        if kt < 2:
                            if gqc >= 1 and kt == 0:
                                v.wait_ge(s_s, AI_BASE[(h, qc)] + gqc)
                            v.tensor_copy(vsum[:, half], src_ap).then_inc(s_vv, 1)
                        else:
                            v.wait_ge(s_vv, AI_BASE[(h, qc)] + kt - 1)
                            v.tensor_add(vsum[:, half], vsum[:, half],
                                         src_ap).then_inc(s_vv, 1)
                        if kt == nkt - 1:
                            v.drain()
                            v.tensor_add(vsum[:, 0:512], vsum[:, 0:512],
                                         vsum[:, 512:1024])
                            v.drain()
                            v.engine_nop().then_inc(s_vs, 1)
                    v.wait_ge(s_s, AI_BASE[(h, qc)] + nkt + gqc + 1)
                    v.drain()
                    v.tensor_copy(rs_sb[:], p_rs[:])
                    v.drain()
                    v.reciprocal(rc_sb[:], rs_sb[:]).then_inc(s_rc, 1)
                    v.wait_ge(s_rp, gqc + 1)
                    v.drain()
                    v.tensor_copy(rep_sb[:], p_rep[:])
                    ai_last = AI_BASE[(h, qc)] + nkt - 1
                    v.wait_ge(s_av, ai_last + 1)
                    if gqc >= 2:
                        v.wait_ge(stpD[gqc % 2], 16 * (gqc // 2))
                    v.drain()
                    v.tensor_mul(Ob[:, (gqc % 2) * 512:(gqc % 2 + 1) * 512],
                                 p_o[:], rep_sb[:]).then_inc(s_oe, 1)


AI_BASE = {}
DI_OF = {}
def _init_attn_sched():
    ai = 0
    di = 0
    for h in range(4):
        for qc in range(16):
            AI_BASE[(h, qc)] = ai
            for kt in range(4 * qc + 4):
                if kt >= 4 * qc:
                    DI_OF[ai] = di
                    di += 1
                ai += 1
_init_attn_sched()




def build_F():
    nc = bass.Bass("TRN2", target_bir_lowering=False)
    OTs = nc.dram_tensor("OTs", [DM, TSL], BF16, kind="ExternalInput").ap()
    wguT = nc.dram_tensor("wguT", [DM, 2 * FI], BF16, kind="ExternalInput").ap()
    wdnT = nc.dram_tensor("wdnT", [FI, DM], BF16, kind="ExternalInput").ap()
    io = dict(
        z2s=nc.dram_tensor("z2s", [TSL, DM], F32, kind="ExternalInput").ap(),
        wout=nc.dram_tensor("wout", [DM, DM], F32, kind="ExternalInput").ap(),
        boutr=nc.dram_tensor("boutr", [128, DM], F32, kind="ExternalInput").ap(),
        idt=nc.dram_tensor("idt", [128, 128], BF16, kind="ExternalInput").ap(),
        outp=nc.dram_tensor("outp", [TSL, DM], F32, kind="ExternalOutput").ap(),
        x3f=nc.dram_tensor("x3f", [TSL, DM], F32, kind="Internal").ap(),
        x3Tf=nc.dram_tensor("x3Tf", [DM, TSL], BF16, kind="Internal").ap(),
        ot_tile=lambda eng, tt: OTs.rearrange("(k p) t -> p k t", p=128)[
            :, :, tt * 128:(tt + 1) * 128],
        wg_tile=lambda pc: wguT.rearrange("(k p) o -> p k o", p=128)[
            :, :, pc * 128:(pc + 1) * 128],
        wu_tile=lambda pc: wguT.rearrange("(k p) o -> p k o", p=128)[
            :, :, FI + pc * 128:FI + (pc + 1) * 128],
        wd_tile=lambda ic, dc: wdnT[ic * 128:(ic + 1) * 128,
                                    dc * 512:(dc + 1) * 512].rearrange(
                                        "p (r d) -> p r d", r=2))
    with ExitStack() as ctx:
        phase_F(nc, NcProxy(nc, "F_"), ctx.enter_context, io)
    return nc


def phase_F(nc, ncx, e, io):
    cut = io.get("cut")
    z2s = io["z2s"]; wout = io["wout"]; boutr = io["boutr"]; idt = io["idt"]
    outp = io["outp"]; x3f = io["x3f"]; x3Tf = io["x3Tf"]
    ot_tile = io["ot_tile"]; wg_tile = io["wg_tile"]; wu_tile = io["wu_tile"]
    wd_tile = io["wd_tile"]

    NT = TSL // 128
    NK = DM // 128
    NPC = FI // 128
    INV = 1.0 / DM
    import os as _osf
    NWN = int(_osf.environ.get("MEGA_FNWN", "4"))
    WL = [int(s) for s in _osf.environ.get(
        "MEGA_FWL", ",".join(str(i) for i in range(NWN))).split(",")]

    if True:
        big = e(ncx.sbuf_tensor("big", [128, NK * DM], BF16))
        x3w = e(ncx.sbuf_tensor("x3w", [128, NK * 512], BF16))
        x3ev = e(ncx.sbuf_tensor("x3ev", [128, NK * 128], BF16))
        otile = e(ncx.sbuf_tensor("otile", [128, NK * 128], BF16))
        z2t = e(ncx.sbuf_tensor("z2t", [128, DM], F32))
        xrd = e(ncx.sbuf_tensor("xrd", [128, DM], F32))
        x3b = e(ncx.sbuf_tensor("x3b", [128, DM], BF16))
        accs = [e(ncx.sbuf_tensor(f"acc{i}", [128, DM], F32)) for i in range(4)]
        outt = e(ncx.sbuf_tensor("outt", [128, 2 * DM], F32))
        wg_sb = e(ncx.sbuf_tensor("wg_sb", [128, 4 * NK * 128], BF16))
        wu_sb = e(ncx.sbuf_tensor("wu_sb", [128, 4 * NK * 128], BF16))
        wd_sb = e(ncx.sbuf_tensor("wd_sb", [128, 8 * 512], BF16))
        sg = e(ncx.sbuf_tensor("sg", [128, 512], BF16))
        mcol = e(ncx.sbuf_tensor("mcol", [128, 4], F32))
        eps5 = e(ncx.sbuf_tensor("eps5", [128, 1], F32))
        eps6 = e(ncx.sbuf_tensor("eps6", [128, 1], F32))
        idt_sb = e(ncx.sbuf_tensor("idt_sb", [128, 128], BF16))
        p_op = e(ncx.psum_tensor("p_op", [128, 512], F32))
        p_tr = e(ncx.psum_tensor("p_tr", [128, 128], BF16))
        p_g = e(ncx.psum_tensor("p_g", [128, 512], F32))
        p_u = e(ncx.psum_tensor("p_u", [128, 512], F32))
        p_d = [e(ncx.psum_tensor(f"p_d{i}", [128, 512], F32)) for i in range(4)]
        ld0 = e(ncx.semaphore("ld0"))
        ldwt = e(ncx.semaphore("ldwt"))
        ld1 = e(ncx.semaphore("ld1"))
        ldgp = [e(ncx.semaphore("ldgA")), e(ncx.semaphore("ldgB")),
                e(ncx.semaphore("ldgC")), e(ncx.semaphore("ldgD"))]
        lddp = [e(ncx.semaphore("ldd%d" % qq)) for qq in range(8)]
        ldx = e(ncx.semaphore("ldx"))
        s_op = e(ncx.semaphore("s_op"))
        s_ev1 = e(ncx.semaphore("s_ev1"))
        s_e1 = e(ncx.semaphore("s_e1"))
        s_xm = e(ncx.semaphore("s_xm"))
        s_sd = e(ncx.semaphore("s_sd"))
        s_tr = e(ncx.semaphore("s_tr"))
        s_te = e(ncx.semaphore("s_te"))
        s_gu = e(ncx.semaphore("s_gu"))
        s_sg = e(ncx.semaphore("s_sg"))
        s_hb = e(ncx.semaphore("s_hb"))
        s_dc = e(ncx.semaphore("s_dc"))
        s_ev2 = e(ncx.semaphore("s_ev2"))
        s_rm = e(ncx.semaphore("s_rm"))
        s_fin = e(ncx.semaphore("s_fin"))
        st1 = e(ncx.semaphore("st1"))
        stx = e(ncx.semaphore("stx"))
        ldxw = e(ncx.semaphore("ldxw"))
        st2p = [e(ncx.semaphore("st2A")), e(ncx.semaphore("st2B"))]
        blk = e(nc.Block())

        @blk.sync
        def _(sy):
            sy.dma_start(out=idt_sb[:], in_=idt[:]).then_inc(ld0, 16)
            sy.dma_start(out=xrd[:], in_=boutr[:]).then_inc(ld0, 16)
            if cut == "ld":
                sy.wait_ge(ld0, 32)
                return
            if cut == "ld2":
                for tt in range(NT):
                    sy.dma_start(out=otile[:].rearrange("p (k t) -> p k t", k=NK),
                                 in_=ot_tile(sy, tt)).then_inc(ld1, 16)
                    sy.dma_start(out=z2t[:],
                                 in_=z2s[tt * 128:(tt + 1) * 128, :]).then_inc(ld1, 16)
                sy.wait_ge(ld1, 32 * NT)
                return
            for tt in range(NT):
                if tt >= 1:
                    sy.wait_ge(s_e1, tt)
                sy.dma_start(out=otile[:].rearrange("p (k t) -> p k t", k=NK),
                             in_=ot_tile(sy, tt)).then_inc(ld1, 16)
                sy.dma_start(out=z2t[:],
                             in_=z2s[tt * 128:(tt + 1) * 128, :]).then_inc(ld1, 16)
                sy.wait_ge(s_e1, tt + 1)
                if cut in ("op", "mm"):
                    continue
                sy.dma_start(out=x3f[tt * 128:(tt + 1) * 128, :],
                             in_=accs[0][:]).then_inc(st1, 16)
                sy.wait_ge(s_te, NK * (tt + 1))
                sy.dma_start(out=x3Tf.rearrange("(k p) t -> p k t", p=128)[
                                 :, :, tt * 128:(tt + 1) * 128],
                             in_=x3ev[:].rearrange("p (k t) -> p k t", k=NK)
                             ).then_inc(stx, 16)
            if cut in ("op", "mm"):
                return
            sy.wait_ge(st1, 16 * NT)
            sy.wait_ge(stx, 16 * NT)
            if cut == "p1":
                return
            li = 0
            for idx, wn in enumerate(WL):
                if idx >= 1:
                    sy.wait_ge(s_gu, idx * NPC)
                sy.dma_start(out=x3w[:].rearrange("p (k t) -> p k t", k=NK),
                             in_=x3Tf.rearrange("(k p) t -> p k t", p=128)[
                                 :, :, wn * 512:(wn + 1) * 512]).then_inc(ldxw, 16)
                for ttl in range(4):
                    if li >= 1:
                        sy.wait_ge(s_xm, NT + li)
                    sy.dma_start(out=xrd[:],
                                 in_=x3f[wn * 512 + ttl * 128:wn * 512 + (ttl + 1) * 128, :]
                                 ).then_inc(ldx, 16)
                    sy.wait_ge(s_fin, li + 1)
                    sy.dma_start(out=outp[wn * 512 + ttl * 128:wn * 512 + (ttl + 1) * 128, :],
                                 in_=outt[:, (li % 2) * DM:(li % 2 + 1) * DM]
                                 ).then_inc(st2p[li % 2], 16)
                    li += 1
            sy.wait_ge(st2p[0], 16 * len(WL) * 2)
            sy.wait_ge(st2p[1], 16 * len(WL) * 2)

        @blk.gpsimd
        def _(g):
            g.dma_start(out=big[:].rearrange("p (k d) -> p k d", k=NK),
                        in_=wout.rearrange("(k p) d -> p k d", p=128)).then_inc(ldwt, 16)
            if cut in ("ld", "ld2"):
                g.wait_ge(ldwt, 16)
                return
            if cut in ("p1", "op", "mm"):
                return
            for idx, wn in enumerate(WL):
                for pc in range(NPC):
                    gi = idx * NPC + pc
                    if gi >= 4:
                        g.wait_ge(s_gu, gi - 3)
                    bb = (gi % 4) * NK * 128
                    g.dma_start(out=wg_sb[:, bb:bb + NK * 128].rearrange(
                                    "p (k j) -> p k j", k=NK),
                                in_=wg_tile(pc)).then_inc(ldgp[gi % 4], 16)
                    g.dma_start(out=wu_sb[:, bb:bb + NK * 128].rearrange(
                                    "p (k j) -> p k j", k=NK),
                                in_=wu_tile(pc)).then_inc(ldgp[gi % 4], 16)

        @blk.tensor
        def _(t):
            if cut in ("ld", "ld2"):
                return
            t.wait_ge(ld0, 32)
            t.wait_ge(ldwt, 16)
            n = 0
            for tt in range(NT):
                t.wait_ge(ld1, 32 * (tt + 1))
                for dc in range(4):
                    if tt + dc >= 1:
                        t.wait_ge(s_ev1, 4 * tt + dc)
                    for k in range(NK):
                        mm = t.matmul(p_op[:],
                                      otile[:, k * 128:(k + 1) * 128],
                                      big[:, k * DM + dc * 512:k * DM + dc * 512 + 512],
                                      start=(k == 0), stop=(k == NK - 1))
                    mm.then_inc(s_op, 1)
                if cut in ("op", "mm"):
                    continue
                t.wait_ge(s_e1, tt + 1)
                for k in range(NK):
                    if n >= 1:
                        t.wait_ge(s_te, n)
                    mm = t.transpose(p_tr[:], x3b[:, k * 128:(k + 1) * 128],
                                     idt_sb[:])
                    mm.then_inc(s_tr, 1)
                    n += 1
            if cut in ("p1", "op", "mm"):
                return
            for idx, wn in enumerate(WL):
                t.wait_ge(ldxw, 16 * (idx + 1))
                for pc in range(NPC):
                    gi = idx * NPC + pc
                    t.wait_ge(ldgp[gi % 4], 32 * (gi // 4 + 1))
                    if gi >= 1:
                        t.wait_ge(s_hb, gi)
                    bb = (gi % 4) * NK * 128
                    for k in range(NK):
                        t.matmul(p_g[:], wg_sb[:, bb + k * 128:bb + (k + 1) * 128],
                                 x3w[:, k * 512:(k + 1) * 512],
                                 start=(k == 0), stop=(k == NK - 1))
                    for k in range(NK):
                        mm = t.matmul(p_u[:], wu_sb[:, bb + k * 128:bb + (k + 1) * 128],
                                      x3w[:, k * 512:(k + 1) * 512],
                                      start=(k == 0), stop=(k == NK - 1))
                    mm.then_inc(s_gu, 1)
                for dc in range(4):
                    wdi = idx * 4 + dc
                    if wdi >= 1:
                        t.wait_ge(s_ev2, wdi)
                    for ic in range(NPC):
                        widx = wdi * NPC + ic
                        t.wait_ge(lddp[widx % 8], 16 * (widx // 8 + 1))
                        t.wait_ge(s_hb, idx * NPC + ic + 1)
                        wb = (widx % 8) * 512
                        for ttl in range(4):
                            mm = t.matmul(p_d[ttl][:],
                                          big[:, ic * 512 + ttl * 128:ic * 512 + (ttl + 1) * 128],
                                          wd_sb[:, wb:wb + 512],
                                          start=(ic == 0), stop=(ic == NPC - 1))
                        mm.then_inc(s_dc, 1)

        @blk.scalar
        def _(a):
            if cut in ("ld", "ld2", "op", "mm"):
                return
            n = 0
            for tt in range(NT):
                a.wait_ge(s_xm, tt + 1)
                if tt >= 1:
                    a.wait_ge(st1, 16 * tt)
                a.drain()
                a.activation(z2t[:], accs[0][:], AF.Square, accum_out=mcol[:, 3:4])
                a.drain()
                a.activation(mcol[:, 1:2], mcol[:, 3:4], AF.Sqrt, scale=INV,
                             bias=eps5[:]).then_inc(s_sd, 1)
                if tt >= 1:
                    a.wait_ge(stx, 16 * tt)
                for k in range(NK):
                    a.wait_ge(s_tr, n + 1)
                    a.drain()
                    a.activation(x3ev[:, k * 128:(k + 1) * 128],
                                 p_tr[:], AF.Identity).then_inc(s_te, 1)
                    n += 1
            if cut == "p1":
                return
            for idx, wn in enumerate(WL):
                for pc in range(NPC):
                    gi = idx * NPC + pc
                    a.wait_ge(s_gu, gi + 1)
                    if gi >= 1:
                        a.wait_ge(s_hb, gi)
                    a.drain()
                    a.activation(sg[:], p_g[:], AF.Silu).then_inc(s_sg, 1)
                for dc in range(4):
                    for ic in range(NPC):
                        widx = (idx * 4 + dc) * NPC + ic
                        if widx >= 8:
                            a.wait_ge(s_dc, widx - 7)
                        wb = (widx % 8) * 512
                        a.dma_start(out=wd_sb[:, wb:wb + 512].rearrange(
                                        "p (r d) -> p r d", r=2),
                                    in_=wd_tile(ic, dc)).then_inc(lddp[widx % 8], 16)
                for ttl in range(4):
                    li = idx * 4 + ttl
                    a.wait_ge(s_ev2, 4 * (idx + 1))
                    a.wait_ge(s_fin, li)
                    a.drain()
                    a.activation(z2t[:], accs[ttl][:], AF.Square,
                                 accum_out=mcol[:, 3:4])
                    a.drain()
                    a.activation(mcol[:, 1:2], mcol[:, 3:4], AF.Sqrt, scale=1.0 / DM,
                                 bias=eps6[:]).then_inc(s_rm, 1)
                    a.wait_ge(s_xm, NT + li + 1)
                    a.drain()
                    a.activation(z2t[:], accs[ttl][:], AF.Square,
                                 accum_out=mcol[:, 3:4])
                    a.drain()
                    a.activation(mcol[:, 1:2], mcol[:, 3:4], AF.Sqrt, scale=INV,
                                 bias=eps5[:]).then_inc(s_sd, 1)

        @blk.vector
        def _(v):
            if cut in ("ld", "ld2"):
                return
            v.memset(eps5[:], 1e-5)
            v.memset(eps6[:], 1e-6)
            for tt in range(NT):
                acc = accs[0]
                if tt >= 1 and cut not in ("op", "mm"):
                    v.wait_ge(st1, 16 * tt)
                for dc in range(4):
                    v.wait_ge(s_op, 4 * tt + dc + 1)
                    v.drain()
                    if cut == "mm":
                        v.engine_nop().then_inc(s_ev1, 1)
                        continue
                    v.tensor_add(acc[:, dc * 512:(dc + 1) * 512], p_op[:],
                                 xrd[:, dc * 512:(dc + 1) * 512]).then_inc(s_ev1, 1)
                v.wait_ge(ld1, 32 * (tt + 1))
                v.drain()
                if cut == "mm":
                    v.engine_nop().then_inc(s_e1, 1)
                    continue
                v.tensor_add(acc[:], acc[:], z2t[:])
                v.drain()
                if cut == "op":
                    v.engine_nop().then_inc(s_e1, 1)
                    continue
                v.tensor_reduce(mcol[:, 0:1], acc[:], AX.X, ALU.add)
                v.drain()
                v.tensor_scalar_mul(mcol[:, 0:1], mcol[:, 0:1], INV)
                v.drain()
                v.tensor_scalar(acc[:], acc[:], mcol[:, 0:1], None,
                                ALU.subtract).then_inc(s_xm, 1)
                v.wait_ge(s_sd, tt + 1)
                v.reciprocal(mcol[:, 2:3], mcol[:, 1:2])
                if tt >= 1:
                    v.wait_ge(s_tr, NK * tt)
                v.drain()
                v.tensor_scalar_mul(acc[:], acc[:], mcol[:, 2:3])
                v.drain()
                v.tensor_copy(x3b[:], acc[:]).then_inc(s_e1, 1)
            if cut in ("p1", "op", "mm"):
                return
            for idx, wn in enumerate(WL):
                for pc in range(NPC):
                    gi = idx * NPC + pc
                    v.wait_ge(s_sg, gi + 1)
                    if idx >= 1 and pc == 0:
                        v.wait_ge(s_dc, idx * 4 * NPC)
                    v.drain()
                    v.tensor_mul(big[:, pc * 512:(pc + 1) * 512], sg[:], p_u[:]
                                 ).then_inc(s_hb, 1)
                for dc in range(4):
                    wdi = idx * 4 + dc
                    v.wait_ge(s_dc, (wdi + 1) * NPC)
                    v.drain()
                    for ttl in range(4):
                        la = v.tensor_copy(accs[ttl][:, dc * 512:(dc + 1) * 512],
                                           p_d[ttl][:])
                    la.then_inc(s_ev2, 1)
                    v.drain()
                for ttl in range(4):
                    li = idx * 4 + ttl
                    acc = accs[ttl]
                    v.wait_ge(s_rm, li + 1)
                    v.reciprocal(mcol[:, 2:3], mcol[:, 1:2])
                    v.drain()
                    v.tensor_scalar_mul(acc[:], acc[:], mcol[:, 2:3])
                    v.wait_ge(ldx, 16 * (li + 1))
                    v.drain()
                    v.tensor_add(acc[:], acc[:], xrd[:])
                    v.drain()
                    v.tensor_reduce(mcol[:, 0:1], acc[:], AX.X, ALU.add)
                    v.drain()
                    v.tensor_scalar_mul(mcol[:, 0:1], mcol[:, 0:1], INV)
                    v.drain()
                    v.tensor_scalar(acc[:], acc[:], mcol[:, 0:1], None,
                                    ALU.subtract).then_inc(s_xm, 1)
                    v.wait_ge(s_sd, NT + li + 1)
                    v.reciprocal(mcol[:, 2:3], mcol[:, 1:2])
                    if li >= 2:
                        v.wait_ge(st2p[li % 2], 16 * (li // 2))
                    v.drain()
                    v.tensor_scalar_mul(outt[:, (li % 2) * DM:(li % 2 + 1) * DM],
                                        acc[:], mcol[:, 2:3]).then_inc(s_fin, 1)


def phase_Ap(nc, ncx, e, io):
    xTf = io["xTf"]; wxT = io["wxT"]; wdtTs = io["wdtTs"]; bdt2 = io["bdt2"]
    dtTs = io["dtTs"]; BTCT = io["BTCT"]

    NK = DM // 128
    NTC = TOK // 512

    if True:
        xb = e(ncx.sbuf_tensor("xb", [128, 2 * NK * 512], BF16))
        wx_sb = e(ncx.sbuf_tensor("wx_sb", [128, NK * 96], BF16))
        wdt_sb = e(ncx.sbuf_tensor("wdt_sb", [64, 256], BF16))
        bdt_sb = e(ncx.sbuf_tensor("bdt_sb", [128, 2], F32))
        bc_sb = e(ncx.sbuf_tensor("bc_sb", [32, TOK], F32))
        dtx_b = e(ncx.sbuf_tensor("dtx_b", [64, 2 * 512], BF16))
        t_z = e(ncx.sbuf_tensor("t_z", [128, 2 * 512], F32))
        t_a = e(ncx.sbuf_tensor("t_a", [128, 512], F32))
        t_e = e(ncx.sbuf_tensor("t_e", [128, 512], F32))
        t_l = e(ncx.sbuf_tensor("t_l", [128, 2 * 512], F32))
        t_r = e(ncx.sbuf_tensor("t_r", [128, 512], F32))
        dt_out = e(ncx.sbuf_tensor("dt_out", [128, 2 * 512], F32))
        one_c = e(ncx.sbuf_tensor("one_c", [128, 1], F32))
        p_xp = e(ncx.psum_tensor("p_xp", [96, 2 * 512], F32))
        p_dt = e(ncx.psum_tensor("p_dt", [128, 2 * 512], F32))
        ld = e(ncx.semaphore("ld"))
        lds = e(ncx.semaphore("lds"))
        ldxp = [e(ncx.semaphore("ldx0")), e(ncx.semaphore("ldx1"))]
        s_xp = e(ncx.semaphore("s_xp"))
        s_dt = e(ncx.semaphore("s_dt"))
        s_dtx = e(ncx.semaphore("s_dtx"))
        s_spZ = e(ncx.semaphore("s_spZ"))
        s_spL = e(ncx.semaphore("s_spL"))
        s_vr = e(ncx.semaphore("s_vr"))
        s_out = e(ncx.semaphore("s_out"))
        stp = [e(ncx.semaphore("st0")), e(ncx.semaphore("st1"))]
        stbc = e(ncx.semaphore("stbc"))
        blk = e(nc.Block())

        @blk.gpsimd
        def _(g):
            g.dma_start(out=wx_sb[:].rearrange("p (k j) -> p k j", k=NK),
                        in_=wxT.rearrange("(k p) j -> p k j", p=128)).then_inc(ld, 16)
            g.dma_start(out=wdt_sb[:], in_=wdtTs[:]).then_inc(ld, 16)
            for tc in range(NTC):
                if tc >= 2:
                    g.wait_ge(s_xp, tc - 1)
                b = (tc % 2) * NK * 512
                for k in range(NK):
                    g.dma_start(out=xb[:, b + k * 512:b + (k + 1) * 512],
                                in_=xTf[k * 128:(k + 1) * 128,
                                        tc * 512:(tc + 1) * 512]).then_inc(ldxp[tc % 2], 16)

        @blk.sync
        def _(sy):
            sy.dma_start(out=bdt_sb[:], in_=bdt2[:]).then_inc(lds, 16)
            gg = 0
            for tc in range(NTC):
                for gr in range(2):
                    sy.wait_ge(s_out, gg + 1)
                    sy.dma_start(out=dtTs[gr * 128:(gr + 1) * 128,
                                          tc * 512:(tc + 1) * 512],
                                 in_=dt_out[:, (gg % 2) * 512:(gg % 2 + 1) * 512]
                                 ).then_inc(stp[gg % 2], 16)
                    gg += 1
            sy.wait_ge(s_dtx, NTC)
            sy.dma_start(out=BTCT[:], in_=bc_sb[:]).then_inc(stbc, 16)
            sy.wait_ge(stp[0], 16 * NTC)
            sy.wait_ge(stp[1], 16 * NTC)
            sy.wait_ge(stbc, 16)

        @blk.tensor
        def _(t):
            t.wait_ge(ld, 32)
            gg = 0
            for tc in range(NTC):
                b = (tc % 2) * NK * 512
                pb = (tc % 2) * 512
                t.wait_ge(ldxp[tc % 2], 256 * (tc // 2 + 1))
                if tc >= 2:
                    t.wait_ge(s_dtx, tc - 1)
                for k in range(NK):
                    mm = t.matmul(p_xp[:, pb:pb + 512],
                                  wx_sb[:, k * 96:(k + 1) * 96],
                                  xb[:, b + k * 512:b + (k + 1) * 512],
                                  start=(k == 0), stop=(k == NK - 1))
                mm.then_inc(s_xp, 1)
                t.wait_ge(s_dtx, tc + 1)
                for gr in range(2):
                    if gg >= 2:
                        t.wait_ge(s_spZ, gg - 1)
                    t.matmul(p_dt[:, (gg % 2) * 512:(gg % 2 + 1) * 512],
                             wdt_sb[:, gr * 128:(gr + 1) * 128],
                             dtx_b[:, (tc % 2) * 512:(tc % 2 + 1) * 512],
                             start=True, stop=True).then_inc(s_dt, 1)
                    gg += 1

        @blk.scalar
        def _(a):
            a.wait_ge(ld, 32)
            a.wait_ge(lds, 16)
            gg = 0
            for tc in range(NTC):
                for gr in range(2):
                    zsl = slice((gg % 2) * 512, (gg % 2 + 1) * 512)
                    a.wait_ge(s_dt, gg + 1)
                    if gg >= 2:
                        a.wait_ge(s_vr, gg - 1)
                    a.activation(t_z[:, zsl], p_dt[:, zsl], AF.Identity,
                                 bias=bdt_sb[:, gr:gr + 1]).then_inc(s_spZ, 1)
                    a.drain()
                    a.activation(t_a[:], t_z[:, zsl], AF.Abs)
                    a.drain()
                    a.activation(t_e[:], t_a[:], AF.Exp, scale=-1.0)
                    a.drain()
                    a.activation(t_l[:, zsl], t_e[:], AF.Ln,
                                 bias=one_c[:]).then_inc(s_spL, 1)
                    gg += 1

        @blk.vector
        def _(v):
            v.memset(one_c[:], 1.0)
            gg = 0
            for tc in range(NTC):
                pb = (tc % 2) * 512
                v.wait_ge(s_xp, tc + 1)
                v.tensor_copy(bc_sb[:, tc * 512:(tc + 1) * 512], p_xp[64:96, pb:pb + 512])
                v.tensor_copy(dtx_b[:, pb:pb + 512],
                              p_xp[0:64, pb:pb + 512]).then_inc(s_dtx, 1)
                for gr in range(2):
                    zsl = slice((gg % 2) * 512, (gg % 2 + 1) * 512)
                    v.wait_ge(s_spZ, gg + 1)
                    v.tensor_scalar(t_r[:], t_z[:, zsl], 0.0, None,
                                    ALU.max).then_inc(s_vr, 1)
                    v.wait_ge(s_spL, gg + 1)
                    if gg >= 2:
                        v.wait_ge(stp[gg % 2], 16 * (gg // 2))
                    v.drain()
                    v.tensor_add(dt_out[:, zsl], t_r[:], t_l[:, zsl]).then_inc(s_out, 1)
                    gg += 1


def phase_Z(nc, ncx, e, io):
    src_f = io["src"]; dst_T = io["dst"]; idt = io["idt"]
    NT = TSL // 128
    NK = DM // 128
    if True:
        xt = e(ncx.sbuf_tensor("xt", [128, 2 * DM], BF16))
        ev = e(ncx.sbuf_tensor("ev", [128, 2 * NK * 128], BF16))
        idt_sb = e(ncx.sbuf_tensor("idt_sb", [128, 128], BF16))
        p_t = e(ncx.psum_tensor("p_t", [128, 128], BF16))
        ld = e(ncx.semaphore("ld"))
        ldi = e(ncx.semaphore("ldi"))
        s_tr = e(ncx.semaphore("s_tr"))
        s_te = e(ncx.semaphore("s_te"))
        st = e(ncx.semaphore("st"))
        blk = e(nc.Block())

        @blk.gpsimd
        def _(g):
            g.dma_start(out=idt_sb[:], in_=idt[:]).then_inc(ldi, 16)
            for tt in range(NT):
                if tt >= 2:
                    g.wait_ge(s_tr, NK * (tt - 1))
                g.dma_start(out=xt[:, (tt % 2) * DM:(tt % 2 + 1) * DM],
                            in_=src_f[tt * 128:(tt + 1) * 128, :]).then_inc(ld, 16)

        @blk.tensor
        def _(t):
            t.wait_ge(ldi, 16)
            n = 0
            for tt in range(NT):
                t.wait_ge(ld, 16 * (tt + 1))
                for k in range(NK):
                    if n >= 1:
                        t.wait_ge(s_te, n)
                    t.transpose(p_t[:],
                                xt[:, (tt % 2) * DM + k * 128:(tt % 2) * DM + (k + 1) * 128],
                                idt_sb[:]).then_inc(s_tr, 1)
                    n += 1

        @blk.scalar
        def _(a):
            n = 0
            for tt in range(NT):
                if tt >= 2:
                    a.wait_ge(st, 16 * (tt - 1))
                for k in range(NK):
                    a.wait_ge(s_tr, n + 1)
                    a.drain()
                    a.activation(ev[:, (tt % 2) * NK * 128 + k * 128:
                                    (tt % 2) * NK * 128 + (k + 1) * 128],
                                 p_t[:], AF.Identity).then_inc(s_te, 1)
                    n += 1

        @blk.sync
        def _(sy):
            for tt in range(NT):
                sy.wait_ge(s_te, NK * (tt + 1))
                sy.dma_start(out=dst_T.rearrange("(k p) t -> p k t", p=128)[
                                 :, :, tt * 128:(tt + 1) * 128],
                             in_=ev[:, (tt % 2) * NK * 128:(tt % 2 + 1) * NK * 128]
                             .rearrange("p (k t) -> p k t", k=NK)).then_inc(st, 16)
            sy.wait_ge(st, 16 * NT)


def build_mega(stop_after=None):
    def _stopper(nc, outp, xa):
        with ExitStack() as c2:
            ee = c2.enter_context
            sbq = ee(nc.sbuf_tensor("stop_sb", [128, DM], F32))
            ldq = ee(nc.semaphore("stop_ld"))
            stq = ee(nc.semaphore("stop_st"))
            bq = ee(nc.Block())
            @bq.sync
            def _(sy):
                sy.dma_start(out=sbq[:], in_=xa[0:128, :]).then_inc(ldq, 16)
                sy.wait_ge(ldq, 16)
                sy.dma_start(out=outp[0:128, :], in_=sbq[:]).then_inc(stq, 16)
                sy.wait_ge(stq, 16)
    nc = bass.Bass("TRN2", target_bir_lowering=False)
    F16 = mybir.dt.float16
    dram = nc.dram_tensor

    xTf = dram("xTf", [DM, TOK], F32, kind="ExternalInput").ap()
    wxT = dram("wxT", [DM, 96], F32, kind="ExternalInput").ap()
    wdtTs = dram("wdtTs", [DTR, 256], F32, kind="ExternalInput").ap()
    bdt2 = dram("bdt2", [128, 2], F32, kind="ExternalInput").ap()
    xTsh = dram("xTsh", [256, TOK], F32, kind="ExternalInput").ap()
    Ash = dram("Ash", [128, 32], F32, kind="ExternalInput").ap()
    Dsh = dram("Dsh", [128, 2], F32, kind="ExternalInput").ap()
    w2T = dram("w2T", [FF0, 256], F32, kind="ExternalInput").ap()
    h128 = dram("h128", [128, 128], BF16, kind="ExternalInput").ap()
    idt = dram("idt", [128, 128], BF16, kind="ExternalInput").ap()
    gpk = dram("gpk", [1408, 256], I32, kind="ExternalInput").ap()
    gsc = dram("gsc", [128, 11 * 16], F16, kind="ExternalInput").ap()
    dpk = dram("dpk", [256, 704], I32, kind="ExternalInput").ap()
    dsc = dram("dsc", [128, 2 * 44], F16, kind="ExternalInput").ap()
    xa = dram("xa", [TSL, DM], F32, kind="ExternalInput").ap()
    w1T = dram("w1T", [DM, FF0], F32, kind="ExternalInput").ap()
    w3T = dram("w3T", [DM, FF0], F32, kind="ExternalInput").ap()
    wq = dram("wq", [DM, 512], F32, kind="ExternalInput").ap()
    wk = dram("wk", [DM, 512], F32, kind="ExternalInput").ap()
    wv = dram("wv", [DM, 512], F32, kind="ExternalInput").ap()
    masks = dram("masks", [128, 5 * 512], BF16, kind="ExternalInput").ap()
    wout = dram("wout", [DM, DM], F32, kind="ExternalInput").ap()
    boutr = dram("boutr", [128, DM], F32, kind="ExternalInput").ap()
    offs = dram("offs", [1, 34], I32, kind="ExternalInput").ap()
    outp = dram("outp", [TSL, DM], F32, kind="ExternalOutput").ap()

    dtTs = dram("dtTs", [256, TOK], F32).ap()
    BTCT = dram("BTCT", [32, TOK], F32).ap()
    wp2in = dram("wp2in", [FF0, 256], BF16)
    wpgin = dram("wpgin", [DM, 1408], BF16)
    wpdin = dram("wpdin", [FI, 256], BF16)
    wp2g = dram("wp2g", [8 * FF0, 256], BF16)
    wpgg = dram("wpgg", [8 * DM, 1408], BF16)
    wpdg = dram("wpdg", [8 * FI, 256], BF16)
    ylocal = dram("ylocal", [TOK, 256], F32)
    ygath = dram("ygath", [8 * TOK, 256], F32)
    ylo = dram("ylo", [TSL, DM], F32).ap()
    OTl = dram("OTl", [DM, TSL], BF16).ap()
    z1f = dram("z1f", [TSL, DM], F32).ap()
    hidT = dram("hidT", [FF0, TSL], BF16).ap()
    z2loc = dram("z2loc", [TSL, DM], F32).ap()
    z2tl = dram("z2tl", [DM, TSL], BF16)
    z2g = dram("z2g", [4 * DM, TSL], BF16)
    Oloc = dram("Oloc", [512, SEQ], BF16)
    Og = dram("Og", [4 * 512, SEQ], BF16)
    x3f = dram("x3f", [TSL, DM], F32).ap()
    x3Tf = dram("x3Tf", [DM, TSL], BF16).ap()

    import os as _os
    RG8 = [list(range(8))]
    RG44 = [[0, 1, 2, 3], [4, 5, 6, 7]]

    with ExitStack() as octx:
        oe = octx.enter_context
        offs_sb = oe(nc.sbuf_tensor("offs_sb", [1, 34], I32))
        s_ccw = oe(nc.semaphore("s_ccw"))
        s_ccy = oe(nc.semaphore("s_ccy"))
        s_ccz = oe(nc.semaphore("s_ccz"))
        s_cco = oe(nc.semaphore("s_cco"))
        s_yl = oe(nc.semaphore("s_yl"))
        s_ol = oe(nc.semaphore("s_ol"))
        ld0m = oe(nc.semaphore("ld0m"))
        barsems = [oe(nc.semaphore("barS%d" % i)) for i in range(20)]
        swsems = [oe(nc.semaphore("sw%d" % i)) for i in range(40)]
        sw_alloc = [swsems, 0]
        clear_from = max(h.num for h in swsems) + 1
        dma_pool = [oe(nc.semaphore("pd%d" % i)) for i in range(12)]
        cmp_pool = [oe(nc.semaphore("pc%d" % i)) for i in range(20)]
        nbar = [0]
        SWN = dict(
            E_={"ld", "ldwA", "ldwB"},
            A_={"ld", "ldx0", "ldx1"},
            B_={"ldt0", "ldt1", "ldb0", "ldb1"},
            C_={"ldwA", "ldwB"} | {"ldw2%d" % qq for qq in range(8)},
            Z_={"ld", "ldi"},
            D_={"ldz", "ldw"},
            F_={"ldwt", "ldgA", "ldgB", "ldgC", "ldgD"}
               | {"ldd%d" % qq for qq in range(8)},
        )

        def PX(pfx):
            return NcProxy(nc, pfx, dma_pool=dma_pool, cmp_pool=cmp_pool,
                           sw_names=SWN[pfx], sw_alloc=sw_alloc)

        def bar():
            sb = barsems[nbar[0]]
            nbar[0] += 1
            _barrier(nc, NcProxy(nc, "M_"), oe, nbar[0], sb=sb,
                     clear_from=clear_from)

        with nc.Block() as b0:
            @b0.sync
            def _(sy):
                sy.dma_start(out=offs_sb[:], in_=offs[:]).then_inc(ld0m, 16)
                sy.wait_ge(ld0m, 16)

        _fonly = bool(int(_os.environ.get("MEGA_FONLY", "0")))
        if _fonly:
            _fcut0 = _os.environ.get("MEGA_FCUT") or None
            with nc.Block() as binit:
                @binit.gpsimd
                def _(g):
                    g.dma_start(out=OTl, in_=xa.rearrange("a b -> b a") if False
                                else OTl, ).then_inc(ld0m, 16) if False else None
                    g.dma_start(out=OTl, in_=xa).then_inc(ld0m, 16)
                    g.wait_ge(ld0m, 32)
            bar()
            with ExitStack() as ctx:
                phase_F(nc, PX("F_"), ctx.enter_context, dict(
                    z2s=xa, wout=wout, boutr=boutr, idt=idt, outp=outp,
                    x3f=x3f, x3Tf=x3Tf,
                    ot_tile=lambda eng, tt: OTl.rearrange("(k p) t -> p k t", p=128)[
                        :, :, tt * 128:(tt + 1) * 128],
                    wg_tile=None, wu_tile=None, wd_tile=None, cut=_fcut0))
            if _fcut0:
                bar()
                _stopper(nc, outp, xa)
            return nc
        with ExitStack() as ctx:
            phase_E(nc, PX("E_"), ctx.enter_context, dict(
                w2T=w2T, h128=h128, idt=idt, gpk=gpk, gsc=gsc, dpk=dpk, dsc=dsc,
                w2pT=wp2in.ap(), wguT=wpgin.ap(), wdnT=wpdin.ap()))
        bar()
        if stop_after == "E":
            _stopper(nc, outp, xa)
            return nc
        with nc.Block() as bw:
            @bw.gpsimd
            def _(g):
                g.collective_compute("AllGather", ALU.bypass, replica_groups=RG8,
                                     ins=[wp2in.ap().opt()], outs=[wp2g.ap().opt()]
                                     ).then_inc(s_ccw, 1)
                g.collective_compute("AllGather", ALU.bypass, replica_groups=RG8,
                                     ins=[wpgin.ap().opt()], outs=[wpgg.ap().opt()]
                                     ).then_inc(s_ccw, 1)
                g.collective_compute("AllGather", ALU.bypass, replica_groups=RG8,
                                     ins=[wpdin.ap().opt()], outs=[wpdg.ap().opt()]
                                     ).then_inc(s_ccw, 1)

        with ExitStack() as ctx:
            phase_Ap(nc, PX("A_"), ctx.enter_context, dict(
                xTf=xTf, wxT=wxT, wdtTs=wdtTs, bdt2=bdt2, dtTs=dtTs, BTCT=BTCT))
        bar()
        if stop_after == "Ap":
            _stopper(nc, outp, xa)
            return nc

        _noytr = bool(int(_os.environ.get("MEGA_NOYTR", "0")))
        ysc = dram("yscdbg", [256, TOK], F32).ap() if _noytr else None
        with ExitStack() as ctx:
            phase_B(nc, PX("B_"), ctx.enter_context, dict(
                dtTs=dtTs, xTs=xTsh, Ash=Ash, Dsh=Dsh,
                BT=BTCT[0:16, :], CT=BTCT[16:32, :], idt=idt,
                yTs=ysc, ytr=None if _noytr else ylocal.ap()))
        bar()
        if stop_after == "B":
            _stopper(nc, outp, xa)
            return nc
        yvr = ygath.ap().rearrange("(r t) d -> r t d", r=8)
        with nc.Block() as by:
            @by.gpsimd
            def _(g):
                g.collective_compute("AllGather", ALU.bypass, replica_groups=RG8,
                                     ins=[ylocal.ap().opt()], outs=[ygath.ap().opt()]
                                     ).then_inc(s_ccy, 1)
                g.wait_ge(s_ccy, 1)
                g.wait_ge(s_ccw, 3)
                ry = g.value_load(offs_sb[0:1, 0:1])
                H = TSL // 2
                for r8 in range(8):
                    for hh in range(2):
                        g.dma_start(
                            out=ylo[hh * H:(hh + 1) * H, r8 * 256:(r8 + 1) * 256],
                            in_=yvr[r8, ds(ry + hh * H, H), :]).then_inc(s_yl, 16)
                g.wait_ge(s_yl, 16 * 16)
        bar()
        if stop_after == "AGy":
            _stopper(nc, outp, xa)
            return nc

        def ya_tile(eng, tt):
            return ylo[tt * 128:(tt + 1) * 128, :].rearrange(
                "t (r d) -> t r d", r=8)

        def w2p_tile(hc, dc):
            return wp2g.ap().rearrange("(r h) d -> h r d", r=8)[
                hc * 128:(hc + 1) * 128, 2 * dc:2 * dc + 2, :]

        with ExitStack() as ctx:
            phase_C(nc, PX("C_"), ctx.enter_context, dict(
                xa=xa, w1T=w1T, w3T=w3T, idt=idt, z2o=z2loc, z1f=z1f, hidT=hidT,
                ya_tile=ya_tile, w2p_tile=w2p_tile, z2tl=None,
                offs_sb=offs_sb))
        bar()
        with ExitStack() as ctx:
            phase_Z(nc, PX("Z_"), ctx.enter_context, dict(
                src=z2loc, dst=z2tl.ap(), idt=idt))
        bar()
        if stop_after == "C":
            _stopper(nc, outp, xa)
            return nc
        with nc.Block() as bz:
            @bz.gpsimd
            def _(g):
                g.collective_compute("AllGather", ALU.bypass, replica_groups=RG44,
                                     ins=[z2tl.ap().opt()], outs=[z2g.ap().opt()]
                                     ).then_inc(s_ccz, 1)
                g.wait_ge(s_ccz, 1)
        bar()

        def z2w_tile(k, w8):
            q, hh = w8 // 2, w8 % 2
            return z2g.ap()[q * DM + k * 128:q * DM + (k + 1) * 128,
                            hh * 1024:(hh + 1) * 1024]

        with ExitStack() as ctx:
            phase_D(nc, PX("D_"), ctx.enter_context, dict(
                wq=wq, wk=wk, wv=wv, masks=masks, Oc=Oloc.ap(),
                z2w_tile=z2w_tile))
        bar()
        if stop_after == "D":
            _stopper(nc, outp, xa)
            return nc
        with nc.Block() as bo:
            @bo.gpsimd
            def _(g):
                g.collective_compute("AllGather", ALU.bypass, replica_groups=RG44,
                                     ins=[Oloc.ap().opt()], outs=[Og.ap().opt()]
                                     ).then_inc(s_cco, 1)
                g.wait_ge(s_cco, 1)
            @bo.sync
            def _(sy):
                sy.wait_ge(s_cco, 1)
                ro = sy.value_load(offs_sb[0:1, 1:2])
                for q4 in range(4):
                    sy.dma_start(
                        out=OTl[q4 * 512:(q4 + 1) * 512, :],
                        in_=Og.ap()[q4 * 512:(q4 + 1) * 512, ds(ro, TSL)]
                        ).then_inc(s_ol, 16)
                sy.wait_ge(s_ol, 16 * 4)
        bar()
        if stop_after == "AGO":
            _stopper(nc, outp, xa)
            return nc

        def ot_tile(eng, tt):
            return OTl.rearrange("(k p) t -> p k t", p=128)[
                :, :, tt * 128:(tt + 1) * 128]

        def wg_tile(pc):
            r, lc = pc // 11, pc % 11
            return wpgg.ap().rearrange("(r k p) o -> r p k o", r=8, p=128)[
                r, :, :, lc * 128:(lc + 1) * 128]

        def wu_tile(pc):
            cc = 44 + pc
            r, lc = cc // 11, cc % 11
            return wpgg.ap().rearrange("(r k p) o -> r p k o", r=8, p=128)[
                r, :, :, lc * 128:(lc + 1) * 128]

        def wd_tile(ic, dc):
            return wpdg.ap().rearrange("(r i) d -> i r d", r=8)[
                ic * 128:(ic + 1) * 128, 2 * dc:2 * dc + 2, :]

        _fcut = _os.environ.get("MEGA_FCUT") or None
        with ExitStack() as ctx:
            phase_F(nc, PX("F_"), ctx.enter_context, dict(
                z2s=z2loc, wout=wout, boutr=boutr, idt=idt, outp=outp,
                x3f=x3f, x3Tf=x3Tf, ot_tile=ot_tile, wg_tile=wg_tile,
                wu_tile=wu_tile, wd_tile=wd_tile, cut=_fcut))
        if _fcut:
            bar()
            _stopper(nc, outp, xa)
    return nc


_BUILT = {}


def _get(name, fn):
    if name not in _BUILT:
        _BUILT[name] = fn()
    return _BUILT[name]


def _hadamard128():
    H = np.array([[1.0]], dtype=np.float32)
    while H.shape[0] < 128:
        H = np.block([[H, H], [H, -H]])
    return H


def kernel_unfused(**ins):
    x = np.ascontiguousarray(ins["L0_attn_x"].reshape(TOK, DM))
    xT = np.ascontiguousarray(x.T)
    wxT = np.ascontiguousarray(ins["L0_attn_W_x"].T)
    wdtT = np.ascontiguousarray(ins["L0_attn_W_dt"].T)
    bdt = np.ascontiguousarray(ins["L0_attn_b_dt"].reshape(16, 128).T)
    idt = np.eye(128, dtype=BF)
    h128 = np.ascontiguousarray(_hadamard128().astype(BF))
    cores = list(range(NCORE))

    W2T = np.ascontiguousarray(ins["L0_ffn_W2"].T)
    gup = ins["L1_ffn_gate_up_packed"]
    gscT = np.ascontiguousarray(ins["L1_ffn_gate_up_scales"].T)
    dpk = ins["L1_ffn_down_packed"]
    dscT = np.ascontiguousarray(ins["L1_ffn_down_scales"].T)
    ncE = _get("E", build_E)
    in_maps = []
    for c in range(NCORE):
        dsl = slice(c * 256, (c + 1) * 256)
        gpk_c = np.ascontiguousarray(np.concatenate(
            [gup[c * 704:(c + 1) * 704],
             gup[FI + c * 704:FI + (c + 1) * 704]], axis=0))
        gsc_c = np.ascontiguousarray(np.concatenate(
            [gscT[c * 704:(c + 1) * 704],
             gscT[FI + c * 704:FI + (c + 1) * 704]], axis=0)
            .reshape(11, 128, 16).transpose(1, 0, 2).reshape(128, 176))
        dpk_c = np.ascontiguousarray(dpk[c * 256:(c + 1) * 256])
        dsc_c = np.ascontiguousarray(dscT[c * 256:(c + 1) * 256]
                                     .reshape(2, 128, 44).transpose(1, 0, 2)
                                     .reshape(128, 88))
        in_maps.append(dict(w2T=np.ascontiguousarray(W2T[:, dsl]), h128=h128,
                            idt=idt, gpk=gpk_c, gsc=gsc_c, dpk=dpk_c, dsc=dsc_c))
    resE = run_bass_kernel_spmd(ncE, in_maps, core_ids=cores)
    w2pT = np.concatenate([r["w2pT"] for r in resE.results], axis=1)
    wguT = np.zeros((DM, 2 * FI), dtype=BF)
    wdnT = np.concatenate([r["wdnT"] for r in resE.results], axis=1)
    for c in range(NCORE):
        blkc = resE.results[c]["wguT"]
        wguT[:, c * 704:(c + 1) * 704] = blkc[:, 0:704]
        wguT[:, FI + c * 704:FI + (c + 1) * 704] = blkc[:, 704:1408]

    ncA = _get("A", build_A)
    in_maps = []
    for c in range(NCORE):
        in_maps.append(dict(
            xT=np.ascontiguousarray(xT[:, c * TSL:(c + 1) * TSL]),
            wxT=wxT, wdtT=wdtT, bdt=bdt))
    resA = run_bass_kernel_spmd(ncA, in_maps, core_ids=cores)
    dtT = np.concatenate([r["dtT"] for r in resA.results], axis=1)
    bcs = np.concatenate([r["bc"] for r in resA.results], axis=1)
    BT = np.ascontiguousarray(bcs[0:16])
    CT = np.ascontiguousarray(bcs[16:32])

    A_log = ins["L0_attn_A_log"]
    Dv = ins["L0_attn_D"]
    ncB = _get("B", build_B)
    in_maps = []
    for c in range(NCORE):
        rs = slice(c * 256, (c + 1) * 256)
        in_maps.append(dict(
            dtTs=np.ascontiguousarray(dtT[rs]),
            xTs=np.ascontiguousarray(xT[rs]),
            Ash=np.ascontiguousarray(
                A_log[rs].reshape(2, 128, NST).transpose(1, 0, 2).reshape(128, 32)),
            Dsh=np.ascontiguousarray(Dv[rs].reshape(2, 128).T),
            BT=BT, CT=CT, idt=idt))
    resB = run_bass_kernel_spmd(ncB, in_maps, core_ids=cores)
    yT = np.concatenate([r["yTs"] for r in resB.results], axis=0)
    y = np.ascontiguousarray(yT.T)

    w1T_h = np.ascontiguousarray(ins["L0_ffn_W1"].T)
    w3T_h = np.ascontiguousarray(ins["L0_ffn_W3"].T)
    ncC = _get("C", build_C)
    in_maps = []
    for c in range(NCORE):
        ts = slice(c * TSL, (c + 1) * TSL)
        in_maps.append(dict(xa=np.ascontiguousarray(x[ts]),
                            ya=np.ascontiguousarray(y[ts]),
                            w1T=w1T_h, w3T=w3T_h, w2pT=w2pT, idt=idt))
    resC = run_bass_kernel_spmd(ncC, in_maps, core_ids=cores)
    z2 = np.concatenate([r["z2o"] for r in resC.results], axis=0)

    wqkv = ins["L1_attn_w_qkv"]
    masks_h = _attn_masks()
    ncD = _get("D", build_D)
    in_maps = []
    for c in range(NCORE):
        beta = c // 4
        h0 = (c % 4) * 4
        z2T_b = np.ascontiguousarray(z2[beta * SEQ:(beta + 1) * SEQ].T)
        in_maps.append(dict(
            z2T=z2T_b,
            wq=np.ascontiguousarray(wqkv[:, h0 * DH:(h0 + 4) * DH]),
            wk=np.ascontiguousarray(wqkv[:, DM + h0 * DH:DM + (h0 + 4) * DH]),
            wv=np.ascontiguousarray(wqkv[:, 2 * DM + h0 * DH:2 * DM + (h0 + 4) * DH]),
            masks=masks_h))
    resD = run_bass_kernel_spmd(ncD, in_maps, core_ids=cores)
    OT = [np.concatenate([resD.results[b * 4 + j]["Oc"] for j in range(4)], axis=0)
          for b in range(2)]
    OTfull = np.concatenate(OT, axis=1)

    wout_h = np.ascontiguousarray(ins["L1_attn_w_out"])
    boutr_h = np.ascontiguousarray(np.tile(ins["L1_attn_b_out"][None, :], (128, 1)))
    ncF = _get("F", build_F)
    in_maps = []
    for c in range(NCORE):
        ts = slice(c * TSL, (c + 1) * TSL)
        in_maps.append(dict(
            OTs=np.ascontiguousarray(OTfull[:, ts]),
            z2s=np.ascontiguousarray(z2[ts]),
            wout=wout_h, boutr=boutr_h, wguT=wguT, wdnT=wdnT, idt=idt))
    resF = run_bass_kernel_spmd(ncF, in_maps, core_ids=cores)
    out = np.concatenate([r["outp"] for r in resF.results], axis=0)
    return out.reshape(2, SEQ, DM)


def _attn_masks():
    m = np.zeros((128, 5 * 512), dtype=BF)
    kk = np.arange(128)[:, None]
    qq = np.arange(512)[None, :]
    for r in range(4):
        m[:, r * 512:(r + 1) * 512] = (kk + r * 128 <= qq).astype(BF)
    corner = ((kk <= qq) | (qq < 128)).astype(BF)
    m[:, 4 * 512:5 * 512] = corner
    return m


def kernel(**ins):
    x = np.ascontiguousarray(ins["L0_attn_x"].reshape(TOK, DM))
    xT = np.ascontiguousarray(x.T)
    wxT = np.ascontiguousarray(ins["L0_attn_W_x"].T)
    wdtT = np.ascontiguousarray(ins["L0_attn_W_dt"].T)
    bdt_full = ins["L0_attn_b_dt"]
    A_log = ins["L0_attn_A_log"]
    Dv = ins["L0_attn_D"]
    W2T = np.ascontiguousarray(ins["L0_ffn_W2"].T)
    gup = ins["L1_ffn_gate_up_packed"]
    gscT = np.ascontiguousarray(ins["L1_ffn_gate_up_scales"].T)
    dpk = ins["L1_ffn_down_packed"]
    dscT = np.ascontiguousarray(ins["L1_ffn_down_scales"].T)
    w1T_h = np.ascontiguousarray(ins["L0_ffn_W1"].T)
    w3T_h = np.ascontiguousarray(ins["L0_ffn_W3"].T)
    wqkv = ins["L1_attn_w_qkv"]
    wout_h = np.ascontiguousarray(ins["L1_attn_w_out"])
    boutr_h = np.ascontiguousarray(np.tile(ins["L1_attn_b_out"][None, :], (128, 1)))
    idt = np.eye(128, dtype=BF)
    h128 = np.ascontiguousarray(_hadamard128().astype(BF))
    masks_h = _attn_masks()

    ncM = _get("M", build_mega)
    in_maps = []
    for c in range(NCORE):
        rs = slice(c * 256, (c + 1) * 256)
        ts = slice(c * TSL, (c + 1) * TSL)
        h0 = (c % 4) * 4
        gpk_c = np.ascontiguousarray(gup[c * 1408:(c + 1) * 1408])
        gsc_c = np.ascontiguousarray(
            gscT[c * 1408:(c + 1) * 1408]
            .reshape(11, 128, 16).transpose(1, 0, 2).reshape(128, 176))
        dpk_c = np.ascontiguousarray(dpk[c * 256:(c + 1) * 256])
        dsc_c = np.ascontiguousarray(dscT[c * 256:(c + 1) * 256]
                                     .reshape(2, 128, 44).transpose(1, 0, 2)
                                     .reshape(128, 88))
        in_maps.append(dict(
            xTf=xT,
            wxT=wxT,
            wdtTs=np.ascontiguousarray(wdtT[:, rs]),
            bdt2=np.ascontiguousarray(bdt_full[rs].reshape(2, 128).T),
            xTsh=np.ascontiguousarray(xT[rs]),
            Ash=np.ascontiguousarray(
                A_log[rs].reshape(2, 128, NST).transpose(1, 0, 2).reshape(128, 32)),
            Dsh=np.ascontiguousarray(Dv[rs].reshape(2, 128).T),
            w2T=np.ascontiguousarray(W2T[:, rs]),
            h128=h128, idt=idt,
            gpk=gpk_c, gsc=gsc_c, dpk=dpk_c, dsc=dsc_c,
            xa=np.ascontiguousarray(x[ts]),
            w1T=w1T_h, w3T=w3T_h,
            wq=np.ascontiguousarray(wqkv[:, h0 * DH:(h0 + 4) * DH]),
            wk=np.ascontiguousarray(wqkv[:, DM + h0 * DH:DM + (h0 + 4) * DH]),
            wv=np.ascontiguousarray(wqkv[:, 2 * DM + h0 * DH:2 * DM + (h0 + 4) * DH]),
            masks=masks_h,
            wout=wout_h, boutr=boutr_h,
            offs=np.array([[c * TSL, (c % 4) * TSL]
                           + [(c % 4) * TSL + tt * 128 for tt in range(16)]
                           + [c * TSL + tt * 128 for tt in range(16)]],
                          dtype=np.int32)))
    res = run_bass_kernel_spmd(ncM, in_maps, core_ids=list(range(NCORE)))
    out = np.concatenate([r["outp"] for r in res.results], axis=0)
    return out.reshape(2, SEQ, DM)


# version 09
# speedup vs baseline: 1.0446x; 1.0139x over previous
import numpy as np
import ml_dtypes
from contextlib import ExitStack

import concourse.bass as bass
import concourse.mybir as mybir
from concourse.bass_utils import run_bass_kernel_spmd
from concourse.bass import ds

F32 = mybir.dt.float32
BF16 = mybir.dt.bfloat16
I32 = mybir.dt.int32
AF = mybir.ActivationFunctionType
ALU = mybir.AluOpType
AX = mybir.AxisListType
BF = ml_dtypes.bfloat16

NCORE = 8
DM = 2048
TOK = 16384
TSL = TOK // NCORE
NST = 16
DTR = 64
SEQ = 8192
NH = 16
DH = 128
FF0 = 8192
FI = 5632



class NcProxy:
    def __init__(self, nc, pfx, dma_pool=None, cmp_pool=None,
                 sw_names=None, sw_alloc=None):
        self.nc = nc
        self.pfx = pfx
        self.dma_pool = dma_pool
        self.cmp_pool = cmp_pool
        self.sw_names = sw_names or set()
        self.sw_alloc = sw_alloc
        self.ndma = 0
        self.ncmp = 0

    def sbuf_tensor(self, name, shape, dtype):
        return self.nc.sbuf_tensor(self.pfx + name, shape, dtype)

    def psum_tensor(self, name, shape, dtype):
        return self.nc.psum_tensor(self.pfx + name, shape, dtype)

    def semaphore(self, name):
        if self.dma_pool is None:
            return self.nc.semaphore(self.pfx + name)
        import contextlib

        @contextlib.contextmanager
        def _fake(handle):
            yield handle
        if name in self.sw_names:
            h = self.sw_alloc[0][self.sw_alloc[1]]
            self.sw_alloc[1] += 1
        elif name.startswith("s_"):
            h = self.cmp_pool[self.ncmp]
            self.ncmp += 1
        else:
            h = self.dma_pool[self.ndma]
            self.ndma += 1
        return _fake(h)


def _barrier(nc, ncx, e, k, sb=None, clear_from=None):
    if sb is None:
        sb = e(ncx.semaphore(f"bar{k}"))
    with nc.Block() as blk:
        for engname in ("tensor", "scalar", "vector", "gpsimd"):
            def mk(engname=engname):
                def _(eng):
                    eng.drain().then_inc(sb, 1)
                    eng.wait_ge(sb, 6 if clear_from is not None else 5)
                return _
            getattr(blk, engname)(mk())

        @blk.sync
        def _(sy):
            sy.drain().then_inc(sb, 1)
            sy.wait_ge(sb, 5)
            if clear_from is not None:
                sy.sem_clear(range(clear_from, 256))
                sy.nop().then_inc(sb, 1)
                sy.wait_ge(sb, 6)
    return sb


def build_A():
    nc = bass.Bass("TRN2", target_bir_lowering=False)
    xT = nc.dram_tensor("xT", [DM, TSL], F32, kind="ExternalInput").ap()
    wxT = nc.dram_tensor("wxT", [DM, 96], F32, kind="ExternalInput").ap()
    wdtT = nc.dram_tensor("wdtT", [DTR, DM], F32, kind="ExternalInput").ap()
    bdt = nc.dram_tensor("bdt", [128, 16], F32, kind="ExternalInput").ap()
    dtT = nc.dram_tensor("dtT", [DM, TSL], F32, kind="ExternalOutput").ap()
    bc = nc.dram_tensor("bc", [32, TSL], F32, kind="ExternalOutput").ap()

    NK = DM // 128
    NTC = TSL // 512

    with ExitStack() as ctx:
        e = ctx.enter_context
        xb = e(nc.sbuf_tensor("xb", [128, NK * TSL], BF16))
        wx_sb = e(nc.sbuf_tensor("wx_sb", [128, NK * 96], BF16))
        wdt_sb = e(nc.sbuf_tensor("wdt_sb", [DTR, DM], BF16))
        bdt_sb = e(nc.sbuf_tensor("bdt_sb", [128, 16], F32))
        xp_sb = e(nc.sbuf_tensor("xp_sb", [96, TSL], F32))
        dtx_b = e(nc.sbuf_tensor("dtx_b", [DTR, TSL], BF16))
        t_z = e(nc.sbuf_tensor("t_z", [128, 2 * 512], F32))
        t_a = e(nc.sbuf_tensor("t_a", [128, 512], F32))
        t_e = e(nc.sbuf_tensor("t_e", [128, 512], F32))
        t_l = e(nc.sbuf_tensor("t_l", [128, 2 * 512], F32))
        t_r = e(nc.sbuf_tensor("t_r", [128, 512], F32))
        dt_out = e(nc.sbuf_tensor("dt_out", [128, 2 * 512], F32))
        one_c = e(nc.sbuf_tensor("one_c", [128, 1], F32))
        p_xp = e(nc.psum_tensor("p_xp", [96, 512], F32))
        p_dt = e(nc.psum_tensor("p_dt", [128, 2 * 512], F32))
        ld = e(nc.semaphore("ld"))
        lds = e(nc.semaphore("lds"))
        s_xp = e(nc.semaphore("s_xp"))
        s_dt = e(nc.semaphore("s_dt"))
        s_dtx = e(nc.semaphore("s_dtx"))
        s_spZ = e(nc.semaphore("s_spZ"))
        s_spL = e(nc.semaphore("s_spL"))
        s_vr = e(nc.semaphore("s_vr"))
        s_out = e(nc.semaphore("s_out"))
        stp = [e(nc.semaphore("st0")), e(nc.semaphore("st1"))]
        blk = e(nc.Block())

        NLD = 16 + 16 + 1 + 1 + 1

        @blk.gpsimd
        def _(g):
            for k in range(NK):
                g.dma_start(out=xb[:, k * TSL:(k + 1) * TSL],
                            in_=xT[k * 128:(k + 1) * 128, :]).then_inc(ld, 16)
            for k in range(NK):
                g.dma_start(out=wx_sb[:, k * 96:(k + 1) * 96],
                            in_=wxT[k * 128:(k + 1) * 128, :]).then_inc(ld, 16)
            g.dma_start(out=wdt_sb[:], in_=wdtT[:]).then_inc(ld, 16)

        @blk.sync
        def _(sy):
            sy.dma_start(out=bdt_sb[:], in_=bdt[:]).then_inc(lds, 16)

        LD_ALL = 16 * (NK + NK + 1)

        @blk.tensor
        def _(t):
            t.wait_ge(ld, LD_ALL)
            g = 0
            for tc in range(NTC):
                cs = slice(tc * 512, (tc + 1) * 512)
                if tc > 0:
                    t.wait_ge(s_dtx, tc)
                for k in range(NK):
                    mm = t.matmul(p_xp[:, :], wx_sb[:, k * 96:(k + 1) * 96],
                                  xb[:, k * TSL + tc * 512: k * TSL + tc * 512 + 512],
                                  start=(k == 0), stop=(k == NK - 1))
                mm.then_inc(s_xp, 1)
                t.wait_ge(s_dtx, tc + 1)
                for k in range(NK):
                    if g >= 2:
                        t.wait_ge(s_spZ, g - 1)
                    b = g % 2
                    t.matmul(p_dt[:, b * 512:(b + 1) * 512],
                             wdt_sb[:, k * 128:(k + 1) * 128], dtx_b[:, cs],
                             start=True, stop=True).then_inc(s_dt, 1)
                    g += 1

        @blk.scalar
        def _(a):
            a.wait_ge(lds, 16)
            g = 0
            for tc in range(NTC):
                for k in range(NK):
                    b = g % 2
                    zsl = slice(b * 512, (b + 1) * 512)
                    a.wait_ge(s_dt, g + 1)
                    if g >= 2:
                        a.wait_ge(s_vr, g - 1)
                    a.activation(t_z[:, zsl], p_dt[:, zsl], AF.Identity,
                                 bias=bdt_sb[:, k:k + 1]).then_inc(s_spZ, 1)
                    a.drain()
                    a.activation(t_a[:], t_z[:, zsl], AF.Abs)
                    a.drain()
                    a.activation(t_e[:], t_a[:], AF.Exp, scale=-1.0)
                    a.drain()
                    a.activation(t_l[:, zsl], t_e[:], AF.Ln,
                                 bias=one_c[:]).then_inc(s_spL, 1)
                    g += 1

        @blk.vector
        def _(v):
            v.memset(one_c[:], 1.0)
            g = 0
            for tc in range(NTC):
                cs = slice(tc * 512, (tc + 1) * 512)
                v.wait_ge(s_xp, tc + 1)
                v.tensor_copy(xp_sb[:, cs], p_xp[:, :])
                v.tensor_copy(dtx_b[:, cs], p_xp[0:64, :]).then_inc(s_dtx, 1)
                for k in range(NK):
                    b = g % 2
                    zsl = slice(b * 512, (b + 1) * 512)
                    v.wait_ge(s_spZ, g + 1)
                    v.tensor_scalar(t_r[:], t_z[:, zsl], 0.0, None,
                                    ALU.max).then_inc(s_vr, 1)
                    v.wait_ge(s_spL, g + 1)
                    if g >= 2:
                        v.wait_ge(stp[g % 2], 16 * (g // 2))
                    v.drain()
                    v.tensor_add(dt_out[:, zsl], t_r[:], t_l[:, zsl]).then_inc(s_out, 1)
                    g += 1

        @blk.sync
        def _(sy):
            g = 0
            for tc in range(NTC):
                for k in range(NK):
                    b = g % 2
                    sy.wait_ge(s_out, g + 1)
                    sy.dma_start(out=dtT[k * 128:(k + 1) * 128,
                                         tc * 512:(tc + 1) * 512],
                                 in_=dt_out[:, b * 512:(b + 1) * 512]).then_inc(stp[g % 2], 16)
                    g += 1
            sy.wait_ge(s_dtx, NTC)
            sy.dma_start(out=bc[:], in_=xp_sb[64:96, :]).then_inc(stp[0], 16)
            sy.wait_ge(stp[0], 16 * (NTC * NK // 2 + 1))
            sy.wait_ge(stp[1], 16 * (NTC * NK // 2))
    return nc


def build_E():
    nc = bass.Bass("TRN2", target_bir_lowering=False)
    F16 = mybir.dt.float16
    io = dict(
        w2T=nc.dram_tensor("w2T", [FF0, 256], F32, kind="ExternalInput").ap(),
        h128=nc.dram_tensor("h128", [128, 128], BF16, kind="ExternalInput").ap(),
        idt=nc.dram_tensor("idt", [128, 128], BF16, kind="ExternalInput").ap(),
        gpk=nc.dram_tensor("gpk", [1408, 256], I32, kind="ExternalInput").ap(),
        gsc=nc.dram_tensor("gsc", [128, 11 * 16], F16, kind="ExternalInput").ap(),
        dpk=nc.dram_tensor("dpk", [256, 704], I32, kind="ExternalInput").ap(),
        dsc=nc.dram_tensor("dsc", [128, 2 * 44], F16, kind="ExternalInput").ap(),
        w2pT=nc.dram_tensor("w2pT", [FF0, 256], BF16, kind="ExternalOutput").ap(),
        wguT=nc.dram_tensor("wguT", [DM, 1408], BF16, kind="ExternalOutput").ap(),
        wdnT=nc.dram_tensor("wdnT", [FI, 256], BF16, kind="ExternalOutput").ap())
    with ExitStack() as ctx:
        phase_E(nc, NcProxy(nc, "E_"), ctx.enter_context, io)
    return nc


def phase_E(nc, ncx, e, io):
    w2T = io["w2T"]; h128 = io["h128"]; idt = io["idt"]
    gpk = io["gpk"]; gsc = io["gsc"]; dpk = io["dpk"]; dsc = io["dsc"]
    w2pT = io["w2pT"]; wguT = io["wguT"]; wdnT = io["wdnT"]

    SC = float(2.0 ** -6.5)
    NTR = 11 * 16 + 2 * 44

    if True:
        w2s = e(ncx.sbuf_tensor("w2s", [128, 2 * 256], BF16))
        t1 = e(ncx.sbuf_tensor("t1", [128, 64 * 256], F32))
        tmph = e(ncx.sbuf_tensor("tmph", [128, 32 * 256], F32))
        w2o = e(ncx.sbuf_tensor("w2o", [128, 64 * 256], BF16))
        h_sb = e(ncx.sbuf_tensor("h_sb", [128, 128], BF16))
        idt_sb = e(ncx.sbuf_tensor("idt_sb", [128, 128], BF16))
        pk_sb = e(ncx.sbuf_tensor("pk_sb", [128, 11 * 256], I32))
        gsc_sb = e(ncx.sbuf_tensor("gsc_sb", [128, 11 * 16], F32))
        dpk_sb = e(ncx.sbuf_tensor("dpk_sb", [128, 2 * 704], I32))
        dsc_sb = e(ncx.sbuf_tensor("dsc_sb", [128, 2 * 44], F32))
        m_i = e(ncx.sbuf_tensor("m_i", [128, 704], I32))
        mf = e(ncx.sbuf_tensor("mf", [128, 704], F32))
        r1 = e(ncx.sbuf_tensor("r1", [128, 704], F32))
        r2 = e(ncx.sbuf_tensor("r2", [128, 704], F32))
        uu = e(ncx.sbuf_tensor("uu", [128, 704], F32))
        vv = e(ncx.sbuf_tensor("vv", [128, 704], F32))
        s31 = e(ncx.sbuf_tensor("s31", [128, 704], I32))
        wde = e(ncx.sbuf_tensor("wde", [128, 5632], F32))
        wdb = e(ncx.sbuf_tensor("wdb", [128, 5632], BF16))
        wtr = e(ncx.sbuf_tensor("wtr", [128, 128], BF16))
        p_f = e(ncx.psum_tensor("p_f", [128, 256], F32))
        p_tr = e(ncx.psum_tensor("p_tr", [128, 128], BF16))
        ld = e(ncx.semaphore("ld"))
        lds2 = e(ncx.semaphore("lds2"))
        ldwp = [e(ncx.semaphore("ldwA")), e(ncx.semaphore("ldwB"))]
        s_t = e(ncx.semaphore("s_t"))
        s_v = e(ncx.semaphore("s_v"))
        s_a = e(ncx.semaphore("s_a"))
        st = e(ncx.semaphore("st"))
        blk = e(nc.Block())

        @blk.gpsimd
        def _(g):
            g.dma_start(out=gsc_sb[:], in_=gsc[:]).then_inc(ld, 16)
            g.dma_start(out=dsc_sb[:], in_=dsc[:]).then_inc(ld, 16)
            for c in range(64):
                if c >= 2:
                    g.wait_ge(s_t, c - 1)
                b = c % 2
                g.dma_start(out=w2s[:, b * 256:(b + 1) * 256],
                            in_=w2T[c * 128:(c + 1) * 128, :]).then_inc(ldwp[c % 2], 16)

        @blk.sync
        def _(sy):
            sy.dma_start(out=h_sb[:], in_=h128[:]).then_inc(lds2, 16)
            sy.dma_start(out=idt_sb[:], in_=idt[:]).then_inc(lds2, 16)
            sy.dma_start(out=pk_sb[:].rearrange("p (c j) -> p c j", c=11),
                         in_=gpk.rearrange("(c p) j -> p c j", p=128)).then_inc(lds2, 16)
            sy.dma_start(out=dpk_sb[:].rearrange("p (c j) -> p c j", c=2),
                         in_=dpk.rearrange("(c p) j -> p c j", p=128)).then_inc(lds2, 16)

        @blk.tensor
        def _(t):
            t.wait_ge(ld, 16 * 2)
            t.wait_ge(lds2, 16 * 4)
            for c in range(64):
                t.wait_ge(ldwp[c % 2], 16 * (c // 2 + 1))
                if c >= 1:
                    t.wait_ge(s_a, c)
                b = c % 2
                mm = t.matmul(p_f[:], h_sb[:], w2s[:, b * 256:(b + 1) * 256],
                              start=True, stop=True)
                mm.then_inc(s_t, 1)
            ntr = 0
            for c in range(11):
                t.wait_ge(s_v, 2 + c)
                for k in range(16):
                    if ntr >= 1:
                        t.wait_ge(s_a, 64 + ntr)
                    mm = t.transpose(p_tr[:], wdb[:, k * 128:(k + 1) * 128], idt_sb[:])
                    mm.then_inc(s_t, 1)
                    ntr += 1
            for c in range(2):
                t.wait_ge(s_v, 13 + c)
                for k in range(44):
                    t.wait_ge(s_a, 64 + ntr)
                    mm = t.transpose(p_tr[:], wdb[:, k * 128:(k + 1) * 128], idt_sb[:])
                    mm.then_inc(s_t, 1)
                    ntr += 1

        @blk.scalar
        def _(a):
            for c in range(64):
                a.wait_ge(s_t, c + 1)
                a.drain()
                a.activation(t1[:, c * 256:(c + 1) * 256], p_f[:],
                             AF.Identity).then_inc(s_a, 1)
            for ntr in range(NTR):
                a.wait_ge(s_t, 64 + ntr + 1)
                if ntr >= 1:
                    a.wait_ge(st, 16 * (ntr + 1))
                a.drain()
                a.activation(wtr[:], p_tr[:], AF.Identity).then_inc(s_a, 1)

        @blk.vector
        def _(v):
            v.wait_ge(s_a, 64)
            for s in range(6):
                bit = 1 << s
                lo = [c for c in range(64) if not (c & bit)]
                v.drain()
                for j, c in enumerate(lo):
                    v.tensor_copy(tmph[:, j * 256:(j + 1) * 256],
                                  t1[:, c * 256:(c + 1) * 256])
                v.drain()
                for j, c in enumerate(lo):
                    cb = c | bit
                    v.tensor_add(t1[:, c * 256:(c + 1) * 256],
                                 t1[:, c * 256:(c + 1) * 256],
                                 t1[:, cb * 256:(cb + 1) * 256])
                v.drain()
                for j, c in enumerate(lo):
                    cb = c | bit
                    v.tensor_sub(t1[:, cb * 256:(cb + 1) * 256],
                                 tmph[:, j * 256:(j + 1) * 256],
                                 t1[:, cb * 256:(cb + 1) * 256])
            v.drain()
            v.tensor_scalar_mul(w2o[:], t1[:], SC).then_inc(s_v, 1)

            def dequant(pk_ap, n_in, sc_ap, n_grp, base_ms):
                wv = wde[:, 0:n_in].rearrange("p (j e) -> p e j", e=8)
                n8 = n_in // 8
                for p in range(8):
                    v.drain()
                    v.tensor_scalar(m_i[:, 0:n8], pk_ap, 4 * p, 7,
                                    ALU.logical_shift_right, ALU.bitwise_and)
                    v.drain()
                    v.tensor_single_scalar(mf[:, 0:n8], m_i[:, 0:n8], 0.0, ALU.add)
                    v.drain()
                    v.tensor_scalar(r1[:, 0:n8], mf[:, 0:n8], -4.0, 0.0,
                                    ALU.add, ALU.max)
                    v.tensor_scalar(s31[:, 0:n8], pk_ap, 28 - 4 * p,
                                    -2147483648, ALU.logical_shift_left,
                                    ALU.bitwise_and)
                    v.tensor_scalar(r2[:, 0:n8], mf[:, 0:n8], -6.0, 0.0,
                                    ALU.add, ALU.max)
                    v.drain()
                    v.tensor_add(uu[:, 0:n8], mf[:, 0:n8], r1[:, 0:n8])
                    v.drain()
                    v.scalar_tensor_tensor(vv[:, 0:n8], uu[:, 0:n8], 0.5,
                                           r2[:, 0:n8], ALU.mult, ALU.add)
                    v.drain()
                    v.tensor_tensor(wv[:, p, :].bitcast(I32),
                                    vv[:, 0:n8].bitcast(I32),
                                    s31[:, 0:n8], ALU.bitwise_xor)
                v.drain()
                for grp in range(n_grp):
                    v.tensor_scalar_mul(wdb[:, grp * 128:(grp + 1) * 128],
                                        wde[:, grp * 128:(grp + 1) * 128],
                                        sc_ap[:, grp:grp + 1])
                v.drain()
                v.engine_nop().then_inc(s_v, 1)

            for c in range(11):
                if c >= 1:
                    v.wait_ge(s_t, 64 + 16 * c)
                dequant(pk_sb[:, c * 256:(c + 1) * 256], 2048,
                        gsc_sb[:, c * 16:(c + 1) * 16], 16, None)
            for c in range(2):
                v.wait_ge(s_t, 64 + 176 + 44 * c)
                dequant(dpk_sb[:, c * 704:(c + 1) * 704], 5632,
                        dsc_sb[:, c * 44:(c + 1) * 44], 44, None)

        @blk.sync
        def _(sy):
            sy.wait_ge(s_v, 1)
            sy.dma_start(out=w2pT.rearrange("(c p) d -> p c d", p=128),
                         in_=w2o[:].rearrange("p (c d) -> p c d", c=64)).then_inc(st, 16)
            ntr = 0
            for c in range(11):
                for k in range(16):
                    sy.wait_ge(s_a, 64 + ntr + 1)
                    sy.dma_start(out=wguT[k * 128:(k + 1) * 128,
                                          c * 128:(c + 1) * 128],
                                 in_=wtr[:]).then_inc(st, 16)
                    ntr += 1
            for c in range(2):
                for k in range(44):
                    sy.wait_ge(s_a, 64 + ntr + 1)
                    sy.dma_start(out=wdnT[k * 128:(k + 1) * 128,
                                          c * 128:(c + 1) * 128],
                                 in_=wtr[:]).then_inc(st, 16)
                    ntr += 1
            sy.wait_ge(st, 16 * (1 + NTR))




def build_C():
    nc = bass.Bass("TRN2", target_bir_lowering=False)
    ya = nc.dram_tensor("ya", [TSL, DM], F32, kind="ExternalInput").ap()
    w2pT = nc.dram_tensor("w2pT", [FF0, DM], BF16, kind="ExternalInput").ap()
    io = dict(
        xa=nc.dram_tensor("xa", [TSL, DM], F32, kind="ExternalInput").ap(),
        w1T=nc.dram_tensor("w1T", [DM, FF0], F32, kind="ExternalInput").ap(),
        w3T=nc.dram_tensor("w3T", [DM, FF0], F32, kind="ExternalInput").ap(),
        idt=nc.dram_tensor("idt", [128, 128], BF16, kind="ExternalInput").ap(),
        z2o=nc.dram_tensor("z2o", [TSL, DM], F32, kind="ExternalOutput").ap(),
        z1f=nc.dram_tensor("z1f", [TSL, DM], F32, kind="Internal").ap(),
        hidT=nc.dram_tensor("hidT", [FF0, TSL], BF16, kind="Internal").ap(),
        ya_tile=lambda eng, tt: ya[tt * 128:(tt + 1) * 128, :].rearrange(
            "t (r d) -> t r d", r=8),
        w2p_tile=lambda hc, dc: w2pT[hc * 128:(hc + 1) * 128,
                                     dc * 512:(dc + 1) * 512].rearrange(
                                         "p (r d) -> p r d", r=2),
        z2tl=None, offs_sb=None)
    with ExitStack() as ctx:
        phase_C(nc, NcProxy(nc, "C_"), ctx.enter_context, io)
    return nc


def phase_C(nc, ncx, e, io):
    xa = io["xa"]; w1T = io["w1T"]; w3T = io["w3T"]; idt = io["idt"]
    z2o = io["z2o"]; z1f = io["z1f"]; hidT = io["hidT"]
    ya_tile = io["ya_tile"]; w2p_tile = io["w2p_tile"]; z2tl = io["z2tl"]

    NT = TSL // 128
    NK = DM // 128
    NH0 = FF0 // 128
    INV = 1.0 / DM

    if True:
        zt = e(ncx.sbuf_tensor("zt", [128, DM], F32))
        yt_in = e(ncx.sbuf_tensor("yt_in", [128, DM], F32))
        z1t = e(ncx.sbuf_tensor("z1t", [128, DM], F32))
        z1b = e(ncx.sbuf_tensor("z1b", [128, DM], BF16))
        mcol = e(ncx.sbuf_tensor("mcol", [128, 4], F32))
        eps_c = e(ncx.sbuf_tensor("eps_c", [128, 1], F32))
        idt_sb = e(ncx.sbuf_tensor("idt_sb", [128, 128], BF16))
        z1T = e(ncx.sbuf_tensor("z1T", [128, NK * TSL], BF16))
        w1_sb = e(ncx.sbuf_tensor("w1_sb", [128, 2 * NK * 128], BF16))
        w3_sb = e(ncx.sbuf_tensor("w3_sb", [128, 2 * NK * 128], BF16))
        if z2tl is not None:
            z2t_sb = w1_sb[:, 0:NK * 128]
        sg = e(ncx.sbuf_tensor("sg", [128, 512], BF16))
        hb = e(ncx.sbuf_tensor("hb", [128, 2 * 512], BF16))
        hw_sb = e(ncx.sbuf_tensor("hw_sb", [128, NH0 * 512], BF16))
        w2_sb = e(ncx.sbuf_tensor("w2_sb", [128, 8 * 512], BF16))
        acc3 = [e(ncx.sbuf_tensor(f"acc{i}", [128, DM], F32)) for i in range(3)]
        accs = acc3 + [yt_in]
        if z2tl is not None:
            z2b = z1b
            s_tr3 = e(ncx.semaphore("s_tr3"))
            s_te3 = e(ncx.semaphore("s_te3"))
            s_zb = e(ncx.semaphore("s_zb"))
            stz = e(ncx.semaphore("stz"))
        p_g = e(ncx.psum_tensor("p_g", [128, 2 * 512], F32))
        p_u = e(ncx.psum_tensor("p_u", [128, 2 * 512], F32))
        p_o = [e(ncx.psum_tensor(f"p_o{i}", [128, 512], F32)) for i in range(4)]
        p_tr = p_g[:, 0:64].bitcast(BF16)
        p_tr2 = p_g[:, 0:128].bitcast(BF16)
        ld = e(ncx.semaphore("ld"))
        ldi = e(ncx.semaphore("ldi"))
        ldwp = [e(ncx.semaphore("ldwA")), e(ncx.semaphore("ldwB"))]
        ldh = e(ncx.semaphore("ldh"))
        ldw2p = [e(ncx.semaphore("ldw2%d" % qq)) for qq in range(8)]
        ldz = e(ncx.semaphore("ldz"))
        s_xm = e(ncx.semaphore("s_xm"))
        s_sd = e(ncx.semaphore("s_sd"))
        s_z1 = e(ncx.semaphore("s_z1"))
        s_tr = e(ncx.semaphore("s_tr"))
        s_te = e(ncx.semaphore("s_te"))
        s_gu = e(ncx.semaphore("s_gu"))
        s_sg = e(ncx.semaphore("s_sg"))
        s_hb = e(ncx.semaphore("s_hb"))
        s_w2c = e(ncx.semaphore("s_w2c"))
        s_ev = e(ncx.semaphore("s_ev"))
        s_ln = e(ncx.semaphore("s_ln"))
        st = e(ncx.semaphore("st"))
        sthp = [e(ncx.semaphore("sthA")), e(ncx.semaphore("sthB"))]
        st2 = e(ncx.semaphore("st2"))
        blk = e(nc.Block())

        @blk.sync
        def _(sy):
            sy.dma_start(out=idt_sb[:], in_=idt[:]).then_inc(ldi, 16)
            for tt in range(NT):
                if tt >= 1:
                    sy.wait_ge(s_z1, tt)
                sy.dma_start(out=zt[:],
                             in_=xa[tt * 128:(tt + 1) * 128, :]).then_inc(ld, 16)
                sy.dma_start(out=yt_in[:].rearrange("p (r d) -> p r d", r=8),
                             in_=ya_tile(sy, tt)).then_inc(ld, 16)
                sy.wait_ge(s_z1, tt + 1)
                sy.dma_start(out=z1f[tt * 128:(tt + 1) * 128, :],
                             in_=z1t[:]).then_inc(st, 16)

        @blk.vector
        def _(v):
            v.memset(eps_c[:], 1e-5)
            for tt in range(NT):
                v.wait_ge(ld, 16 * (2 * tt + 2))
                v.tensor_add(zt[:], zt[:], yt_in[:])
                v.drain()
                v.tensor_reduce(mcol[:, 0:1], zt[:], AX.X, ALU.add)
                v.drain()
                v.tensor_scalar_mul(mcol[:, 0:1], mcol[:, 0:1], INV)
                v.drain()
                v.tensor_scalar(zt[:], zt[:], mcol[:, 0:1], None,
                                ALU.subtract).then_inc(s_xm, 1)
                v.wait_ge(s_sd, tt + 1)
                v.reciprocal(mcol[:, 2:3], mcol[:, 1:2])
                if tt >= 1:
                    v.wait_ge(st, 16 * tt)
                    v.wait_ge(s_tr, NK * tt)
                v.drain()
                v.tensor_scalar_mul(z1t[:], zt[:], mcol[:, 2:3])
                v.drain()
                v.tensor_copy(z1b[:], z1t[:]).then_inc(s_z1, 1)

        @blk.scalar
        def _(a):
            n = 0
            for tt in range(NT):
                a.wait_ge(s_xm, tt + 1)
                if tt >= 1:
                    a.wait_ge(st, 16 * tt)
                a.drain()
                a.activation(z1t[:], zt[:], AF.Square, accum_out=mcol[:, 3:4])
                a.drain()
                a.activation(mcol[:, 1:2], mcol[:, 3:4], AF.Sqrt, scale=INV,
                             bias=eps_c[:]).then_inc(s_sd, 1)
                for k in range(NK):
                    a.wait_ge(s_tr, n + 1)
                    a.drain()
                    a.activation(z1T[:, k * TSL + tt * 128:k * TSL + (tt + 1) * 128],
                                 p_tr, AF.Identity).then_inc(s_te, 1)
                    n += 1

        @blk.tensor
        def _(t):
            t.wait_ge(ldi, 16)
            n = 0
            for tt in range(NT):
                t.wait_ge(s_z1, tt + 1)
                for k in range(NK):
                    if n >= 1:
                        t.wait_ge(s_te, n)
                    mm = t.transpose(p_tr, z1b[:, k * 128:(k + 1) * 128],
                                     idt_sb[:])
                    mm.then_inc(s_tr, 1)
                    n += 1
            t.wait_ge(s_te, NT * NK)
            gi = 0
            for hc in range(NH0):
                t.wait_ge(ldwp[hc % 2], 32 * (hc // 2 + 1))
                hb_ = (hc % 2) * NK * 128
                for tcx in range(4):
                    gb = (gi % 2) * 512
                    if gi >= 2:
                        t.wait_ge(s_hb, gi - 1)
                    for k in range(NK):
                        t.matmul(p_g[:, gb:gb + 512],
                                 w1_sb[:, hb_ + k * 128:hb_ + (k + 1) * 128],
                                 z1T[:, k * TSL + tcx * 512:k * TSL + tcx * 512 + 512],
                                 start=(k == 0), stop=(k == NK - 1))
                    for k in range(NK):
                        mm = t.matmul(p_u[:, gb:gb + 512],
                                      w3_sb[:, hb_ + k * 128:hb_ + (k + 1) * 128],
                                      z1T[:, k * TSL + tcx * 512:k * TSL + tcx * 512 + 512],
                                      start=(k == 0), stop=(k == NK - 1))
                    mm.then_inc(s_gu, 1)
                    gi += 1
            for w in range(4):
                t.wait_ge(ldh, 16 * NH0 * (w + 1))
                for dc in range(4):
                    idx = w * 4 + dc
                    if idx > 0:
                        t.wait_ge(s_ev, idx)
                    t.wait_ge(ldz, 16 * (w + 1) * 4)
                    for hc in range(NH0):
                        widx = idx * NH0 + hc
                        t.wait_ge(ldw2p[widx % 8], 16 * (widx // 8 + 1))
                        wb = (widx % 8) * 512
                        for ttl in range(4):
                            mm = t.matmul(p_o[ttl][:],
                                          hw_sb[:, hc * 512 + ttl * 128:hc * 512 + (ttl + 1) * 128],
                                          w2_sb[:, wb:wb + 512],
                                          start=(hc == 0), stop=(hc == NH0 - 1))
                        mm.then_inc(s_w2c, 1)
                if z2tl is not None:
                    for ttl in range(4):
                        li = w * 4 + ttl
                        t.wait_ge(s_zb, li + 1)
                        for k in range(NK):
                            ti = li * NK + k
                            if ti >= 1:
                                t.wait_ge(s_te3, ti)
                            t.transpose(p_tr2[:, 0:128],
                                        z2b[:, k * 128:(k + 1) * 128],
                                        idt_sb[:]).then_inc(s_tr3, 1)

        @blk.gpsimd
        def _(g):
            for hc in range(NH0):
                if hc >= 2:
                    g.wait_ge(s_gu, 4 * (hc - 1))
                hb_ = (hc % 2) * NK * 128
                g.dma_start(out=w1_sb[:, hb_:hb_ + NK * 128].rearrange(
                                "p (k h) -> p k h", k=NK),
                            in_=w1T.rearrange("(k p) h -> p k h", p=128)[
                                :, :, hc * 128:(hc + 1) * 128]).then_inc(ldwp[hc % 2], 16)
                g.dma_start(out=w3_sb[:, hb_:hb_ + NK * 128].rearrange(
                                "p (k h) -> p k h", k=NK),
                            in_=w3T.rearrange("(k p) h -> p k h", p=128)[
                                :, :, hc * 128:(hc + 1) * 128]).then_inc(ldwp[hc % 2], 16)
            for w in range(4):
                for dc in range(4):
                    idx = w * 4 + dc
                    for hc in range(NH0):
                        widx = idx * NH0 + hc
                        if widx >= 8:
                            g.wait_ge(s_w2c, widx - 7)
                        wb = (widx % 8) * 512
                        g.dma_start(out=w2_sb[:, wb:wb + 512].rearrange(
                                        "p (r d) -> p r d", r=2),
                                    in_=w2p_tile(hc, dc)).then_inc(ldw2p[widx % 8], 16)

        @blk.scalar
        def _(a):
            gi = 0
            for hc in range(NH0):
                for tcx in range(4):
                    a.wait_ge(s_gu, gi + 1)
                    if gi >= 1:
                        a.wait_ge(s_hb, gi)
                    a.activation(sg[:], p_g[:, (gi % 2) * 512:(gi % 2) * 512 + 512],
                                 AF.Silu).then_inc(s_sg, 1)
                    gi += 1
            a.wait_ge(sthp[0], 16 * 128)
            a.wait_ge(sthp[1], 16 * 128)
            li = 0
            for w in range(4):
                if w >= 1:
                    a.wait_ge(s_w2c, w * 4 * NH0)
                for hc in range(NH0):
                    a.dma_start(out=hw_sb[:, hc * 512:(hc + 1) * 512],
                                in_=hidT[hc * 128:(hc + 1) * 128,
                                         w * 512:(w + 1) * 512]).then_inc(ldh, 16)
                for ttl in range(4):
                    a.wait_ge(s_xm, NT + li + 1)
                    if li >= 1:
                        a.wait_ge(st2, 16 * li)
                    a.drain()
                    a.activation(z1t[:], accs[li % 4][:], AF.Square,
                                 accum_out=mcol[:, 3:4])
                    a.drain()
                    a.activation(mcol[:, 1:2], mcol[:, 3:4], AF.Sqrt, scale=INV,
                                 bias=eps_c[:]).then_inc(s_sd, 1)
                    li += 1

        @blk.vector
        def _(v):
            gi = 0
            for hc in range(NH0):
                for tcx in range(4):
                    v.wait_ge(s_sg, gi + 1)
                    if gi >= 2:
                        v.wait_ge(sthp[gi % 2], 16 * (gi // 2))
                    v.tensor_mul(hb[:, (gi % 2) * 512:(gi % 2 + 1) * 512],
                                 sg[:], p_u[:, (gi % 2) * 512:(gi % 2) * 512 + 512]
                                 ).then_inc(s_hb, 1)
                    gi += 1
            for w in range(4):
                for dc in range(4):
                    idx = w * 4 + dc
                    v.wait_ge(s_w2c, (idx + 1) * NH0)
                    v.wait_ge(ldz, 16 * (w + 1) * 4)
                    v.drain()
                    for ttl in range(4):
                        la = v.tensor_add(accs[ttl][:, dc * 512:(dc + 1) * 512],
                                          accs[ttl][:, dc * 512:(dc + 1) * 512],
                                          p_o[ttl][:])
                    la.then_inc(s_ev, 1)
                    v.drain()
                for ttl in range(4):
                    li = w * 4 + ttl
                    acc = accs[ttl]
                    v.tensor_reduce(mcol[:, 0:1], acc[:], AX.X, ALU.add)
                    v.drain()
                    v.tensor_scalar_mul(mcol[:, 0:1], mcol[:, 0:1], INV)
                    v.drain()
                    v.tensor_scalar(acc[:], acc[:], mcol[:, 0:1], None,
                                    ALU.subtract).then_inc(s_xm, 1)
                    v.wait_ge(s_sd, NT + li + 1)
                    v.reciprocal(mcol[:, 2:3], mcol[:, 1:2])
                    if li >= 1:
                        v.wait_ge(st2, 16 * li)
                    v.drain()
                    v.tensor_scalar_mul(z1t[:], acc[:], mcol[:, 2:3]).then_inc(s_ln, 1)
                    if z2tl is not None:
                        if li >= 1:
                            v.wait_ge(s_tr3, NK * li)
                        v.drain()
                        v.tensor_copy(z2b[:], z1t[:]).then_inc(s_zb, 1)

        @blk.sync
        def _(sy):
            gi = 0
            for hc in range(NH0):
                for tcx in range(4):
                    sy.wait_ge(s_hb, gi + 1)
                    sy.dma_start(out=hidT[hc * 128:(hc + 1) * 128,
                                          tcx * 512:(tcx + 1) * 512],
                                 in_=hb[:, (gi % 2) * 512:(gi % 2 + 1) * 512]
                                 ).then_inc(sthp[gi % 2], 16)
                    gi += 1
            sy.wait_ge(st, 16 * NT)
            for w in range(4):
                for ttl in range(4):
                    if w >= 1:
                        sy.wait_ge(s_ln, w * 4)
                    sy.dma_start(out=accs[ttl][:],
                                 in_=z1f[w * 512 + ttl * 128:w * 512 + (ttl + 1) * 128,
                                         :]).then_inc(ldz, 16)
                for ttl in range(4):
                    li = w * 4 + ttl
                    sy.wait_ge(s_ln, li + 1)
                    sy.dma_start(out=z2o[w * 512 + ttl * 128:w * 512 + (ttl + 1) * 128, :],
                                 in_=z1t[:]).then_inc(st2, 16)
                    if z2tl is not None:
                        li = w * 4 + ttl
                        sy.wait_ge(s_te3, NK * (li + 1))
                        sy.dma_start(out=z2tl.rearrange("(k p) t -> p k t", p=128)[
                                         :, :, li * 128:(li + 1) * 128],
                                     in_=z2t_sb[:].rearrange("p (k t) -> p k t", k=NK)
                                     ).then_inc(stz, 16)
            sy.wait_ge(st2, 16 * NT)
            if z2tl is not None:
                sy.wait_ge(stz, 16 * NT)




def build_B():
    nc = bass.Bass("TRN2", target_bir_lowering=False)
    io = dict(
        dtTs=nc.dram_tensor("dtTs", [256, TOK], F32, kind="ExternalInput").ap(),
        xTs=nc.dram_tensor("xTs", [256, TOK], F32, kind="ExternalInput").ap(),
        Ash=nc.dram_tensor("Ash", [128, 32], F32, kind="ExternalInput").ap(),
        Dsh=nc.dram_tensor("Dsh", [128, 2], F32, kind="ExternalInput").ap(),
        BT=nc.dram_tensor("BT", [NST, TOK], F32, kind="ExternalInput").ap(),
        CT=nc.dram_tensor("CT", [NST, TOK], F32, kind="ExternalInput").ap(),
        idt=nc.dram_tensor("idt", [128, 128], BF16, kind="ExternalInput").ap(),
        yTs=nc.dram_tensor("yTs", [256, TOK], F32, kind="ExternalOutput").ap(),
        ytr=None)
    with ExitStack() as ctx:
        phase_B(nc, NcProxy(nc, "B_"), ctx.enter_context, io)
    return nc


def phase_B(nc, ncx, e, io):
    dtTs = io["dtTs"]; xTs = io["xTs"]; Ash = io["Ash"]; Dsh = io["Dsh"]
    BT = io["BT"]; CT = io["CT"]; idt = io["idt"]; yTs = io["yTs"]
    ytr = io["ytr"]

    NTC = TOK // 512
    NITER = NTC * 2

    if True:
        btst = e(ncx.sbuf_tensor("btst", [1, 2 * NST * 512], BF16))
        ctst = e(ncx.sbuf_tensor("ctst", [1, 2 * NST * 512], BF16))
        alog_sb = e(ncx.sbuf_tensor("alog_sb", [128, 32], F32))
        a_sb = e(ncx.sbuf_tensor("a_sb", [128, 32], F32))
        d_sb = e(ncx.sbuf_tensor("d_sb", [128, 2], F32))
        ones1 = e(ncx.sbuf_tensor("ones1", [1, 128], BF16))
        idt_sb = e(ncx.sbuf_tensor("idt_sb", [128, 128], BF16))
        dt_t = e(ncx.sbuf_tensor("dt_t", [128, 2 * 512], F32))
        x_t = e(ncx.sbuf_tensor("x_t", [128, 2 * 512], F32))
        u_sb = e(ncx.sbuf_tensor("u_sb", [128, 512], F32))
        dA_sb = e(ncx.sbuf_tensor("dA_sb", [128, 2 * 512], F32))
        dbx_sb = e(ncx.sbuf_tensor("dbx_sb", [128, 2 * 512], F32))
        h_sb = e(ncx.sbuf_tensor("h_sb", [128, 2 * 512], F32))
        yt_bf = e(ncx.sbuf_tensor("yt_bf", [128, 2 * 512], BF16))
        yo_sb = e(ncx.sbuf_tensor("yo_sb", [128, 2 * 512], F32))
        state = e(ncx.sbuf_tensor("state", [128, 32], F32))
        if ytr is not None:
            ytr_sb = e(ncx.sbuf_tensor("ytr_sb", [128, 2 * 512], F32))
            if32 = e(ncx.sbuf_tensor("if32", [128, 128], F32))
            p_yt = e(ncx.psum_tensor("p_yt", [128, 128], F32))
            s_tr2 = e(ncx.semaphore("s_tr2"))
            s_te2 = e(ncx.semaphore("s_te2"))
        p_br = e(ncx.psum_tensor("p_br", [128, 2 * 512], F32))
        p_cr = e(ncx.psum_tensor("p_cr", [128, 3 * 512], F32))
        p_y = e(ncx.psum_tensor("p_y", [128, 2 * 512], F32))
        ld = e(ncx.semaphore("ld"))
        ldtp = [e(ncx.semaphore("ldt0")), e(ncx.semaphore("ldt1"))]
        ldbp = [e(ncx.semaphore("ldb0")), e(ncx.semaphore("ldb1"))]
        s_t = e(ncx.semaphore("s_t"))
        s_a = e(ncx.semaphore("s_a"))
        s_vb = e(ncx.semaphore("s_vb"))
        s_vs = e(ncx.semaphore("s_vs"))
        s_vy = e(ncx.semaphore("s_vy"))
        s_ty = e(ncx.semaphore("s_ty"))
        s_y = e(ncx.semaphore("s_y"))
        s_ai = e(ncx.semaphore("s_ai"))
        stpB = [e(ncx.semaphore("stB0")), e(ncx.semaphore("stB1"))]
        blk = e(nc.Block())

        @blk.sync
        def _(sy):
            sy.dma_start(out=alog_sb[:], in_=Ash[:]).then_inc(ld, 16)
            sy.dma_start(out=d_sb[:], in_=Dsh[:]).then_inc(ld, 16)
            sy.dma_start(out=idt_sb[:], in_=idt[:]).then_inc(ld, 16)

        @blk.gpsimd
        def _(g):
            for i in range(NITER):
                tc, gr = i // 2, i % 2
                if gr == 0:
                    if tc >= 2:
                        g.wait_ge(s_t, 32 * (tc - 1))
                    sb = slice((tc % 2) * NST * 512, (tc % 2 + 1) * NST * 512)
                    g.dma_start(out=btst[0:1, sb],
                                in_=BT[:, tc * 512:(tc + 1) * 512]).then_inc(ldbp[tc % 2], 16)
                    g.dma_start(out=ctst[0:1, sb],
                                in_=CT[:, tc * 512:(tc + 1) * 512]).then_inc(ldbp[tc % 2], 16)
                if i >= 2:
                    g.wait_ge(s_y, i - 1)
                b = i % 2
                g.dma_start(out=dt_t[:, b * 512:(b + 1) * 512],
                            in_=dtTs[gr * 128:(gr + 1) * 128,
                                     tc * 512:(tc + 1) * 512]).then_inc(ldtp[i % 2], 16)
                g.dma_start(out=x_t[:, b * 512:(b + 1) * 512],
                            in_=xTs[gr * 128:(gr + 1) * 128,
                                    tc * 512:(tc + 1) * 512]).then_inc(ldtp[i % 2], 16)

        @blk.tensor
        def _(t):
            t.wait_ge(s_ai, 1)
            nrep = 0
            for i in range(NITER):
                tc = i // 2
                t.wait_ge(ldbp[tc % 2], 32 * (tc // 2 + 1))
                base = (tc % 2) * NST * 512
                if i >= 2:
                    t.wait_ge(s_y, i - 1)
                for n in range(NST):
                    nb = nrep % 2
                    psl = slice(nb * 512, (nb + 1) * 512)
                    psl3 = slice((nrep % 3) * 512, (nrep % 3) * 512 + 512)
                    ssl = slice(base + n * 512, base + (n + 1) * 512)
                    if nrep >= 2:
                        t.wait_ge(s_vb, nrep - 1)
                    if nrep >= 3:
                        t.wait_ge(s_vy, nrep - 2)
                    t.matmul(p_br[:, psl], ones1[:], btst[0:1, ssl],
                             start=True, stop=True)
                    mm = t.matmul(p_cr[:, psl3], ones1[:], ctst[0:1, ssl],
                                  start=True, stop=True)
                    mm.then_inc(s_t, 1)
                    if n >= 2:
                        nn = nrep - 2
                        t.wait_ge(s_vy, nn + 1)
                        t.matmul(p_y[:, (i % 2) * 512:(i % 2 + 1) * 512],
                                 idt_sb[:], yt_bf[:, (nn % 2) * 512:(nn % 2 + 1) * 512],
                                 start=(nn % NST == 0), stop=False).then_inc(s_ty, 1)
                    nrep += 1
                for nn in range(nrep - 2, nrep):
                    t.wait_ge(s_vy, nn + 1)
                    t.matmul(p_y[:, (i % 2) * 512:(i % 2 + 1) * 512],
                             idt_sb[:], yt_bf[:, (nn % 2) * 512:(nn % 2 + 1) * 512],
                             start=False, stop=(nn % NST == NST - 1)).then_inc(s_ty, 1)
                if ytr is not None and i >= 1:
                    j = i - 1
                    t.wait_ge(s_y, j + 1)
                    for q in range(4):
                        ti = 4 * j + q
                        if ti >= 1:
                            t.wait_ge(s_te2, ti)
                        t.transpose(p_yt[:],
                                    yo_sb[:, (j % 2) * 512 + q * 128:(j % 2) * 512 + (q + 1) * 128],
                                    if32[:]).then_inc(s_tr2, 1)
            if ytr is not None:
                j = NITER - 1
                t.wait_ge(s_y, j + 1)
                for q in range(4):
                    ti = 4 * j + q
                    t.wait_ge(s_te2, ti)
                    t.transpose(p_yt[:],
                                yo_sb[:, (j % 2) * 512 + q * 128:(j % 2) * 512 + (q + 1) * 128],
                                if32[:]).then_inc(s_tr2, 1)

        @blk.scalar
        def _(a):
            a.wait_ge(ld, 48)
            a.activation(a_sb[:], alog_sb[:], AF.Exp).then_inc(s_a, 1)
            a.wait_ge(s_ai, 1)
            nrep = 0
            for i in range(NITER):
                for n in range(NST):
                    nb = nrep % 2
                    a.wait_ge(ldtp[i % 2], 32 * (i // 2 + 1))
                    if nrep >= 2:
                        a.wait_ge(s_vs, nrep - 1)
                    a.activation(dA_sb[:, nb * 512:(nb + 1) * 512],
                                 dt_t[:, (i % 2) * 512:(i % 2 + 1) * 512],
                                 AF.Exp,
                                 scale=a_sb[:, (i % 2) * 16 + n:(i % 2) * 16 + n + 1]
                                 ).then_inc(s_a, 1)
                    nrep += 1
                if ytr is not None and i >= 1:
                    j = i - 1
                    if j >= 2:
                        a.wait_ge(stpB[j % 2], 64 * (j // 2))
                    for q in range(4):
                        ti = 4 * j + q
                        a.wait_ge(s_tr2, ti + 1)
                        a.drain()
                        a.activation(ytr_sb[:, (j % 2) * 512 + q * 128:(j % 2) * 512 + (q + 1) * 128],
                                     p_yt[:],
                                     AF.Identity).then_inc(s_te2, 1)
            if ytr is not None:
                j = NITER - 1
                a.wait_ge(stpB[j % 2], 64 * (j // 2))
                for q in range(4):
                    ti = 4 * j + q
                    a.wait_ge(s_tr2, ti + 1)
                    a.drain()
                    a.activation(ytr_sb[:, (j % 2) * 512 + q * 128:(j % 2) * 512 + (q + 1) * 128],
                                 p_yt[:],
                                 AF.Identity).then_inc(s_te2, 1)

        @blk.vector
        def _(v):
            v.memset(ones1[:], 1.0)
            v.memset(state[:], 0.0)
            v.wait_ge(ld, 48)
            if ytr is not None:
                v.tensor_copy(if32[:], idt_sb[:])
            v.wait_ge(s_a, 1)
            v.tensor_scalar_mul(a_sb[:], a_sb[:], -1.0)
            v.drain()
            v.engine_nop().then_inc(s_ai, 1)
            nrep = 0
            for i in range(NITER):
                tc, gr = i // 2, i % 2
                b = i % 2
                bsl = slice(b * 512, (b + 1) * 512)
                v.wait_ge(ldtp[i % 2], 32 * (i // 2 + 1))
                v.drain()
                v.tensor_mul(u_sb[:], dt_t[:, bsl], x_t[:, bsl])
                v.drain()

                def em_dbx(n):
                    g = nrep0 + n
                    nb = g % 2
                    if g >= 2:
                        v.wait_ge(s_vs, g - 1)
                    v.wait_ge(s_t, g + 1)
                    v.tensor_mul(dbx_sb[:, nb * 512:(nb + 1) * 512], u_sb[:],
                                 p_br[:, nb * 512:(nb + 1) * 512]).then_inc(s_vb, 1)

                def em_scan(n):
                    g = nrep0 + n
                    nb = g % 2
                    scol = gr * 16 + n
                    v.wait_ge(s_vb, g + 1)
                    v.wait_ge(s_a, g + 2)
                    if g >= 2:
                        v.wait_ge(s_vy, g - 1)
                    init = 0.0 if tc % 16 == 0 else state[:, scol:scol + 1]
                    v.tensor_tensor_scan(h_sb[:, nb * 512:(nb + 1) * 512],
                                         dA_sb[:, nb * 512:(nb + 1) * 512],
                                         dbx_sb[:, nb * 512:(nb + 1) * 512],
                                         init, ALU.mult, ALU.add).then_inc(s_vs, 1)

                def em_tail(n):
                    g = nrep0 + n
                    nb = g % 2
                    scol = gr * 16 + n
                    v.wait_ge(s_vs, g + 1)
                    v.tensor_copy(state[:, scol:scol + 1],
                                  h_sb[:, nb * 512 + 511:nb * 512 + 512])
                    if g >= 2:
                        v.wait_ge(s_ty, g - 1)
                    v.tensor_mul(yt_bf[:, nb * 512:(nb + 1) * 512],
                                 h_sb[:, nb * 512:(nb + 1) * 512],
                                 p_cr[:, (g % 3) * 512:(g % 3) * 512 + 512]).then_inc(s_vy, 1)

                nrep0 = nrep
                em_dbx(0)
                em_dbx(1)
                em_scan(0)
                for n in range(2, NST):
                    em_dbx(n)
                    em_scan(n - 1)
                    em_tail(n - 2)
                em_scan(NST - 1)
                em_tail(NST - 2)
                em_tail(NST - 1)
                nrep += NST
                v.wait_ge(s_ty, nrep)
                if i >= 2:
                    if ytr is not None:
                        v.wait_ge(s_tr2, 4 * (i - 1))
                    else:
                        v.wait_ge(stpB[i % 2], 16 * (i // 2))
                v.drain()
                v.scalar_tensor_tensor(yo_sb[:, bsl], x_t[:, bsl],
                                       d_sb[:, gr:gr + 1],
                                       p_y[:, b * 512:(b + 1) * 512],
                                       ALU.mult, ALU.add).then_inc(s_y, 1)

        @blk.sync
        def _(sy):
            if ytr is None:
                for i in range(NITER):
                    tc, gr = i // 2, i % 2
                    b = i % 2
                    sy.wait_ge(s_y, i + 1)
                    sy.dma_start(out=yTs[gr * 128:(gr + 1) * 128,
                                         tc * 512:(tc + 1) * 512],
                                 in_=yo_sb[:, b * 512:(b + 1) * 512]).then_inc(stpB[i % 2], 16)
                sy.wait_ge(stpB[0], 16 * NITER // 2)
                sy.wait_ge(stpB[1], 16 * NITER // 2)
            else:
                for j in range(NITER):
                    tc, gr = j // 2, j % 2
                    for q in range(4):
                        ti = 4 * j + q
                        sy.wait_ge(s_te2, ti + 1)
                        sy.dma_start(out=ytr[tc * 512 + q * 128:tc * 512 + (q + 1) * 128,
                                             gr * 128:(gr + 1) * 128],
                                     in_=ytr_sb[:, (j % 2) * 512 + q * 128:(j % 2) * 512 + (q + 1) * 128]
                                     ).then_inc(stpB[j % 2], 16)
                sy.wait_ge(stpB[0], 64 * NITER // 2)
                sy.wait_ge(stpB[1], 64 * NITER // 2)




def build_D():
    nc = bass.Bass("TRN2", target_bir_lowering=False)
    z2T = nc.dram_tensor("z2T", [DM, SEQ], F32, kind="ExternalInput").ap()
    io = dict(
        wq=nc.dram_tensor("wq", [DM, 512], F32, kind="ExternalInput").ap(),
        wk=nc.dram_tensor("wk", [DM, 512], F32, kind="ExternalInput").ap(),
        wv=nc.dram_tensor("wv", [DM, 512], F32, kind="ExternalInput").ap(),
        masks=nc.dram_tensor("masks", [128, 5 * 512], BF16, kind="ExternalInput").ap(),
        Oc=nc.dram_tensor("Oc", [512, SEQ], BF16, kind="ExternalOutput").ap(),
        z2w_tile=lambda k, w8: z2T[k * 128:(k + 1) * 128,
                                   w8 * 1024:(w8 + 1) * 1024])
    with ExitStack() as ctx:
        phase_D(nc, NcProxy(nc, "D_"), ctx.enter_context, io)
    return nc


def phase_D(nc, ncx, e, io):
    wq = io["wq"]; wk = io["wk"]; wv = io["wv"]; masks = io["masks"]
    Oc = io["Oc"]; z2w_tile = io["z2w_tile"]

    NK = DM // 128
    NQC = SEQ // 512
    SCL = float(DH) ** -0.5

    if True:
        z2w = e(ncx.sbuf_tensor("z2w", [128, 2 * NK * 1024], BF16))
        wq_sb = e(ncx.sbuf_tensor("wq_sb", [128, NK * 128], BF16))
        wk_sb = e(ncx.sbuf_tensor("wk_sb", [128, NK * 128], BF16))
        wv_sb = e(ncx.sbuf_tensor("wv_sb", [128, NK * 128], BF16))
        QT = e(ncx.sbuf_tensor("QT", [128, SEQ], BF16))
        KT = e(ncx.sbuf_tensor("KT", [128, SEQ], BF16))
        V = e(ncx.sbuf_tensor("V", [128, SEQ], BF16))
        msk = e(ncx.sbuf_tensor("msk", [128, 5 * 512], BF16))
        pU = e(ncx.sbuf_tensor("pU", [128, 2 * 512], BF16))
        pM = e(ncx.sbuf_tensor("pM", [128, 4 * 512], BF16))
        rs_sb = e(ncx.sbuf_tensor("rs_sb", [1, 512], F32))
        rc_sb = e(ncx.sbuf_tensor("rc_sb", [1, 512], F32))
        vsum = e(ncx.sbuf_tensor("vsum", [128, 2 * 512], F32))
        ones_kf = e(ncx.sbuf_tensor("ones_kf", [128, 1], F32))
        rep_sb = e(ncx.sbuf_tensor("rep_sb", [128, 512], F32))
        Ob = e(ncx.sbuf_tensor("Ob", [128, 2 * 512], BF16))
        n20 = e(ncx.sbuf_tensor("n20", [128, 1], F32))
        ones_k = e(ncx.sbuf_tensor("ones_k", [128, 1], BF16))
        ones_r = e(ncx.sbuf_tensor("ones_r", [1, 128], F32))
        p_q = e(ncx.psum_tensor("p_q", [128, 2 * 512], F32))
        p_s = e(ncx.psum_tensor("p_s", [128, 2 * 512], F32))
        p_rs = e(ncx.psum_tensor("p_rs", [1, 512], F32))
        p_o = e(ncx.psum_tensor("p_o", [128, 512], F32))
        p_rep = e(ncx.psum_tensor("p_rep", [128, 512], F32))
        ld0 = e(ncx.semaphore("ld0"))
        ldz = e(ncx.semaphore("ldz"))
        ldw = e(ncx.semaphore("ldw"))
        s_qkv = e(ncx.semaphore("s_qkv"))
        s_qe = e(ncx.semaphore("s_qe"))
        s_s = e(ncx.semaphore("s_s"))
        s_p = e(ncx.semaphore("s_p"))
        s_pm = e(ncx.semaphore("s_pm"))
        s_av = e(ncx.semaphore("s_av"))
        s_vv = e(ncx.semaphore("s_vv"))
        s_vs = e(ncx.semaphore("s_vs"))
        s_rc = e(ncx.semaphore("s_rc"))
        s_rp = e(ncx.semaphore("s_rp"))
        s_oe = e(ncx.semaphore("s_oe"))
        s_vc = e(ncx.semaphore("s_vc"))
        stpD = [e(ncx.semaphore("stD0")), e(ncx.semaphore("stD1"))]
        blk = e(nc.Block())

        def sched():
            ai = 0
            diag_ai = []
            plan = []
            for h in range(4):
                for qc in range(NQC):
                    for kt in range(4 * qc + 4):
                        diag = kt >= 4 * qc
                        di = len(diag_ai) if diag else -1
                        if diag:
                            diag_ai.append(ai)
                        plan.append((h, qc, kt, diag, ai, di))
                        ai += 1
            return plan, diag_ai
        PLAN, DIAG_AI = sched()
        NAI = len(PLAN)

        @blk.sync
        def _(sy):
            sy.dma_start(out=msk[:], in_=masks[:]).then_inc(ld0, 16)
            oi = 0
            for h in range(4):
                for qc in range(NQC):
                    sy.wait_ge(s_oe, oi + 1)
                    sy.dma_start(out=Oc[h * 128:(h + 1) * 128,
                                        qc * 512:(qc + 1) * 512],
                                 in_=Ob[:, (oi % 2) * 512:(oi % 2 + 1) * 512]
                                 ).then_inc(stpD[oi % 2], 16)
                    oi += 1
            sy.wait_ge(stpD[0], 16 * 32)
            sy.wait_ge(stpD[1], 16 * 32)

        @blk.gpsimd
        def _(g):
            for h in range(4):
                if h >= 1:
                    g.wait_ge(s_qkv, 96 * h)
                g.dma_start(out=wq_sb[:].rearrange("p (k j) -> p k j", k=NK),
                            in_=wq.rearrange("(k p) j -> p k j", p=128)[
                                :, :, h * 128:(h + 1) * 128]).then_inc(ldw, 16)
                g.dma_start(out=wk_sb[:].rearrange("p (k j) -> p k j", k=NK),
                            in_=wk.rearrange("(k p) j -> p k j", p=128)[
                                :, :, h * 128:(h + 1) * 128]).then_inc(ldw, 16)
                g.dma_start(out=wv_sb[:].rearrange("p (k j) -> p k j", k=NK),
                            in_=wv.rearrange("(k p) j -> p k j", p=128)[
                                :, :, h * 128:(h + 1) * 128]).then_inc(ldw, 16)
                for w8 in range(8):
                    gw = 8 * h + w8
                    if gw >= 2:
                        g.wait_ge(s_qkv, 12 * (gw - 1))
                    zb = (gw % 2) * NK * 1024
                    for k in range(NK):
                        g.dma_start(out=z2w[:, zb + k * 1024:zb + (k + 1) * 1024],
                                    in_=z2w_tile(k, w8)).then_inc(ldz, 16)

        @blk.tensor
        def _(t):
            t.wait_ge(ld0, 16)
            t.wait_ge(s_vc, 1)
            qi = 0
            for h in range(4):
                t.wait_ge(ldw, 48 * (h + 1))
                for w8 in range(8):
                    gw = 8 * h + w8
                    t.wait_ge(ldz, 256 * (gw + 1))
                    zb = (gw % 2) * NK * 1024
                    for which, wsb in ((0, wq_sb), (1, wk_sb)):
                        for tcq in range(2):
                            if qi >= 2:
                                t.wait_ge(s_qe, qi - 1)
                            pb = (qi % 2) * 512
                            for k in range(NK):
                                mm = t.matmul(
                                    p_q[:, pb:pb + 512],
                                    wsb[:, k * 128:(k + 1) * 128],
                                    z2w[:, zb + k * 1024 + tcq * 512:zb + k * 1024 + tcq * 512 + 512],
                                    start=(k == 0), stop=(k == NK - 1))
                            mm.then_inc(s_qkv, 1)
                            qi += 1
                    for tt in range(8):
                        if qi >= 2:
                            t.wait_ge(s_qe, qi - 1)
                        pb = (qi % 2) * 512
                        for k in range(NK):
                            mm = t.matmul(
                                p_q[:, pb:pb + 128],
                                z2w[:, zb + k * 1024 + tt * 128:zb + k * 1024 + (tt + 1) * 128],
                                wv_sb[:, k * 128:(k + 1) * 128],
                                start=(k == 0), stop=(k == NK - 1))
                        mm.then_inc(s_qkv, 1)
                        qi += 1
                for qc in range(NQC):
                    gqc = h * NQC + qc
                    if gqc >= 1:
                        t.wait_ge(s_oe, gqc)
                    nkt = 4 * qc + 4
                    base = AI_BASE[(h, qc)]

                    def emit_scores(kt):
                        ai = base + kt
                        if ai >= 2:
                            t.wait_ge(s_p, ai - 1)
                        sb_ = (ai % 2) * 512
                        t.matmul(p_s[:, sb_:sb_ + 512],
                                 KT[:, kt * 128:(kt + 1) * 128],
                                 QT[:, qc * 512:(qc + 1) * 512],
                                 start=True, stop=True).then_inc(s_s, 1)

                    def emit_av(kt):
                        ai = base + kt
                        if kt >= 4 * qc:
                            di = DI_OF[ai]
                            t.wait_ge(s_pm, di + 1)
                            rhs = pM[:, (di % 4) * 512:(di % 4 + 1) * 512]
                        else:
                            t.wait_ge(s_p, ai + 1)
                            rhs = pU[:, (ai % 2) * 512:(ai % 2 + 1) * 512]
                        mm = t.matmul(p_o[:], V[:, kt * 128:(kt + 1) * 128], rhs,
                                      start=(kt == 0), stop=(kt == nkt - 1))
                        mm.then_inc(s_av, 1)

                    emit_scores(0)
                    for kt in range(1, nkt):
                        emit_scores(kt)
                        emit_av(kt - 1)
                    emit_av(nkt - 1)
                    t.wait_ge(s_vs, gqc + 1)
                    t.matmul(p_rs[:], ones_kf[:], vsum[:, 0:512],
                             start=True, stop=True).then_inc(s_s, 1)
                    t.wait_ge(s_rc, gqc + 1)
                    t.matmul(p_rep[:], ones_r[:], rc_sb[:],
                             start=True, stop=True).then_inc(s_rp, 1)

        @blk.scalar
        def _(a):
            qi = 0
            for h in range(4):
                for w8 in range(8):
                    for which in range(2):
                        for tcq in range(2):
                            a.wait_ge(s_qkv, qi + 1)
                            dst = (QT if which == 0 else KT)[
                                :, w8 * 1024 + tcq * 512:w8 * 1024 + tcq * 512 + 512]
                            pb = (qi % 2) * 512
                            if which == 0:
                                a.activation(dst, p_q[:, pb:pb + 512],
                                             AF.Identity, scale=SCL).then_inc(s_qe, 1)
                            else:
                                a.activation(dst, p_q[:, pb:pb + 512],
                                             AF.Identity).then_inc(s_qe, 1)
                            qi += 1
                    for tt in range(8):
                        a.wait_ge(s_qkv, qi + 1)
                        pb = (qi % 2) * 512
                        a.activation(V[:, (w8 * 8 + tt) * 128:(w8 * 8 + tt + 1) * 128],
                                     p_q[:, pb:pb + 128],
                                     AF.Identity).then_inc(s_qe, 1)
                        qi += 1
                for qc in range(NQC):
                    nkt = 4 * qc + 4
                    for kt in range(nkt):
                        ai = AI_BASE[(h, qc)] + kt + (h * NQC + qc)
                        ai0 = AI_BASE[(h, qc)] + kt
                        a.wait_ge(s_s, ai + 1)
                        if ai0 >= 2:
                            a.wait_ge(s_av, ai0 - 1)
                            a.wait_ge(s_vv, ai0 - 1)
                        sb_ = (ai0 % 2) * 512
                        a.activation(pU[:, (ai0 % 2) * 512:(ai0 % 2 + 1) * 512],
                                     p_s[:, sb_:sb_ + 512], AF.Exp,
                                     bias=n20[:]).then_inc(s_p, 1)

        @blk.vector
        def _(v):
            v.memset(n20[:], -20.0)
            v.memset(ones_k[:], 1.0)
            v.memset(ones_kf[:], 1.0)
            v.memset(ones_r[:], 1.0).then_inc(s_vc, 1)
            v.wait_ge(ld0, 16)
            for h in range(4):
                for qc in range(NQC):
                    gqc = h * NQC + qc
                    nkt = 4 * qc + 4
                    for kt in range(nkt):
                        ai = AI_BASE[(h, qc)] + kt
                        sp_ai = ai + gqc
                        diag = kt >= 4 * qc
                        v.wait_ge(s_p, ai + 1)
                        if diag:
                            di = DI_OF[ai]
                            r = kt - 4 * qc
                            mc = 4 if (qc == 0 and kt == 0) else r
                            if di >= 4:
                                v.wait_ge(s_av, DIAG_AI[di - 4] + 1)
                            v.tensor_mul(pM[:, (di % 4) * 512:(di % 4 + 1) * 512],
                                         pU[:, (ai % 2) * 512:(ai % 2 + 1) * 512],
                                         msk[:, mc * 512:(mc + 1) * 512]).then_inc(s_pm, 1)
                            v.drain()
                            src_ap = pM[:, (di % 4) * 512:(di % 4 + 1) * 512]
                        else:
                            src_ap = pU[:, (ai % 2) * 512:(ai % 2 + 1) * 512]
                        half = slice((kt % 2) * 512, (kt % 2) * 512 + 512)
                        if kt < 2:
                            if gqc >= 1 and kt == 0:
                                v.wait_ge(s_s, AI_BASE[(h, qc)] + gqc)
                            v.tensor_copy(vsum[:, half], src_ap).then_inc(s_vv, 1)
                        else:
                            v.wait_ge(s_vv, AI_BASE[(h, qc)] + kt - 1)
                            v.tensor_add(vsum[:, half], vsum[:, half],
                                         src_ap).then_inc(s_vv, 1)
                        if kt == nkt - 1:
                            v.drain()
                            v.tensor_add(vsum[:, 0:512], vsum[:, 0:512],
                                         vsum[:, 512:1024])
                            v.drain()
                            v.engine_nop().then_inc(s_vs, 1)
                    v.wait_ge(s_s, AI_BASE[(h, qc)] + nkt + gqc + 1)
                    v.drain()
                    v.tensor_copy(rs_sb[:], p_rs[:])
                    v.drain()
                    v.reciprocal(rc_sb[:], rs_sb[:]).then_inc(s_rc, 1)
                    v.wait_ge(s_rp, gqc + 1)
                    v.drain()
                    v.tensor_copy(rep_sb[:], p_rep[:])
                    ai_last = AI_BASE[(h, qc)] + nkt - 1
                    v.wait_ge(s_av, ai_last + 1)
                    if gqc >= 2:
                        v.wait_ge(stpD[gqc % 2], 16 * (gqc // 2))
                    v.drain()
                    v.tensor_mul(Ob[:, (gqc % 2) * 512:(gqc % 2 + 1) * 512],
                                 p_o[:], rep_sb[:]).then_inc(s_oe, 1)


AI_BASE = {}
DI_OF = {}
def _init_attn_sched():
    ai = 0
    di = 0
    for h in range(4):
        for qc in range(16):
            AI_BASE[(h, qc)] = ai
            for kt in range(4 * qc + 4):
                if kt >= 4 * qc:
                    DI_OF[ai] = di
                    di += 1
                ai += 1
_init_attn_sched()




def build_F():
    nc = bass.Bass("TRN2", target_bir_lowering=False)
    OTs = nc.dram_tensor("OTs", [DM, TSL], BF16, kind="ExternalInput").ap()
    wguT = nc.dram_tensor("wguT", [DM, 2 * FI], BF16, kind="ExternalInput").ap()
    wdnT = nc.dram_tensor("wdnT", [FI, DM], BF16, kind="ExternalInput").ap()
    io = dict(
        z2s=nc.dram_tensor("z2s", [TSL, DM], F32, kind="ExternalInput").ap(),
        wout=nc.dram_tensor("wout", [DM, DM], F32, kind="ExternalInput").ap(),
        boutr=nc.dram_tensor("boutr", [128, DM], F32, kind="ExternalInput").ap(),
        idt=nc.dram_tensor("idt", [128, 128], BF16, kind="ExternalInput").ap(),
        outp=nc.dram_tensor("outp", [TSL, DM], F32, kind="ExternalOutput").ap(),
        x3f=nc.dram_tensor("x3f", [TSL, DM], F32, kind="Internal").ap(),
        x3Tf=nc.dram_tensor("x3Tf", [DM, TSL], BF16, kind="Internal").ap(),
        ot_tile=lambda eng, tt: OTs.rearrange("(k p) t -> p k t", p=128)[
            :, :, tt * 128:(tt + 1) * 128],
        wg_tile=lambda pc: wguT.rearrange("(k p) o -> p k o", p=128)[
            :, :, pc * 128:(pc + 1) * 128],
        wu_tile=lambda pc: wguT.rearrange("(k p) o -> p k o", p=128)[
            :, :, FI + pc * 128:FI + (pc + 1) * 128],
        wd_tile=lambda ic, dc: wdnT[ic * 128:(ic + 1) * 128,
                                    dc * 512:(dc + 1) * 512].rearrange(
                                        "p (r d) -> p r d", r=2))
    with ExitStack() as ctx:
        phase_F(nc, NcProxy(nc, "F_"), ctx.enter_context, io)
    return nc


def phase_F(nc, ncx, e, io):
    cut = io.get("cut")
    z2s = io["z2s"]; wout = io["wout"]; boutr = io["boutr"]; idt = io["idt"]
    outp = io["outp"]; x3f = io["x3f"]; x3Tf = io["x3Tf"]
    ot_tile = io["ot_tile"]; wg_tile = io["wg_tile"]; wu_tile = io["wu_tile"]
    wd_tile = io["wd_tile"]

    NT = TSL // 128
    NK = DM // 128
    NPC = FI // 128
    INV = 1.0 / DM
    import os as _osf
    NWN = int(_osf.environ.get("MEGA_FNWN", "4"))
    WL = [int(s) for s in _osf.environ.get(
        "MEGA_FWL", ",".join(str(i) for i in range(NWN))).split(",")]

    if True:
        big = e(ncx.sbuf_tensor("big", [128, NK * DM], BF16))
        x3w = e(ncx.sbuf_tensor("x3w", [128, NK * 512], BF16))
        x3ev = e(ncx.sbuf_tensor("x3ev", [128, NK * 128], BF16))
        otile = e(ncx.sbuf_tensor("otile", [128, NK * 128], BF16))
        z2t = e(ncx.sbuf_tensor("z2t", [128, DM], F32))
        xrd = e(ncx.sbuf_tensor("xrd", [128, DM], F32))
        x3b = e(ncx.sbuf_tensor("x3b", [128, DM], BF16))
        accs = [e(ncx.sbuf_tensor(f"acc{i}", [128, DM], F32)) for i in range(4)]
        outt = e(ncx.sbuf_tensor("outt", [128, 2 * DM], F32))
        wg_sb = e(ncx.sbuf_tensor("wg_sb", [128, 4 * NK * 128], BF16))
        wu_sb = e(ncx.sbuf_tensor("wu_sb", [128, 4 * NK * 128], BF16))
        wd_sb = e(ncx.sbuf_tensor("wd_sb", [128, 8 * 512], BF16))
        sg = e(ncx.sbuf_tensor("sg", [128, 512], BF16))
        mcol = e(ncx.sbuf_tensor("mcol", [128, 4], F32))
        eps5 = e(ncx.sbuf_tensor("eps5", [128, 1], F32))
        eps6 = e(ncx.sbuf_tensor("eps6", [128, 1], F32))
        idt_sb = e(ncx.sbuf_tensor("idt_sb", [128, 128], BF16))
        p_op = e(ncx.psum_tensor("p_op", [128, 512], F32))
        p_tr = e(ncx.psum_tensor("p_tr", [128, 128], BF16))
        p_g = e(ncx.psum_tensor("p_g", [128, 512], F32))
        p_u = e(ncx.psum_tensor("p_u", [128, 512], F32))
        p_d = [e(ncx.psum_tensor(f"p_d{i}", [128, 512], F32)) for i in range(4)]
        ld0 = e(ncx.semaphore("ld0"))
        ldwt = e(ncx.semaphore("ldwt"))
        ld1 = e(ncx.semaphore("ld1"))
        ldgp = [e(ncx.semaphore("ldgA")), e(ncx.semaphore("ldgB")),
                e(ncx.semaphore("ldgC")), e(ncx.semaphore("ldgD"))]
        lddp = [e(ncx.semaphore("ldd%d" % qq)) for qq in range(8)]
        ldx = e(ncx.semaphore("ldx"))
        s_op = e(ncx.semaphore("s_op"))
        s_ev1 = e(ncx.semaphore("s_ev1"))
        s_e1 = e(ncx.semaphore("s_e1"))
        s_xm = e(ncx.semaphore("s_xm"))
        s_sd = e(ncx.semaphore("s_sd"))
        s_tr = e(ncx.semaphore("s_tr"))
        s_te = e(ncx.semaphore("s_te"))
        s_gu = e(ncx.semaphore("s_gu"))
        s_sg = e(ncx.semaphore("s_sg"))
        s_hb = e(ncx.semaphore("s_hb"))
        s_dc = e(ncx.semaphore("s_dc"))
        s_ev2 = e(ncx.semaphore("s_ev2"))
        s_rm = e(ncx.semaphore("s_rm"))
        s_fin = e(ncx.semaphore("s_fin"))
        st1 = e(ncx.semaphore("st1"))
        stx = e(ncx.semaphore("stx"))
        ldxw = e(ncx.semaphore("ldxw"))
        st2p = [e(ncx.semaphore("st2A")), e(ncx.semaphore("st2B"))]
        blk = e(nc.Block())

        @blk.sync
        def _(sy):
            sy.dma_start(out=idt_sb[:], in_=idt[:]).then_inc(ld0, 16)
            sy.dma_start(out=xrd[:], in_=boutr[:]).then_inc(ld0, 16)
            if cut == "ld":
                sy.wait_ge(ld0, 32)
                return
            if cut == "ld2":
                for tt in range(NT):
                    sy.dma_start(out=otile[:].rearrange("p (k t) -> p k t", k=NK),
                                 in_=ot_tile(sy, tt)).then_inc(ld1, 16)
                    sy.dma_start(out=z2t[:],
                                 in_=z2s[tt * 128:(tt + 1) * 128, :]).then_inc(ld1, 16)
                sy.wait_ge(ld1, 32 * NT)
                return
            for tt in range(NT):
                if tt >= 1:
                    sy.wait_ge(s_e1, tt)
                sy.dma_start(out=otile[:].rearrange("p (k t) -> p k t", k=NK),
                             in_=ot_tile(sy, tt)).then_inc(ld1, 16)
                sy.dma_start(out=z2t[:],
                             in_=z2s[tt * 128:(tt + 1) * 128, :]).then_inc(ld1, 16)
                if cut in ("op", "mm"):
                    sy.wait_ge(s_e1, tt + 1)
                    continue
                if tt >= 1:
                    sy.wait_ge(s_te, NK * tt)
                    sy.dma_start(out=x3Tf.rearrange("(k p) t -> p k t", p=128)[
                                     :, :, (tt - 1) * 128:tt * 128],
                                 in_=x3ev[:].rearrange("p (k t) -> p k t", k=NK)
                                 ).then_inc(stx, 16)
                sy.wait_ge(s_e1, tt + 1)
                sy.dma_start(out=x3f[tt * 128:(tt + 1) * 128, :],
                             in_=accs[0][:]).then_inc(st1, 16)
            if cut not in ("op", "mm"):
                sy.wait_ge(s_te, NK * NT)
                sy.dma_start(out=x3Tf.rearrange("(k p) t -> p k t", p=128)[
                                 :, :, (NT - 1) * 128:NT * 128],
                             in_=x3ev[:].rearrange("p (k t) -> p k t", k=NK)
                             ).then_inc(stx, 16)
            if cut in ("op", "mm"):
                return
            sy.wait_ge(st1, 16 * NT)
            sy.wait_ge(stx, 16 * NT)
            if cut == "p1":
                return
            li = 0
            sy.dma_start(out=x3w[:].rearrange("p (k t) -> p k t", k=NK),
                         in_=x3Tf.rearrange("(k p) t -> p k t", p=128)[
                             :, :, WL[0] * 512:(WL[0] + 1) * 512]).then_inc(ldxw, 16)
            for idx, wn in enumerate(WL):
                if idx + 1 < len(WL):
                    wn2 = WL[idx + 1]
                    sy.wait_ge(s_gu, (idx + 1) * NPC)
                    sy.dma_start(out=x3w[:].rearrange("p (k t) -> p k t", k=NK),
                                 in_=x3Tf.rearrange("(k p) t -> p k t", p=128)[
                                     :, :, wn2 * 512:(wn2 + 1) * 512]).then_inc(ldxw, 16)
                for ttl in range(4):
                    if li >= 1:
                        sy.wait_ge(s_xm, NT + li)
                    sy.dma_start(out=xrd[:],
                                 in_=x3f[wn * 512 + ttl * 128:wn * 512 + (ttl + 1) * 128, :]
                                 ).then_inc(ldx, 16)
                    sy.wait_ge(s_fin, li + 1)
                    sy.dma_start(out=outp[wn * 512 + ttl * 128:wn * 512 + (ttl + 1) * 128, :],
                                 in_=outt[:, (li % 2) * DM:(li % 2 + 1) * DM]
                                 ).then_inc(st2p[li % 2], 16)
                    li += 1
            sy.wait_ge(st2p[0], 16 * len(WL) * 2)
            sy.wait_ge(st2p[1], 16 * len(WL) * 2)

        @blk.gpsimd
        def _(g):
            g.dma_start(out=big[:].rearrange("p (k d) -> p k d", k=NK),
                        in_=wout.rearrange("(k p) d -> p k d", p=128)).then_inc(ldwt, 16)
            if cut in ("ld", "ld2"):
                g.wait_ge(ldwt, 16)
                return
            if cut in ("p1", "op", "mm"):
                return
            for idx, wn in enumerate(WL):
                for pc in range(NPC):
                    gi = idx * NPC + pc
                    if gi >= 4:
                        g.wait_ge(s_gu, gi - 3)
                    bb = (gi % 4) * NK * 128
                    g.dma_start(out=wg_sb[:, bb:bb + NK * 128].rearrange(
                                    "p (k j) -> p k j", k=NK),
                                in_=wg_tile(pc)).then_inc(ldgp[gi % 4], 16)
                    g.dma_start(out=wu_sb[:, bb:bb + NK * 128].rearrange(
                                    "p (k j) -> p k j", k=NK),
                                in_=wu_tile(pc)).then_inc(ldgp[gi % 4], 16)

        @blk.tensor
        def _(t):
            if cut in ("ld", "ld2"):
                return
            t.wait_ge(ld0, 32)
            t.wait_ge(ldwt, 16)
            n = 0
            for tt in range(NT):
                t.wait_ge(ld1, 32 * (tt + 1))
                for dc in range(4):
                    if tt + dc >= 1:
                        t.wait_ge(s_ev1, 4 * tt + dc)
                    for k in range(NK):
                        mm = t.matmul(p_op[:],
                                      otile[:, k * 128:(k + 1) * 128],
                                      big[:, k * DM + dc * 512:k * DM + dc * 512 + 512],
                                      start=(k == 0), stop=(k == NK - 1))
                    mm.then_inc(s_op, 1)
                if cut in ("op", "mm"):
                    continue
                if tt >= 1:
                    t.wait_ge(s_e1, tt)
                    for k in range(NK):
                        if n >= 1:
                            t.wait_ge(s_te, n)
                        mm = t.transpose(p_tr[:], x3b[:, k * 128:(k + 1) * 128],
                                         idt_sb[:])
                        mm.then_inc(s_tr, 1)
                        n += 1
            if cut not in ("op", "mm"):
                t.wait_ge(s_e1, NT)
                for k in range(NK):
                    if n >= 1:
                        t.wait_ge(s_te, n)
                    mm = t.transpose(p_tr[:], x3b[:, k * 128:(k + 1) * 128],
                                     idt_sb[:])
                    mm.then_inc(s_tr, 1)
                    n += 1
            if cut in ("p1", "op", "mm"):
                return
            for idx, wn in enumerate(WL):
                t.wait_ge(ldxw, 16 * (idx + 1))
                for pc in range(NPC):
                    gi = idx * NPC + pc
                    t.wait_ge(ldgp[gi % 4], 32 * (gi // 4 + 1))
                    if gi >= 1:
                        t.wait_ge(s_hb, gi)
                    bb = (gi % 4) * NK * 128
                    for k in range(NK):
                        t.matmul(p_g[:], wg_sb[:, bb + k * 128:bb + (k + 1) * 128],
                                 x3w[:, k * 512:(k + 1) * 512],
                                 start=(k == 0), stop=(k == NK - 1))
                    for k in range(NK):
                        mm = t.matmul(p_u[:], wu_sb[:, bb + k * 128:bb + (k + 1) * 128],
                                      x3w[:, k * 512:(k + 1) * 512],
                                      start=(k == 0), stop=(k == NK - 1))
                    mm.then_inc(s_gu, 1)
                for dc in range(4):
                    wdi = idx * 4 + dc
                    if wdi >= 1:
                        t.wait_ge(s_ev2, wdi)
                    for ic in range(NPC):
                        widx = wdi * NPC + ic
                        t.wait_ge(lddp[widx % 8], 16 * (widx // 8 + 1))
                        t.wait_ge(s_hb, idx * NPC + ic + 1)
                        wb = (widx % 8) * 512
                        for ttl in range(4):
                            mm = t.matmul(p_d[ttl][:],
                                          big[:, ic * 512 + ttl * 128:ic * 512 + (ttl + 1) * 128],
                                          wd_sb[:, wb:wb + 512],
                                          start=(ic == 0), stop=(ic == NPC - 1))
                        mm.then_inc(s_dc, 1)

        @blk.scalar
        def _(a):
            if cut in ("ld", "ld2", "op", "mm"):
                return
            n = 0
            for tt in range(NT):
                a.wait_ge(s_xm, tt + 1)
                if tt >= 1:
                    a.wait_ge(st1, 16 * tt)
                a.drain()
                a.activation(z2t[:], accs[0][:], AF.Square, accum_out=mcol[:, 3:4])
                a.drain()
                a.activation(mcol[:, 1:2], mcol[:, 3:4], AF.Sqrt, scale=INV,
                             bias=eps5[:]).then_inc(s_sd, 1)
                if tt >= 1:
                    a.wait_ge(stx, 16 * tt)
                for k in range(NK):
                    a.wait_ge(s_tr, n + 1)
                    a.drain()
                    a.activation(x3ev[:, k * 128:(k + 1) * 128],
                                 p_tr[:], AF.Identity).then_inc(s_te, 1)
                    n += 1
            if cut == "p1":
                return
            for idx, wn in enumerate(WL):
                for pc in range(NPC):
                    gi = idx * NPC + pc
                    a.wait_ge(s_gu, gi + 1)
                    if gi >= 1:
                        a.wait_ge(s_hb, gi)
                    a.drain()
                    a.activation(sg[:], p_g[:], AF.Silu).then_inc(s_sg, 1)
                for dc in range(4):
                    for ic in range(NPC):
                        widx = (idx * 4 + dc) * NPC + ic
                        if widx >= 8:
                            a.wait_ge(s_dc, widx - 7)
                        wb = (widx % 8) * 512
                        a.dma_start(out=wd_sb[:, wb:wb + 512].rearrange(
                                        "p (r d) -> p r d", r=2),
                                    in_=wd_tile(ic, dc)).then_inc(lddp[widx % 8], 16)
                for ttl in range(4):
                    li = idx * 4 + ttl
                    a.wait_ge(s_ev2, 4 * (idx + 1))
                    a.wait_ge(s_fin, li)
                    a.drain()
                    a.activation(z2t[:], accs[ttl][:], AF.Square,
                                 accum_out=mcol[:, 3:4])
                    a.drain()
                    a.activation(mcol[:, 1:2], mcol[:, 3:4], AF.Sqrt, scale=1.0 / DM,
                                 bias=eps6[:]).then_inc(s_rm, 1)
                    a.wait_ge(s_xm, NT + li + 1)
                    a.drain()
                    a.activation(z2t[:], accs[ttl][:], AF.Square,
                                 accum_out=mcol[:, 3:4])
                    a.drain()
                    a.activation(mcol[:, 1:2], mcol[:, 3:4], AF.Sqrt, scale=INV,
                                 bias=eps5[:]).then_inc(s_sd, 1)

        @blk.vector
        def _(v):
            if cut in ("ld", "ld2"):
                return
            v.memset(eps5[:], 1e-5)
            v.memset(eps6[:], 1e-6)
            for tt in range(NT):
                acc = accs[0]
                if tt >= 1 and cut not in ("op", "mm"):
                    v.wait_ge(st1, 16 * tt)
                for dc in range(4):
                    v.wait_ge(s_op, 4 * tt + dc + 1)
                    v.drain()
                    if cut == "mm":
                        v.engine_nop().then_inc(s_ev1, 1)
                        continue
                    v.tensor_add(acc[:, dc * 512:(dc + 1) * 512], p_op[:],
                                 xrd[:, dc * 512:(dc + 1) * 512]).then_inc(s_ev1, 1)
                v.wait_ge(ld1, 32 * (tt + 1))
                v.drain()
                if cut == "mm":
                    v.engine_nop().then_inc(s_e1, 1)
                    continue
                v.tensor_add(acc[:], acc[:], z2t[:])
                v.drain()
                if cut == "op":
                    v.engine_nop().then_inc(s_e1, 1)
                    continue
                v.tensor_reduce(mcol[:, 0:1], acc[:], AX.X, ALU.add)
                v.drain()
                v.tensor_scalar_mul(mcol[:, 0:1], mcol[:, 0:1], INV)
                v.drain()
                v.tensor_scalar(acc[:], acc[:], mcol[:, 0:1], None,
                                ALU.subtract).then_inc(s_xm, 1)
                v.wait_ge(s_sd, tt + 1)
                v.reciprocal(mcol[:, 2:3], mcol[:, 1:2])
                if tt >= 1:
                    v.wait_ge(s_tr, NK * tt)
                v.drain()
                v.tensor_scalar_mul(acc[:], acc[:], mcol[:, 2:3])
                v.drain()
                v.tensor_copy(x3b[:], acc[:]).then_inc(s_e1, 1)
            if cut in ("p1", "op", "mm"):
                return
            for idx, wn in enumerate(WL):
                for pc in range(NPC):
                    gi = idx * NPC + pc
                    v.wait_ge(s_sg, gi + 1)
                    if idx >= 1 and pc == 0:
                        v.wait_ge(s_dc, idx * 4 * NPC)
                    v.drain()
                    v.tensor_mul(big[:, pc * 512:(pc + 1) * 512], sg[:], p_u[:]
                                 ).then_inc(s_hb, 1)
                for dc in range(4):
                    wdi = idx * 4 + dc
                    v.wait_ge(s_dc, (wdi + 1) * NPC)
                    v.drain()
                    for ttl in range(4):
                        la = v.tensor_copy(accs[ttl][:, dc * 512:(dc + 1) * 512],
                                           p_d[ttl][:])
                    la.then_inc(s_ev2, 1)
                    v.drain()
                for ttl in range(4):
                    li = idx * 4 + ttl
                    acc = accs[ttl]
                    v.wait_ge(s_rm, li + 1)
                    v.reciprocal(mcol[:, 2:3], mcol[:, 1:2])
                    v.drain()
                    v.tensor_scalar_mul(acc[:], acc[:], mcol[:, 2:3])
                    v.wait_ge(ldx, 16 * (li + 1))
                    v.drain()
                    v.tensor_add(acc[:], acc[:], xrd[:])
                    v.drain()
                    v.tensor_reduce(mcol[:, 0:1], acc[:], AX.X, ALU.add)
                    v.drain()
                    v.tensor_scalar_mul(mcol[:, 0:1], mcol[:, 0:1], INV)
                    v.drain()
                    v.tensor_scalar(acc[:], acc[:], mcol[:, 0:1], None,
                                    ALU.subtract).then_inc(s_xm, 1)
                    v.wait_ge(s_sd, NT + li + 1)
                    v.reciprocal(mcol[:, 2:3], mcol[:, 1:2])
                    if li >= 2:
                        v.wait_ge(st2p[li % 2], 16 * (li // 2))
                    v.drain()
                    v.tensor_scalar_mul(outt[:, (li % 2) * DM:(li % 2 + 1) * DM],
                                        acc[:], mcol[:, 2:3]).then_inc(s_fin, 1)


def phase_Ap(nc, ncx, e, io):
    xTf = io["xTf"]; wxT = io["wxT"]; wdtTs = io["wdtTs"]; bdt2 = io["bdt2"]
    dtTs = io["dtTs"]; BTCT = io["BTCT"]

    NK = DM // 128
    NTC = TOK // 512

    if True:
        xb = e(ncx.sbuf_tensor("xb", [128, 2 * NK * 512], BF16))
        wx_sb = e(ncx.sbuf_tensor("wx_sb", [128, NK * 96], BF16))
        wdt_sb = e(ncx.sbuf_tensor("wdt_sb", [64, 256], BF16))
        bdt_sb = e(ncx.sbuf_tensor("bdt_sb", [128, 2], F32))
        bc_sb = e(ncx.sbuf_tensor("bc_sb", [32, TOK], F32))
        dtx_b = e(ncx.sbuf_tensor("dtx_b", [64, 2 * 512], BF16))
        t_z = e(ncx.sbuf_tensor("t_z", [128, 2 * 512], F32))
        t_a = e(ncx.sbuf_tensor("t_a", [128, 512], F32))
        t_e = e(ncx.sbuf_tensor("t_e", [128, 512], F32))
        t_l = e(ncx.sbuf_tensor("t_l", [128, 2 * 512], F32))
        t_r = e(ncx.sbuf_tensor("t_r", [128, 512], F32))
        dt_out = e(ncx.sbuf_tensor("dt_out", [128, 2 * 512], F32))
        one_c = e(ncx.sbuf_tensor("one_c", [128, 1], F32))
        p_xp = e(ncx.psum_tensor("p_xp", [96, 2 * 512], F32))
        p_dt = e(ncx.psum_tensor("p_dt", [128, 2 * 512], F32))
        ld = e(ncx.semaphore("ld"))
        lds = e(ncx.semaphore("lds"))
        ldxp = [e(ncx.semaphore("ldx0")), e(ncx.semaphore("ldx1"))]
        s_xp = e(ncx.semaphore("s_xp"))
        s_dt = e(ncx.semaphore("s_dt"))
        s_dtx = e(ncx.semaphore("s_dtx"))
        s_spZ = e(ncx.semaphore("s_spZ"))
        s_spL = e(ncx.semaphore("s_spL"))
        s_vr = e(ncx.semaphore("s_vr"))
        s_out = e(ncx.semaphore("s_out"))
        stp = [e(ncx.semaphore("st0")), e(ncx.semaphore("st1"))]
        stbc = e(ncx.semaphore("stbc"))
        blk = e(nc.Block())

        @blk.gpsimd
        def _(g):
            g.dma_start(out=wx_sb[:].rearrange("p (k j) -> p k j", k=NK),
                        in_=wxT.rearrange("(k p) j -> p k j", p=128)).then_inc(ld, 16)
            g.dma_start(out=wdt_sb[:], in_=wdtTs[:]).then_inc(ld, 16)
            for tc in range(NTC):
                if tc >= 2:
                    g.wait_ge(s_xp, tc - 1)
                b = (tc % 2) * NK * 512
                for k in range(NK):
                    g.dma_start(out=xb[:, b + k * 512:b + (k + 1) * 512],
                                in_=xTf[k * 128:(k + 1) * 128,
                                        tc * 512:(tc + 1) * 512]).then_inc(ldxp[tc % 2], 16)

        @blk.sync
        def _(sy):
            sy.dma_start(out=bdt_sb[:], in_=bdt2[:]).then_inc(lds, 16)
            gg = 0
            for tc in range(NTC):
                for gr in range(2):
                    sy.wait_ge(s_out, gg + 1)
                    sy.dma_start(out=dtTs[gr * 128:(gr + 1) * 128,
                                          tc * 512:(tc + 1) * 512],
                                 in_=dt_out[:, (gg % 2) * 512:(gg % 2 + 1) * 512]
                                 ).then_inc(stp[gg % 2], 16)
                    gg += 1
            sy.wait_ge(s_dtx, NTC)
            sy.dma_start(out=BTCT[:], in_=bc_sb[:]).then_inc(stbc, 16)
            sy.wait_ge(stp[0], 16 * NTC)
            sy.wait_ge(stp[1], 16 * NTC)
            sy.wait_ge(stbc, 16)

        @blk.tensor
        def _(t):
            t.wait_ge(ld, 32)
            gg = 0
            for tc in range(NTC):
                b = (tc % 2) * NK * 512
                pb = (tc % 2) * 512
                t.wait_ge(ldxp[tc % 2], 256 * (tc // 2 + 1))
                if tc >= 2:
                    t.wait_ge(s_dtx, tc - 1)
                for k in range(NK):
                    mm = t.matmul(p_xp[:, pb:pb + 512],
                                  wx_sb[:, k * 96:(k + 1) * 96],
                                  xb[:, b + k * 512:b + (k + 1) * 512],
                                  start=(k == 0), stop=(k == NK - 1))
                mm.then_inc(s_xp, 1)
                t.wait_ge(s_dtx, tc + 1)
                for gr in range(2):
                    if gg >= 2:
                        t.wait_ge(s_spZ, gg - 1)
                    t.matmul(p_dt[:, (gg % 2) * 512:(gg % 2 + 1) * 512],
                             wdt_sb[:, gr * 128:(gr + 1) * 128],
                             dtx_b[:, (tc % 2) * 512:(tc % 2 + 1) * 512],
                             start=True, stop=True).then_inc(s_dt, 1)
                    gg += 1

        @blk.scalar
        def _(a):
            a.wait_ge(ld, 32)
            a.wait_ge(lds, 16)
            gg = 0
            for tc in range(NTC):
                for gr in range(2):
                    zsl = slice((gg % 2) * 512, (gg % 2 + 1) * 512)
                    a.wait_ge(s_dt, gg + 1)
                    if gg >= 2:
                        a.wait_ge(s_vr, gg - 1)
                    a.activation(t_z[:, zsl], p_dt[:, zsl], AF.Identity,
                                 bias=bdt_sb[:, gr:gr + 1]).then_inc(s_spZ, 1)
                    a.drain()
                    a.activation(t_a[:], t_z[:, zsl], AF.Abs)
                    a.drain()
                    a.activation(t_e[:], t_a[:], AF.Exp, scale=-1.0)
                    a.drain()
                    a.activation(t_l[:, zsl], t_e[:], AF.Ln,
                                 bias=one_c[:]).then_inc(s_spL, 1)
                    gg += 1

        @blk.vector
        def _(v):
            v.memset(one_c[:], 1.0)
            gg = 0
            for tc in range(NTC):
                pb = (tc % 2) * 512
                v.wait_ge(s_xp, tc + 1)
                v.tensor_copy(bc_sb[:, tc * 512:(tc + 1) * 512], p_xp[64:96, pb:pb + 512])
                v.tensor_copy(dtx_b[:, pb:pb + 512],
                              p_xp[0:64, pb:pb + 512]).then_inc(s_dtx, 1)
                for gr in range(2):
                    zsl = slice((gg % 2) * 512, (gg % 2 + 1) * 512)
                    v.wait_ge(s_spZ, gg + 1)
                    v.tensor_scalar(t_r[:], t_z[:, zsl], 0.0, None,
                                    ALU.max).then_inc(s_vr, 1)
                    v.wait_ge(s_spL, gg + 1)
                    if gg >= 2:
                        v.wait_ge(stp[gg % 2], 16 * (gg // 2))
                    v.drain()
                    v.tensor_add(dt_out[:, zsl], t_r[:], t_l[:, zsl]).then_inc(s_out, 1)
                    gg += 1


def phase_Z(nc, ncx, e, io):
    src_f = io["src"]; dst_T = io["dst"]; idt = io["idt"]
    NT = TSL // 128
    NK = DM // 128
    if True:
        xt = e(ncx.sbuf_tensor("xt", [128, 2 * DM], BF16))
        ev = e(ncx.sbuf_tensor("ev", [128, 2 * NK * 128], BF16))
        idt_sb = e(ncx.sbuf_tensor("idt_sb", [128, 128], BF16))
        p_t = e(ncx.psum_tensor("p_t", [128, 128], BF16))
        ld = e(ncx.semaphore("ld"))
        ldi = e(ncx.semaphore("ldi"))
        s_tr = e(ncx.semaphore("s_tr"))
        s_te = e(ncx.semaphore("s_te"))
        st = e(ncx.semaphore("st"))
        blk = e(nc.Block())

        @blk.gpsimd
        def _(g):
            g.dma_start(out=idt_sb[:], in_=idt[:]).then_inc(ldi, 16)
            for tt in range(NT):
                if tt >= 2:
                    g.wait_ge(s_tr, NK * (tt - 1))
                g.dma_start(out=xt[:, (tt % 2) * DM:(tt % 2 + 1) * DM],
                            in_=src_f[tt * 128:(tt + 1) * 128, :]).then_inc(ld, 16)

        @blk.tensor
        def _(t):
            t.wait_ge(ldi, 16)
            n = 0
            for tt in range(NT):
                t.wait_ge(ld, 16 * (tt + 1))
                for k in range(NK):
                    if n >= 1:
                        t.wait_ge(s_te, n)
                    t.transpose(p_t[:],
                                xt[:, (tt % 2) * DM + k * 128:(tt % 2) * DM + (k + 1) * 128],
                                idt_sb[:]).then_inc(s_tr, 1)
                    n += 1

        @blk.scalar
        def _(a):
            n = 0
            for tt in range(NT):
                if tt >= 2:
                    a.wait_ge(st, 16 * (tt - 1))
                for k in range(NK):
                    a.wait_ge(s_tr, n + 1)
                    a.drain()
                    a.activation(ev[:, (tt % 2) * NK * 128 + k * 128:
                                    (tt % 2) * NK * 128 + (k + 1) * 128],
                                 p_t[:], AF.Identity).then_inc(s_te, 1)
                    n += 1

        @blk.sync
        def _(sy):
            for tt in range(NT):
                sy.wait_ge(s_te, NK * (tt + 1))
                sy.dma_start(out=dst_T.rearrange("(k p) t -> p k t", p=128)[
                                 :, :, tt * 128:(tt + 1) * 128],
                             in_=ev[:, (tt % 2) * NK * 128:(tt % 2 + 1) * NK * 128]
                             .rearrange("p (k t) -> p k t", k=NK)).then_inc(st, 16)
            sy.wait_ge(st, 16 * NT)


def build_mega(stop_after=None):
    def _stopper(nc, outp, xa):
        with ExitStack() as c2:
            ee = c2.enter_context
            sbq = ee(nc.sbuf_tensor("stop_sb", [128, DM], F32))
            ldq = ee(nc.semaphore("stop_ld"))
            stq = ee(nc.semaphore("stop_st"))
            bq = ee(nc.Block())
            @bq.sync
            def _(sy):
                sy.dma_start(out=sbq[:], in_=xa[0:128, :]).then_inc(ldq, 16)
                sy.wait_ge(ldq, 16)
                sy.dma_start(out=outp[0:128, :], in_=sbq[:]).then_inc(stq, 16)
                sy.wait_ge(stq, 16)
    nc = bass.Bass("TRN2", target_bir_lowering=False)
    F16 = mybir.dt.float16
    dram = nc.dram_tensor

    xTf = dram("xTf", [DM, TOK], F32, kind="ExternalInput").ap()
    wxT = dram("wxT", [DM, 96], F32, kind="ExternalInput").ap()
    wdtTs = dram("wdtTs", [DTR, 256], F32, kind="ExternalInput").ap()
    bdt2 = dram("bdt2", [128, 2], F32, kind="ExternalInput").ap()
    xTsh = dram("xTsh", [256, TOK], F32, kind="ExternalInput").ap()
    Ash = dram("Ash", [128, 32], F32, kind="ExternalInput").ap()
    Dsh = dram("Dsh", [128, 2], F32, kind="ExternalInput").ap()
    w2T = dram("w2T", [FF0, 256], F32, kind="ExternalInput").ap()
    h128 = dram("h128", [128, 128], BF16, kind="ExternalInput").ap()
    idt = dram("idt", [128, 128], BF16, kind="ExternalInput").ap()
    gpk = dram("gpk", [1408, 256], I32, kind="ExternalInput").ap()
    gsc = dram("gsc", [128, 11 * 16], F16, kind="ExternalInput").ap()
    dpk = dram("dpk", [256, 704], I32, kind="ExternalInput").ap()
    dsc = dram("dsc", [128, 2 * 44], F16, kind="ExternalInput").ap()
    xa = dram("xa", [TSL, DM], F32, kind="ExternalInput").ap()
    w1T = dram("w1T", [DM, FF0], F32, kind="ExternalInput").ap()
    w3T = dram("w3T", [DM, FF0], F32, kind="ExternalInput").ap()
    wq = dram("wq", [DM, 512], F32, kind="ExternalInput").ap()
    wk = dram("wk", [DM, 512], F32, kind="ExternalInput").ap()
    wv = dram("wv", [DM, 512], F32, kind="ExternalInput").ap()
    masks = dram("masks", [128, 5 * 512], BF16, kind="ExternalInput").ap()
    wout = dram("wout", [DM, DM], F32, kind="ExternalInput").ap()
    boutr = dram("boutr", [128, DM], F32, kind="ExternalInput").ap()
    offs = dram("offs", [1, 34], I32, kind="ExternalInput").ap()
    outp = dram("outp", [TSL, DM], F32, kind="ExternalOutput").ap()

    dtTs = dram("dtTs", [256, TOK], F32).ap()
    BTCT = dram("BTCT", [32, TOK], F32).ap()
    wp2in = dram("wp2in", [FF0, 256], BF16)
    wpgin = dram("wpgin", [DM, 1408], BF16)
    wpdin = dram("wpdin", [FI, 256], BF16)
    wp2g = dram("wp2g", [8 * FF0, 256], BF16)
    wpgg = dram("wpgg", [8 * DM, 1408], BF16)
    wpdg = dram("wpdg", [8 * FI, 256], BF16)
    ylocal = dram("ylocal", [TOK, 256], F32)
    ygath = dram("ygath", [8 * TOK, 256], F32)
    ylo = dram("ylo", [TSL, DM], F32).ap()
    OTl = dram("OTl", [DM, TSL], BF16).ap()
    z1f = dram("z1f", [TSL, DM], F32).ap()
    hidT = dram("hidT", [FF0, TSL], BF16).ap()
    z2loc = dram("z2loc", [TSL, DM], F32).ap()
    z2tl = dram("z2tl", [DM, TSL], BF16)
    z2g = dram("z2g", [4 * DM, TSL], BF16)
    Oloc = dram("Oloc", [512, SEQ], BF16)
    Og = dram("Og", [4 * 512, SEQ], BF16)
    x3f = dram("x3f", [TSL, DM], F32).ap()
    x3Tf = dram("x3Tf", [DM, TSL], BF16).ap()

    import os as _os
    RG8 = [list(range(8))]
    RG44 = [[0, 1, 2, 3], [4, 5, 6, 7]]

    with ExitStack() as octx:
        oe = octx.enter_context
        offs_sb = oe(nc.sbuf_tensor("offs_sb", [1, 34], I32))
        s_ccw = oe(nc.semaphore("s_ccw"))
        s_ccy = oe(nc.semaphore("s_ccy"))
        s_ccz = oe(nc.semaphore("s_ccz"))
        s_cco = oe(nc.semaphore("s_cco"))
        s_yl = oe(nc.semaphore("s_yl"))
        s_ol = oe(nc.semaphore("s_ol"))
        ld0m = oe(nc.semaphore("ld0m"))
        barsems = [oe(nc.semaphore("barS%d" % i)) for i in range(20)]
        swsems = [oe(nc.semaphore("sw%d" % i)) for i in range(40)]
        sw_alloc = [swsems, 0]
        clear_from = max(h.num for h in swsems) + 1
        dma_pool = [oe(nc.semaphore("pd%d" % i)) for i in range(12)]
        cmp_pool = [oe(nc.semaphore("pc%d" % i)) for i in range(20)]
        nbar = [0]
        SWN = dict(
            E_={"ld", "ldwA", "ldwB"},
            A_={"ld", "ldx0", "ldx1"},
            B_={"ldt0", "ldt1", "ldb0", "ldb1"},
            C_={"ldwA", "ldwB"} | {"ldw2%d" % qq for qq in range(8)},
            Z_={"ld", "ldi"},
            D_={"ldz", "ldw"},
            F_={"ldwt", "ldgA", "ldgB", "ldgC", "ldgD"}
               | {"ldd%d" % qq for qq in range(8)},
        )

        def PX(pfx):
            return NcProxy(nc, pfx, dma_pool=dma_pool, cmp_pool=cmp_pool,
                           sw_names=SWN[pfx], sw_alloc=sw_alloc)

        def bar():
            sb = barsems[nbar[0]]
            nbar[0] += 1
            _barrier(nc, NcProxy(nc, "M_"), oe, nbar[0], sb=sb,
                     clear_from=clear_from)

        with nc.Block() as b0:
            @b0.sync
            def _(sy):
                sy.dma_start(out=offs_sb[:], in_=offs[:]).then_inc(ld0m, 16)
                sy.wait_ge(ld0m, 16)

        _fonly = bool(int(_os.environ.get("MEGA_FONLY", "0")))
        if _fonly:
            _fcut0 = _os.environ.get("MEGA_FCUT") or None
            with nc.Block() as binit:
                @binit.gpsimd
                def _(g):
                    g.dma_start(out=OTl, in_=xa.rearrange("a b -> b a") if False
                                else OTl, ).then_inc(ld0m, 16) if False else None
                    g.dma_start(out=OTl, in_=xa).then_inc(ld0m, 16)
                    g.wait_ge(ld0m, 32)
            bar()
            with ExitStack() as ctx:
                phase_F(nc, PX("F_"), ctx.enter_context, dict(
                    z2s=xa, wout=wout, boutr=boutr, idt=idt, outp=outp,
                    x3f=x3f, x3Tf=x3Tf,
                    ot_tile=lambda eng, tt: OTl.rearrange("(k p) t -> p k t", p=128)[
                        :, :, tt * 128:(tt + 1) * 128],
                    wg_tile=None, wu_tile=None, wd_tile=None, cut=_fcut0))
            if _fcut0:
                bar()
                _stopper(nc, outp, xa)
            return nc
        with ExitStack() as ctx:
            phase_E(nc, PX("E_"), ctx.enter_context, dict(
                w2T=w2T, h128=h128, idt=idt, gpk=gpk, gsc=gsc, dpk=dpk, dsc=dsc,
                w2pT=wp2in.ap(), wguT=wpgin.ap(), wdnT=wpdin.ap()))
        bar()
        if stop_after == "E":
            _stopper(nc, outp, xa)
            return nc
        with nc.Block() as bw:
            @bw.gpsimd
            def _(g):
                g.collective_compute("AllGather", ALU.bypass, replica_groups=RG8,
                                     ins=[wp2in.ap().opt()], outs=[wp2g.ap().opt()]
                                     ).then_inc(s_ccw, 1)
                g.collective_compute("AllGather", ALU.bypass, replica_groups=RG8,
                                     ins=[wpgin.ap().opt()], outs=[wpgg.ap().opt()]
                                     ).then_inc(s_ccw, 1)
                g.collective_compute("AllGather", ALU.bypass, replica_groups=RG8,
                                     ins=[wpdin.ap().opt()], outs=[wpdg.ap().opt()]
                                     ).then_inc(s_ccw, 1)

        with ExitStack() as ctx:
            phase_Ap(nc, PX("A_"), ctx.enter_context, dict(
                xTf=xTf, wxT=wxT, wdtTs=wdtTs, bdt2=bdt2, dtTs=dtTs, BTCT=BTCT))
        bar()
        if stop_after == "Ap":
            _stopper(nc, outp, xa)
            return nc

        _noytr = bool(int(_os.environ.get("MEGA_NOYTR", "0")))
        ysc = dram("yscdbg", [256, TOK], F32).ap() if _noytr else None
        with ExitStack() as ctx:
            phase_B(nc, PX("B_"), ctx.enter_context, dict(
                dtTs=dtTs, xTs=xTsh, Ash=Ash, Dsh=Dsh,
                BT=BTCT[0:16, :], CT=BTCT[16:32, :], idt=idt,
                yTs=ysc, ytr=None if _noytr else ylocal.ap()))
        bar()
        if stop_after == "B":
            _stopper(nc, outp, xa)
            return nc
        yvr = ygath.ap().rearrange("(r t) d -> r t d", r=8)
        with nc.Block() as by:
            @by.gpsimd
            def _(g):
                g.collective_compute("AllGather", ALU.bypass, replica_groups=RG8,
                                     ins=[ylocal.ap().opt()], outs=[ygath.ap().opt()]
                                     ).then_inc(s_ccy, 1)
                g.wait_ge(s_ccy, 1)
                g.wait_ge(s_ccw, 3)
                ry = g.value_load(offs_sb[0:1, 0:1])
                H = TSL // 2
                for r8 in range(8):
                    for hh in range(2):
                        g.dma_start(
                            out=ylo[hh * H:(hh + 1) * H, r8 * 256:(r8 + 1) * 256],
                            in_=yvr[r8, ds(ry + hh * H, H), :]).then_inc(s_yl, 16)
                g.wait_ge(s_yl, 16 * 16)
        bar()
        if stop_after == "AGy":
            _stopper(nc, outp, xa)
            return nc

        def ya_tile(eng, tt):
            return ylo[tt * 128:(tt + 1) * 128, :].rearrange(
                "t (r d) -> t r d", r=8)

        def w2p_tile(hc, dc):
            return wp2g.ap().rearrange("(r h) d -> h r d", r=8)[
                hc * 128:(hc + 1) * 128, 2 * dc:2 * dc + 2, :]

        with ExitStack() as ctx:
            phase_C(nc, PX("C_"), ctx.enter_context, dict(
                xa=xa, w1T=w1T, w3T=w3T, idt=idt, z2o=z2loc, z1f=z1f, hidT=hidT,
                ya_tile=ya_tile, w2p_tile=w2p_tile, z2tl=None,
                offs_sb=offs_sb))
        bar()
        with ExitStack() as ctx:
            phase_Z(nc, PX("Z_"), ctx.enter_context, dict(
                src=z2loc, dst=z2tl.ap(), idt=idt))
        bar()
        if stop_after == "C":
            _stopper(nc, outp, xa)
            return nc
        with nc.Block() as bz:
            @bz.gpsimd
            def _(g):
                g.collective_compute("AllGather", ALU.bypass, replica_groups=RG44,
                                     ins=[z2tl.ap().opt()], outs=[z2g.ap().opt()]
                                     ).then_inc(s_ccz, 1)
                g.wait_ge(s_ccz, 1)
        bar()

        def z2w_tile(k, w8):
            q, hh = w8 // 2, w8 % 2
            return z2g.ap()[q * DM + k * 128:q * DM + (k + 1) * 128,
                            hh * 1024:(hh + 1) * 1024]

        with ExitStack() as ctx:
            phase_D(nc, PX("D_"), ctx.enter_context, dict(
                wq=wq, wk=wk, wv=wv, masks=masks, Oc=Oloc.ap(),
                z2w_tile=z2w_tile))
        bar()
        if stop_after == "D":
            _stopper(nc, outp, xa)
            return nc
        with nc.Block() as bo:
            @bo.gpsimd
            def _(g):
                g.collective_compute("AllGather", ALU.bypass, replica_groups=RG44,
                                     ins=[Oloc.ap().opt()], outs=[Og.ap().opt()]
                                     ).then_inc(s_cco, 1)
                g.wait_ge(s_cco, 1)
            @bo.sync
            def _(sy):
                sy.wait_ge(s_cco, 1)
                ro = sy.value_load(offs_sb[0:1, 1:2])
                for q4 in range(4):
                    sy.dma_start(
                        out=OTl[q4 * 512:(q4 + 1) * 512, :],
                        in_=Og.ap()[q4 * 512:(q4 + 1) * 512, ds(ro, TSL)]
                        ).then_inc(s_ol, 16)
                sy.wait_ge(s_ol, 16 * 4)
        bar()
        if stop_after == "AGO":
            _stopper(nc, outp, xa)
            return nc

        def ot_tile(eng, tt):
            return OTl.rearrange("(k p) t -> p k t", p=128)[
                :, :, tt * 128:(tt + 1) * 128]

        def wg_tile(pc):
            r, lc = pc // 11, pc % 11
            return wpgg.ap().rearrange("(r k p) o -> r p k o", r=8, p=128)[
                r, :, :, lc * 128:(lc + 1) * 128]

        def wu_tile(pc):
            cc = 44 + pc
            r, lc = cc // 11, cc % 11
            return wpgg.ap().rearrange("(r k p) o -> r p k o", r=8, p=128)[
                r, :, :, lc * 128:(lc + 1) * 128]

        def wd_tile(ic, dc):
            return wpdg.ap().rearrange("(r i) d -> i r d", r=8)[
                ic * 128:(ic + 1) * 128, 2 * dc:2 * dc + 2, :]

        _fcut = _os.environ.get("MEGA_FCUT") or None
        with ExitStack() as ctx:
            phase_F(nc, PX("F_"), ctx.enter_context, dict(
                z2s=z2loc, wout=wout, boutr=boutr, idt=idt, outp=outp,
                x3f=x3f, x3Tf=x3Tf, ot_tile=ot_tile, wg_tile=wg_tile,
                wu_tile=wu_tile, wd_tile=wd_tile, cut=_fcut))
        if _fcut:
            bar()
            _stopper(nc, outp, xa)
    return nc


_BUILT = {}


def _get(name, fn):
    if name not in _BUILT:
        _BUILT[name] = fn()
    return _BUILT[name]


def _hadamard128():
    H = np.array([[1.0]], dtype=np.float32)
    while H.shape[0] < 128:
        H = np.block([[H, H], [H, -H]])
    return H


def kernel_unfused(**ins):
    x = np.ascontiguousarray(ins["L0_attn_x"].reshape(TOK, DM))
    xT = np.ascontiguousarray(x.T)
    wxT = np.ascontiguousarray(ins["L0_attn_W_x"].T)
    wdtT = np.ascontiguousarray(ins["L0_attn_W_dt"].T)
    bdt = np.ascontiguousarray(ins["L0_attn_b_dt"].reshape(16, 128).T)
    idt = np.eye(128, dtype=BF)
    h128 = np.ascontiguousarray(_hadamard128().astype(BF))
    cores = list(range(NCORE))

    W2T = np.ascontiguousarray(ins["L0_ffn_W2"].T)
    gup = ins["L1_ffn_gate_up_packed"]
    gscT = np.ascontiguousarray(ins["L1_ffn_gate_up_scales"].T)
    dpk = ins["L1_ffn_down_packed"]
    dscT = np.ascontiguousarray(ins["L1_ffn_down_scales"].T)
    ncE = _get("E", build_E)
    in_maps = []
    for c in range(NCORE):
        dsl = slice(c * 256, (c + 1) * 256)
        gpk_c = np.ascontiguousarray(np.concatenate(
            [gup[c * 704:(c + 1) * 704],
             gup[FI + c * 704:FI + (c + 1) * 704]], axis=0))
        gsc_c = np.ascontiguousarray(np.concatenate(
            [gscT[c * 704:(c + 1) * 704],
             gscT[FI + c * 704:FI + (c + 1) * 704]], axis=0)
            .reshape(11, 128, 16).transpose(1, 0, 2).reshape(128, 176))
        dpk_c = np.ascontiguousarray(dpk[c * 256:(c + 1) * 256])
        dsc_c = np.ascontiguousarray(dscT[c * 256:(c + 1) * 256]
                                     .reshape(2, 128, 44).transpose(1, 0, 2)
                                     .reshape(128, 88))
        in_maps.append(dict(w2T=np.ascontiguousarray(W2T[:, dsl]), h128=h128,
                            idt=idt, gpk=gpk_c, gsc=gsc_c, dpk=dpk_c, dsc=dsc_c))
    resE = run_bass_kernel_spmd(ncE, in_maps, core_ids=cores)
    w2pT = np.concatenate([r["w2pT"] for r in resE.results], axis=1)
    wguT = np.zeros((DM, 2 * FI), dtype=BF)
    wdnT = np.concatenate([r["wdnT"] for r in resE.results], axis=1)
    for c in range(NCORE):
        blkc = resE.results[c]["wguT"]
        wguT[:, c * 704:(c + 1) * 704] = blkc[:, 0:704]
        wguT[:, FI + c * 704:FI + (c + 1) * 704] = blkc[:, 704:1408]

    ncA = _get("A", build_A)
    in_maps = []
    for c in range(NCORE):
        in_maps.append(dict(
            xT=np.ascontiguousarray(xT[:, c * TSL:(c + 1) * TSL]),
            wxT=wxT, wdtT=wdtT, bdt=bdt))
    resA = run_bass_kernel_spmd(ncA, in_maps, core_ids=cores)
    dtT = np.concatenate([r["dtT"] for r in resA.results], axis=1)
    bcs = np.concatenate([r["bc"] for r in resA.results], axis=1)
    BT = np.ascontiguousarray(bcs[0:16])
    CT = np.ascontiguousarray(bcs[16:32])

    A_log = ins["L0_attn_A_log"]
    Dv = ins["L0_attn_D"]
    ncB = _get("B", build_B)
    in_maps = []
    for c in range(NCORE):
        rs = slice(c * 256, (c + 1) * 256)
        in_maps.append(dict(
            dtTs=np.ascontiguousarray(dtT[rs]),
            xTs=np.ascontiguousarray(xT[rs]),
            Ash=np.ascontiguousarray(
                A_log[rs].reshape(2, 128, NST).transpose(1, 0, 2).reshape(128, 32)),
            Dsh=np.ascontiguousarray(Dv[rs].reshape(2, 128).T),
            BT=BT, CT=CT, idt=idt))
    resB = run_bass_kernel_spmd(ncB, in_maps, core_ids=cores)
    yT = np.concatenate([r["yTs"] for r in resB.results], axis=0)
    y = np.ascontiguousarray(yT.T)

    w1T_h = np.ascontiguousarray(ins["L0_ffn_W1"].T)
    w3T_h = np.ascontiguousarray(ins["L0_ffn_W3"].T)
    ncC = _get("C", build_C)
    in_maps = []
    for c in range(NCORE):
        ts = slice(c * TSL, (c + 1) * TSL)
        in_maps.append(dict(xa=np.ascontiguousarray(x[ts]),
                            ya=np.ascontiguousarray(y[ts]),
                            w1T=w1T_h, w3T=w3T_h, w2pT=w2pT, idt=idt))
    resC = run_bass_kernel_spmd(ncC, in_maps, core_ids=cores)
    z2 = np.concatenate([r["z2o"] for r in resC.results], axis=0)

    wqkv = ins["L1_attn_w_qkv"]
    masks_h = _attn_masks()
    ncD = _get("D", build_D)
    in_maps = []
    for c in range(NCORE):
        beta = c // 4
        h0 = (c % 4) * 4
        z2T_b = np.ascontiguousarray(z2[beta * SEQ:(beta + 1) * SEQ].T)
        in_maps.append(dict(
            z2T=z2T_b,
            wq=np.ascontiguousarray(wqkv[:, h0 * DH:(h0 + 4) * DH]),
            wk=np.ascontiguousarray(wqkv[:, DM + h0 * DH:DM + (h0 + 4) * DH]),
            wv=np.ascontiguousarray(wqkv[:, 2 * DM + h0 * DH:2 * DM + (h0 + 4) * DH]),
            masks=masks_h))
    resD = run_bass_kernel_spmd(ncD, in_maps, core_ids=cores)
    OT = [np.concatenate([resD.results[b * 4 + j]["Oc"] for j in range(4)], axis=0)
          for b in range(2)]
    OTfull = np.concatenate(OT, axis=1)

    wout_h = np.ascontiguousarray(ins["L1_attn_w_out"])
    boutr_h = np.ascontiguousarray(np.tile(ins["L1_attn_b_out"][None, :], (128, 1)))
    ncF = _get("F", build_F)
    in_maps = []
    for c in range(NCORE):
        ts = slice(c * TSL, (c + 1) * TSL)
        in_maps.append(dict(
            OTs=np.ascontiguousarray(OTfull[:, ts]),
            z2s=np.ascontiguousarray(z2[ts]),
            wout=wout_h, boutr=boutr_h, wguT=wguT, wdnT=wdnT, idt=idt))
    resF = run_bass_kernel_spmd(ncF, in_maps, core_ids=cores)
    out = np.concatenate([r["outp"] for r in resF.results], axis=0)
    return out.reshape(2, SEQ, DM)


def _attn_masks():
    m = np.zeros((128, 5 * 512), dtype=BF)
    kk = np.arange(128)[:, None]
    qq = np.arange(512)[None, :]
    for r in range(4):
        m[:, r * 512:(r + 1) * 512] = (kk + r * 128 <= qq).astype(BF)
    corner = ((kk <= qq) | (qq < 128)).astype(BF)
    m[:, 4 * 512:5 * 512] = corner
    return m


def kernel(**ins):
    x = np.ascontiguousarray(ins["L0_attn_x"].reshape(TOK, DM))
    xT = np.ascontiguousarray(x.T)
    wxT = np.ascontiguousarray(ins["L0_attn_W_x"].T)
    wdtT = np.ascontiguousarray(ins["L0_attn_W_dt"].T)
    bdt_full = ins["L0_attn_b_dt"]
    A_log = ins["L0_attn_A_log"]
    Dv = ins["L0_attn_D"]
    W2T = np.ascontiguousarray(ins["L0_ffn_W2"].T)
    gup = ins["L1_ffn_gate_up_packed"]
    gscT = np.ascontiguousarray(ins["L1_ffn_gate_up_scales"].T)
    dpk = ins["L1_ffn_down_packed"]
    dscT = np.ascontiguousarray(ins["L1_ffn_down_scales"].T)
    w1T_h = np.ascontiguousarray(ins["L0_ffn_W1"].T)
    w3T_h = np.ascontiguousarray(ins["L0_ffn_W3"].T)
    wqkv = ins["L1_attn_w_qkv"]
    wout_h = np.ascontiguousarray(ins["L1_attn_w_out"])
    boutr_h = np.ascontiguousarray(np.tile(ins["L1_attn_b_out"][None, :], (128, 1)))
    idt = np.eye(128, dtype=BF)
    h128 = np.ascontiguousarray(_hadamard128().astype(BF))
    masks_h = _attn_masks()

    ncM = _get("M", build_mega)
    in_maps = []
    for c in range(NCORE):
        rs = slice(c * 256, (c + 1) * 256)
        ts = slice(c * TSL, (c + 1) * TSL)
        h0 = (c % 4) * 4
        gpk_c = np.ascontiguousarray(gup[c * 1408:(c + 1) * 1408])
        gsc_c = np.ascontiguousarray(
            gscT[c * 1408:(c + 1) * 1408]
            .reshape(11, 128, 16).transpose(1, 0, 2).reshape(128, 176))
        dpk_c = np.ascontiguousarray(dpk[c * 256:(c + 1) * 256])
        dsc_c = np.ascontiguousarray(dscT[c * 256:(c + 1) * 256]
                                     .reshape(2, 128, 44).transpose(1, 0, 2)
                                     .reshape(128, 88))
        in_maps.append(dict(
            xTf=xT,
            wxT=wxT,
            wdtTs=np.ascontiguousarray(wdtT[:, rs]),
            bdt2=np.ascontiguousarray(bdt_full[rs].reshape(2, 128).T),
            xTsh=np.ascontiguousarray(xT[rs]),
            Ash=np.ascontiguousarray(
                A_log[rs].reshape(2, 128, NST).transpose(1, 0, 2).reshape(128, 32)),
            Dsh=np.ascontiguousarray(Dv[rs].reshape(2, 128).T),
            w2T=np.ascontiguousarray(W2T[:, rs]),
            h128=h128, idt=idt,
            gpk=gpk_c, gsc=gsc_c, dpk=dpk_c, dsc=dsc_c,
            xa=np.ascontiguousarray(x[ts]),
            w1T=w1T_h, w3T=w3T_h,
            wq=np.ascontiguousarray(wqkv[:, h0 * DH:(h0 + 4) * DH]),
            wk=np.ascontiguousarray(wqkv[:, DM + h0 * DH:DM + (h0 + 4) * DH]),
            wv=np.ascontiguousarray(wqkv[:, 2 * DM + h0 * DH:2 * DM + (h0 + 4) * DH]),
            masks=masks_h,
            wout=wout_h, boutr=boutr_h,
            offs=np.array([[c * TSL, (c % 4) * TSL]
                           + [(c % 4) * TSL + tt * 128 for tt in range(16)]
                           + [c * TSL + tt * 128 for tt in range(16)]],
                          dtype=np.int32)))
    res = run_bass_kernel_spmd(ncM, in_maps, core_ids=list(range(NCORE)))
    out = np.concatenate([r["outp"] for r in res.results], axis=0)
    return out.reshape(2, SEQ, DM)
